# version 07
# baseline (speedup 1.0000x reference)
import numpy as np
import ml_dtypes
import concourse.bass as bass
import concourse.mybir as mybir
from concourse.bass_utils import run_bass_kernel_spmd

F32 = mybir.dt.float32
BF16 = mybir.dt.bfloat16
AF = mybir.ActivationFunctionType
ALU = mybir.AluOpType
AX = mybir.AxisListType
bfd = ml_dtypes.bfloat16

H = 2048
S = 2048
B = 4
NH = 16
HD = 128
NCK = 16
E = 8
EF = 2048
EPS = 1e-5
TQ = 1024
RSQ = float(1.0 / np.sqrt(np.float32(HD)))
MASKNEG = -1.0e9

ENGS = ("sync", "scalar", "vector", "tensor", "gpsimd")

DEBUG_DUMP = []
PHASE_LIMIT = 99
OP_LIMIT = None


LANES = {"gpsimd": 12, "sync": 6}


class Op:
    __slots__ = ("eng", "fn", "dma", "eidx", "deps", "inc", "waits", "tag", "chan",
                 "cidx")

    def __init__(self, eng, fn, dma, eidx, tag):
        self.eng = eng
        self.fn = fn
        self.dma = dma
        self.eidx = eidx
        self.deps = set()
        self.inc = False
        self.waits = []
        self.tag = tag
        self.chan = eng
        self.cidx = 0


class Prog:
    def __init__(self):
        self.eops = {e: [] for e in ENGS}
        self.segs = {}
        self.nemit = 0

    def emit(self, eng, fn, reads=(), writes=(), dma=False, tag=""):
        self.nemit += 1
        if OP_LIMIT is not None and self.nemit > OP_LIMIT and tag not in ("stub_out", "join"):
            return None
        op = Op(eng, fn, dma, len(self.eops[eng]), tag)
        self.eops[eng].append(op)
        for (sp, lo, hi) in reads:
            segs = self.segs.setdefault(sp, [])
            for seg in segs:
                if seg[0] < hi and lo < seg[1]:
                    if seg[2] is not None:
                        op.deps.add(seg[2])
                    seg[3].add(op)
        for (sp, lo, hi) in writes:
            segs = self.segs.setdefault(sp, [])
            new = []
            for seg in segs:
                if seg[0] < hi and lo < seg[1]:
                    if seg[2] is not None:
                        op.deps.add(seg[2])
                    op.deps.update(seg[3])
                    if seg[0] < lo:
                        new.append([seg[0], lo, seg[2], set(seg[3])])
                    if hi < seg[1]:
                        new.append([hi, seg[1], seg[2], set(seg[3])])
                else:
                    new.append(seg)
            new.append([lo, hi, op, set()])
            self.segs[sp] = new
        op.deps.discard(op)
        return op

    def finalize(self):
        lane_prev = {}
        for e in ENGS:
            nl = LANES.get(e, 0)
            ctr = 0
            for op in self.eops[e]:
                if op.dma and nl:
                    lane = ctr % nl
                    ctr += 1
                    op.chan = f"{e}:L{lane}"
                    prev = lane_prev.get(op.chan)
                    if prev is not None:
                        op.deps.add(prev)
                    lane_prev[op.chan] = op
        chans = list(ENGS) + sorted({op.chan for e in ENGS for op in self.eops[e]
                                     if op.chan not in ENGS})
        cctr = {c: 0 for c in chans}
        for e in ENGS:
            for op in self.eops[e]:
                op.cidx = cctr[op.chan]
                cctr[op.chan] += 1
        for e in ENGS:
            for op in self.eops[e]:
                best = {}
                for d in op.deps:
                    if d.eng == "tensor" and op.eng == "tensor":
                        continue
                    if d is op:
                        continue
                    cur = best.get(d.chan)
                    if cur is None or d.cidx > cur.cidx:
                        best[d.chan] = d
                op.deps = best
        wm = {e: {c: -1 for c in chans} for e in ENGS}
        for e in ENGS:
            w = wm[e]
            for op in self.eops[e]:
                kept = []
                for d in sorted(op.deps.values(), key=lambda d: d.chan):
                    if d.cidx <= w[d.chan]:
                        continue
                    if d.chan == op.chan and d.cidx >= op.cidx:
                        raise RuntimeError(f"self-dep ahead: {op.tag} <- {d.tag}")
                    w[d.chan] = d.cidx
                    kept.append(d)
                    d.inc = True
                op.deps = kept
        for e in ENGS:
            for op in self.eops[e]:
                if op.dma:
                    op.inc = True
        ccnt = {c: 0 for c in chans}
        cumv = {}
        for e in ENGS:
            for op in self.eops[e]:
                if op.inc:
                    ccnt[op.chan] += 16 if op.dma else 1
                cumv[(op.chan, op.cidx)] = ccnt[op.chan]
        for e in ENGS:
            for op in self.eops[e]:
                op.waits = [(d.chan, cumv[(d.chan, d.cidx)]) for d in op.deps]
        self.chans = chans
        pos = {e: 0 for e in ENGS}
        cnt = {c: 0 for c in chans}
        progress = True
        while progress:
            progress = False
            for e in ENGS:
                while pos[e] < len(self.eops[e]):
                    op = self.eops[e][pos[e]]
                    if all(cnt[se] >= v for (se, v) in op.waits):
                        if op.inc:
                            cnt[op.chan] += 16 if op.dma else 1
                        pos[e] += 1
                        progress = True
                    else:
                        break
        stuck = [e for e in ENGS if pos[e] < len(self.eops[e])]
        if stuck:
            msgs = []
            for e in stuck:
                op = self.eops[e][pos[e]]
                msgs.append(f"{e}[{pos[e]}] {op.tag} waits {op.waits}")
            raise RuntimeError("DEADLOCK:\n" + "\n".join(msgs))
        return cumv


def build_nc(phase_limit=None):
    if phase_limit is None:
        phase_limit = PHASE_LIMIT
    nc = bass.Bass("TRN2", target_bir_lowering=False, debug=False)

    xT = nc.dram_tensor("xT", [H, S], F32, kind="ExternalInput").ap()
    wq_img = nc.dram_tensor("wq_img", [H, H], F32, kind="ExternalInput").ap()
    wk_img = nc.dram_tensor("wk_img", [H, H], F32, kind="ExternalInput").ap()
    wv_img = nc.dram_tensor("wv_img", [1024, 4096], F32, kind="ExternalInput").ap()
    wo_img = nc.dram_tensor("wo_img", [H, H], F32, kind="ExternalInput").ap()
    g_img = nc.dram_tensor("g_img", [H, H], F32, kind="ExternalInput").ap()
    u_img = nc.dram_tensor("u_img", [H, H], F32, kind="ExternalInput").ap()
    d_img = nc.dram_tensor("d_img", [H, H], F32, kind="ExternalInput").ap()
    lATd = nc.dram_tensor("lAT", [128, 2048], F32, kind="ExternalInput").ap()
    lBTd = nc.dram_tensor("lBT", [128, H], F32, kind="ExternalInput").ap()
    Gd = nc.dram_tensor("G_img", [128, 128], F32, kind="ExternalInput").ap()
    bqd = nc.dram_tensor("bq_img", [128, 16], F32, kind="ExternalInput").ap()
    bkd = nc.dram_tensor("bk_img", [128, 16], F32, kind="ExternalInput").ap()
    bod = nc.dram_tensor("bo_img", [128, 16], F32, kind="ExternalInput").ap()
    bvd = nc.dram_tensor("bv_row", [1, H], F32, kind="ExternalInput").ap()
    coskd = nc.dram_tensor("cosk", [128, S], F32, kind="ExternalInput").ap()
    sinkd = nc.dram_tensor("sink", [128, S], F32, kind="ExternalInput").ap()
    mcold = nc.dram_tensor("maskcol", [128, 1], F32, kind="ExternalInput").ap()
    y_out = nc.dram_tensor("y_out", [H, TQ], F32, kind="ExternalOutput").ap()

    def bfc(a):
        return np.ascontiguousarray(np.asarray(a, np.float32).astype(bfd))

    tri_np = (np.arange(128)[:, None] <= np.arange(128)[None, :]).astype(np.float32)
    tri_d = nc.inline_tensor(bfc(tri_np), "c_tri")
    id128_d = nc.inline_tensor(bfc(np.eye(128)), "c_id128")
    id8f_d = nc.inline_tensor(np.ascontiguousarray(np.eye(8, dtype=np.float32)), "c_id8f")
    eer_np = np.zeros((8, 128), np.float32)
    for e_ in range(8):
        eer_np[e_, e_ * 16:(e_ + 1) * 16] = 1.0
    eer_d = nc.inline_tensor(bfc(eer_np), "c_eer")
    e8s_np = np.zeros((8, 8 * 128), np.float32)
    for e_ in range(8):
        e8s_np[e_, e_ * 128:(e_ + 1) * 128] = 1.0
    e8s_d = nc.inline_tensor(bfc(e8s_np), "c_e8s")

    dbg = {}
    for name, shape, dt in [
        ("xb", [128, NCK * S], BF16), ("qp", [128, TQ], BF16),
        ("kp", [128, S], BF16), ("vp", [128, 16 * 129], BF16),
        ("ctx", [H, TQ], BF16), ("y1", [H, TQ], F32),
        ("ln2", [128, NCK * TQ], BF16), ("w8", [8, TQ], BF16),
        ("gated", [128, NCK * 512], BF16), ("lrw", [128, TQ], BF16),
    ]:
        if name in DEBUG_DUMP:
            dbg[name] = nc.dram_tensor("dbg_" + name, shape, dt, kind="ExternalOutput").ap()

    SBUF_BASE = 16512
    SBUF_LIMIT = 212864
    A = {}
    meta = {}

    def place(name, ncols, dt, off):
        esz = 2 if dt == BF16 else 4
        off = (off + 31) // 32 * 32
        nb = ncols * esz
        assert off + nb <= SBUF_LIMIT, (name, off, nb)
        A[name] = nc.alloc_sbuf_tensor_at(name, [128, ncols], dt, offset=SBUF_BASE + off)
        meta[name] = (off, esz, ncols)
        return off + nb

    o = 0
    for nm, ncols, dt in [
        ("cosk", S, BF16), ("sink", S, BF16), ("bvbc", H, BF16),
        ("lATb", 2048, BF16), ("lBTb", H, BF16), ("Gb", 128, BF16),
        ("tri", 128, BF16), ("id128", 128, BF16), ("id8f", 8, F32),
        ("eer", 128, BF16), ("e8s", 8 * 128, BF16), ("bqc", 16, F32), ("bkc", 16, F32), ("boc", 16, F32),
        ("mcol", 1, F32), ("onecb", 1, BF16), ("onerf", 128, F32),
        ("onerb", 128, BF16), ("mbc", S, BF16), ("ibc", S, BF16),
        ("joinpad", 16, F32),
    ]:
        o = place(nm, ncols, dt, o)
    REG_P_END = o
    REG_X = (REG_P_END + 31) // 32 * 32
    place("xb", NCK * S, BF16, REG_X)
    place("y1", NCK * TQ, F32, REG_X)
    REG_X_END = REG_X + NCK * TQ * 4
    REG_C = (REG_X_END + 31) // 32 * 32
    place("ctxsb", NCK * TQ, BF16, REG_C)
    place("ln2", NCK * TQ, BF16, REG_C)
    REG_C_END = REG_C + NCK * TQ * 2
    o = (REG_C_END + 31) // 32 * 32
    REG_W = o
    for nm, ncols, dt in [
        ("wqh0", 2048, BF16), ("wqh1", 2048, BF16),
        ("wkh0", 2048, BF16), ("wkh1", 2048, BF16),
        ("wvg0", 4096, BF16), ("wvg1", 4096, BF16),
        ("vp0", 16 * 129, BF16), ("vp1", 16 * 129, BF16),
        ("qp0", TQ, BF16), ("qp1", TQ, BF16),
        ("kp0", S, BF16), ("kp1", S, BF16),
        ("pt0", TQ, BF16), ("pt1", TQ, BF16),
        ("qktmp", S, BF16), ("r3", S, BF16),
        ("ctxq", 256, BF16), ("rcol", 8, F32),
        ("rows", 1024, F32), ("rows2", 1024, F32), ("rsc", 512, F32),
        ("sqs", 1024, BF16),
    ]:
        o = place(nm, ncols, dt, o)
    W12_END = o
    o = REG_W
    for nm, ncols, dt in [
        ("gated", NCK * 512, BF16),
        ("wch0", 2048, BF16), ("wch1", 2048, BF16),
        ("wgh0", 2048, BF16), ("wgh1", 2048, BF16),
        ("wuh0", 2048, BF16), ("wuh1", 2048, BF16),
        ("xqs0", TQ, F32), ("xqs1", TQ, F32),
        ("fout0", 512, F32), ("fout1", 512, F32),
        ("ytmp0", 512, F32), ("ytmp1", 512, F32),
        ("wbce0", 512, BF16), ("wbce1", 512, BF16),
        ("wer", TQ, BF16), ("lrw", TQ, BF16),
        ("logsb", 512, F32), ("tk", 64, F32), ("wtk", 8, BF16),
        ("w8sb", TQ, BF16), ("sil", TQ, BF16),
    ]:
        o = place(nm, ncols, dt, o)
    W36_END = o
    assert max(W12_END, W36_END) <= SBUF_LIMIT, (W12_END, W36_END)

    PS = {}
    psmeta = {}
    psoff = [0]

    def psalloc(name, ncols, dt):
        t = nc.alloc_psum_tensor("ps_" + name, [128, ncols], dt)
        PS[name] = t
        esz = 2 if dt == BF16 else 4
        psmeta[name] = (psoff[0], esz)
        psoff[0] += 2048 * ((ncols * esz + 2047) // 2048)

    psalloc("a", 512, F32)
    psalloc("b", 512, F32)
    psalloc("s0", 512, F32)
    psalloc("s1", 512, F32)
    psalloc("c0", 512, F32)
    psalloc("c1", 512, F32)
    psalloc("t", 128, BF16)
    psalloc("t2", 512, F32)
    assert psoff[0] <= 8 * 2048, psoff[0]

    def rg(name, c0=None, c1=None):
        off, esz, ncols = meta[name]
        if c0 is None:
            c0, c1 = 0, ncols
        return ("sbuf", off + c0 * esz, off + c1 * esz)

    def prg(name, c0=None, c1=None):
        off, esz = psmeta[name]
        if c0 is None:
            return ("psum", off, off + 2048)
        return ("psum", off + c0 * esz, off + c1 * esz)

    P = Prog()
    em = P.emit

    def materialize():
        P.finalize()
        import contextlib
        with contextlib.ExitStack() as stack:
            sems = {}
            for c in P.chans:
                sems[c] = stack.enter_context(nc.semaphore("sem_" + c.replace(":", "_")))
            block = stack.enter_context(nc.Block())

            def runner(eng):
                def run(h):
                    for op in P.eops[eng]:
                        for (se, v) in op.waits:
                            h.wait_ge(sems[se], v)
                        inst = op.fn(h)
                        if op.inc:
                            inst.then_inc(sems[op.chan], 16 if op.dma else 1)
                return run

            block.sync(runner("sync"))
            block.scalar(runner("scalar"))
            block.vector(runner("vector"))
            block.tensor(runner("tensor"))
            block.gpsimd(runner("gpsimd"))
        return nc

    def stop_after(k):
        if phase_limit > k:
            return False
        em("gpsimd", lambda h: h.dma_start(out=y_out[0:128, :], in_=A["mbc"][:, 0:TQ]),
           reads=[rg("mbc", 0, TQ)], writes=[("yout", 0, 512)], dma=True, tag="stub_out")
        em("gpsimd", lambda h: h.memset(A["joinpad"][:, 0:1], 0.0),
           reads=[("yout", 0, 512)], writes=[rg("joinpad")], tag="join")
        return True

    for nm, src, np_ in [("tri", tri_d.ap(), 128), ("id128", id128_d.ap(), 128),
                         ("eer", eer_d.ap(), 8), ("e8s", e8s_d.ap(), 8)]:
        em("sync", lambda h, nm=nm, src=src, np_=np_: h.dma_start(out=A[nm][0:np_, :], in_=src),
           writes=[rg(nm)], dma=True, tag="ld_" + nm)
    em("sync", lambda h: h.dma_start(out=A["id8f"][0:8, :], in_=id8f_d.ap()),
       writes=[rg("id8f")], dma=True, tag="ld_id8f")
    for nm, src in [("bqc", bqd), ("bkc", bkd), ("boc", bod), ("mcol", mcold)]:
        em("sync", lambda h, nm=nm, src=src: h.dma_start(out=A[nm][:], in_=src[:, :]),
           writes=[rg(nm)], dma=True, tag="ld_" + nm)
    em("sync", lambda h: h.dma_start(out=A["rows"][0:1, :], in_=bvd[0:1, 0:1024]),
       writes=[rg("rows")], dma=True, tag="ld_bv0")
    em("sync", lambda h: h.dma_start(out=A["rows2"][0:1, :], in_=bvd[0:1, 1024:2048]),
       writes=[rg("rows2")], dma=True, tag="ld_bv1")
    for nm, src in [("cosk", coskd), ("sink", sinkd), ("Gb", Gd)]:
        em("gpsimd", lambda h, nm=nm, src=src: h.dma_start(out=A[nm][:], in_=src[:, :]),
           writes=[rg(nm)], dma=True, tag="ld_" + nm)
    em("gpsimd", lambda h: h.dma_start(out=A["lATb"][:], in_=lATd[:, :]),
       writes=[rg("lATb")], dma=True, tag="ld_lAT")
    em("gpsimd", lambda h: h.dma_start(out=A["lBTb"][:], in_=lBTd[:, :]),
       writes=[rg("lBTb")], dma=True, tag="ld_lBT")
    em("vector", lambda h: h.memset(A["onecb"][:], 1.0), writes=[rg("onecb")], tag="ms1")
    em("vector", lambda h: h.memset(A["onerf"][0:1, :], 1.0), writes=[rg("onerf")], tag="ms2")
    em("vector", lambda h: h.memset(A["onerb"][0:1, :], 1.0), writes=[rg("onerb")], tag="ms3")
    for p in range(2):
        em("vector", lambda h, p=p: h.memset(A[f"vp{p}"][:, 128:16 * 129:129], 1.0),
           writes=[rg(f"vp{p}")], tag=f"ms_vp{p}")
    for g in range(4):
        src = "rows" if g < 2 else "rows2"
        sc0 = (g % 2) * 512
        ps = ("a", "b")[g % 2]
        em("tensor", lambda h, src=src, sc0=sc0, ps=ps: h.matmul(
            PS[ps][:, :], A["onerf"][0:1, :], A[src][0:1, sc0:sc0 + 512],
            start=True, stop=True),
           reads=[rg("onerf"), rg(src, sc0, sc0 + 512)], writes=[prg(ps)], tag=f"bvb{g}")
        em("scalar", lambda h, g=g, ps=ps: h.copy(
            A["bvbc"][:, g * 512:(g + 1) * 512], PS[ps][:, :]),
           reads=[prg(ps)], writes=[rg("bvbc", g * 512, (g + 1) * 512)], tag=f"bve{g}")

    if stop_after(0):
        return materialize()

    for ic in range(NCK):
        em("gpsimd", lambda h, ic=ic: h.dma_start(
            out=A["xb"][:, ic * S:(ic + 1) * S], in_=xT[ic * 128:(ic + 1) * 128, :]),
           writes=[rg("xb", ic * S, (ic + 1) * S)], dma=True, tag=f"ld_x{ic}")

    def ln_stats_center(buf, bufstride, ntok, mdst, tag, upto=3):
        nhalf = ntok // 1024
        for hf in range(max(nhalf, 1)):
            if upto < 1:
                break
            t0 = hf * 1024
            tn = min(1024, ntok - t0)
            for ic in range(NCK):
                em("vector", lambda h, ic=ic, t0=t0, tn=tn: h.tensor_tensor(
                    A["sqs"][:, 0:tn],
                    A[buf][:, ic * bufstride + t0: ic * bufstride + t0 + tn],
                    A[buf][:, ic * bufstride + t0: ic * bufstride + t0 + tn], ALU.mult),
                   reads=[rg(buf, ic * bufstride + t0, ic * bufstride + t0 + tn)],
                   writes=[rg("sqs", 0, tn)], tag=f"{tag}sq{hf}_{ic}")
                for g in range(2):
                    if g * 512 >= tn:
                        continue
                    ps = ("a", "b")[g]
                    ps2 = ("c0", "c1")[g]
                    c0 = t0 + g * 512
                    em("tensor", lambda h, ic=ic, c0=c0, ps=ps: h.matmul(
                        PS[ps][0:1, :], A["onecb"][:, :],
                        A[buf][:, ic * bufstride + c0: ic * bufstride + c0 + 512],
                        start=(ic == 0), stop=(ic == NCK - 1)),
                       reads=[rg("onecb"), rg(buf, ic * bufstride + c0, ic * bufstride + c0 + 512)],
                       writes=[prg(ps, 0, 512)], tag=f"{tag}sx{hf}{g}_{ic}")
                    em("tensor", lambda h, ic=ic, g=g, ps2=ps2: h.matmul(
                        PS[ps2][0:1, :], A["onecb"][:, :], A["sqs"][:, g * 512:(g + 1) * 512],
                        start=(ic == 0), stop=(ic == NCK - 1)),
                       reads=[rg("onecb"), rg("sqs", g * 512, (g + 1) * 512)],
                       writes=[prg(ps2, 0, 512)], tag=f"{tag}s2{hf}{g}_{ic}")
            if upto < 2:
                continue
            for g in range(2):
                if g * 512 >= tn:
                    continue
                ps = ("a", "b")[g]
                ps2 = ("c0", "c1")[g]
                gc = g * 512
                em("vector", lambda h, gc=gc, ps=ps: h.tensor_scalar(
                    A["rows"][0:1, gc:gc + 512], PS[ps][0:1, :], 1.0 / H, None, ALU.mult),
                   reads=[prg(ps, 0, 512)], writes=[rg("rows", gc, gc + 512)],
                   tag=f"{tag}m{hf}{g}")
                em("vector", lambda h, gc=gc: h.tensor_tensor(
                    A["rsc"][0:1, :], A["rows"][0:1, gc:gc + 512],
                    A["rows"][0:1, gc:gc + 512], ALU.mult),
                   reads=[rg("rows", gc, gc + 512)], writes=[rg("rsc")], tag=f"{tag}ms{hf}{g}")
                em("vector", lambda h, gc=gc, ps2=ps2: h.tensor_scalar(
                    A["rows2"][0:1, gc:gc + 512], PS[ps2][0:1, :], 1.0 / H, None, ALU.mult),
                   reads=[prg(ps2, 0, 512)], writes=[rg("rows2", gc, gc + 512)],
                   tag=f"{tag}v{hf}{g}")
                em("vector", lambda h, gc=gc: h.tensor_tensor(
                    A["rows2"][0:1, gc:gc + 512], A["rows2"][0:1, gc:gc + 512],
                    A["rsc"][0:1, :], ALU.subtract),
                   reads=[rg("rows2", gc, gc + 512), rg("rsc")],
                   writes=[rg("rows2", gc, gc + 512)], tag=f"{tag}v2{hf}{g}")
                em("vector", lambda h, gc=gc: h.tensor_scalar(
                    A["rows2"][0:1, gc:gc + 512], A["rows2"][0:1, gc:gc + 512],
                    float(EPS), None, ALU.add),
                   reads=[rg("rows2", gc, gc + 512)], writes=[rg("rows2", gc, gc + 512)],
                   tag=f"{tag}v3{hf}{g}")
                em("scalar", lambda h, gc=gc: h.activation(
                    A["rsc"][0:1, :], A["rows2"][0:1, gc:gc + 512], AF.Sqrt,
                    bias=0.0, scale=1.0),
                   reads=[rg("rows2", gc, gc + 512)], writes=[rg("rsc")], tag=f"{tag}sd{hf}{g}")
                em("vector", lambda h, gc=gc: h.reciprocal(
                    A["rows2"][0:1, gc:gc + 512], A["rsc"][0:1, :]),
                   reads=[rg("rsc")], writes=[rg("rows2", gc, gc + 512)], tag=f"{tag}i{hf}{g}")
                for (src, dst) in (("rows", "mbc"), ("rows2", "ibc")):
                    ps3 = (("a", "c0"), ("b", "c1"))[g][src == "rows2"]
                    em("tensor", lambda h, gc=gc, src=src, ps3=ps3: h.matmul(
                        PS[ps3][:, :], A["onerf"][0:1, :], A[src][0:1, gc:gc + 512],
                        start=True, stop=True),
                       reads=[rg("onerf"), rg(src, gc, gc + 512)], writes=[prg(ps3)],
                       tag=f"{tag}bc{hf}{g}{dst}")
                    em("scalar", lambda h, t0=t0, gc=gc, dst=dst, ps3=ps3: h.copy(
                        A[dst][:, t0 + gc: t0 + gc + 512], PS[ps3][:, :]),
                       reads=[prg(ps3)], writes=[rg(dst, t0 + gc, t0 + gc + 512)],
                       tag=f"{tag}be{hf}{g}{dst}")
        if upto < 3:
            return
        for ic in range(NCK):
            em("vector", lambda h, ic=ic: h.tensor_tensor(
                A[buf][:, ic * bufstride: ic * bufstride + ntok],
                A[buf][:, ic * bufstride: ic * bufstride + ntok],
                A["mbc"][:, 0:ntok], ALU.subtract),
               reads=[rg(buf, ic * bufstride, ic * bufstride + ntok), rg("mbc", 0, ntok)],
               writes=[rg(buf, ic * bufstride, ic * bufstride + ntok)], tag=f"{tag}ce{ic}")
            em("vector", lambda h, ic=ic: h.tensor_tensor(
                A[buf][:, ic * bufstride: ic * bufstride + ntok],
                A[buf][:, ic * bufstride: ic * bufstride + ntok],
                A["ibc"][:, 0:ntok], ALU.mult),
               reads=[rg(buf, ic * bufstride, ic * bufstride + ntok), rg("ibc", 0, ntok)],
               writes=[rg(buf, ic * bufstride, ic * bufstride + ntok)], tag=f"{tag}cs{ic}")

    if stop_after(0.2):
        return materialize()
    ln_stats_center("xb", S, S, "mbc", "ln1",
                    upto=(0 if phase_limit <= 0.4 else 1 if phase_limit <= 0.6
                          else 2 if phase_limit <= 0.8 else 3))
    if "xb" in dbg:
        em("sync", lambda h: h.dma_start(out=dbg["xb"][:, :], in_=A["xb"][:, :]),
           reads=[rg("xb")], dma=True, tag="dbg_xb")

    if stop_after(1):
        return materialize()

    def load_head_weights(hd):
        p = hd % 2
        em("gpsimd", lambda h, hd=hd, p=p: h.dma_start(
            out=A[f"wqh{p}"][:, :], in_=wq_img[hd * 128:(hd + 1) * 128, :]),
           writes=[rg(f"wqh{p}")], dma=True, tag=f"ldwq{hd}")
        em("gpsimd", lambda h, hd=hd, p=p: h.dma_start(
            out=A[f"wkh{p}"][:, :], in_=wk_img[hd * 128:(hd + 1) * 128, :]),
           writes=[rg(f"wkh{p}")], dma=True, tag=f"ldwk{hd}")
        if hd % 2 == 0:
            g = hd // 2
            pv = g % 2
            em("gpsimd", lambda h, g=g, pv=pv: h.dma_start(
                out=A[f"wvg{pv}"][:, :], in_=wv_img[g * 128:(g + 1) * 128, :]),
               writes=[rg(f"wvg{pv}")], dma=True, tag=f"ldwv{g}")

    def proj_qk(hd, kind):
        p = hd % 2
        wb = f"wqh{p}" if kind == "q" else f"wkh{p}"
        dst = f"qp{p}" if kind == "q" else f"kp{p}"
        bias = "bqc" if kind == "q" else "bkc"
        ntt = 2 if kind == "q" else 4
        n = ntt * 512
        for tt in range(ntt):
            ps = ("a", "b")[tt % 2]
            for ic in range(NCK):
                em("tensor", lambda h, ic=ic, tt=tt, wb=wb, ps=ps: h.matmul(
                    PS[ps][:, :], A[wb][:, ic * 128:(ic + 1) * 128],
                    A["xb"][:, ic * S + tt * 512: ic * S + (tt + 1) * 512],
                    start=(ic == 0), stop=(ic == NCK - 1)),
                   reads=[rg(wb, ic * 128, (ic + 1) * 128),
                          rg("xb", ic * S + tt * 512, ic * S + (tt + 1) * 512)],
                   writes=[prg(ps)], tag=f"{kind}mm{hd}_{tt}_{ic}")
            em("scalar", lambda h, tt=tt, hd=hd, bias=bias, ps=ps: h.activation(
                A["qktmp"][:, tt * 512:(tt + 1) * 512], PS[ps][:, :], AF.Identity,
                bias=A[bias][:, hd:hd + 1], scale=1.0),
               reads=[prg(ps), rg(bias, hd, hd + 1)],
               writes=[rg("qktmp", tt * 512, (tt + 1) * 512)], tag=f"{kind}ev{hd}_{tt}")
        em("vector", lambda h, dst=dst, n=n: h.tensor_tensor(
            A[dst][:, 0:n], A["qktmp"][:, 0:n], A["cosk"][:, 0:n], ALU.mult),
           reads=[rg("qktmp", 0, n), rg("cosk", 0, n)], writes=[rg(dst, 0, n)],
           tag=f"{kind}rc{hd}")
        em("vector", lambda h, n=n: h.tensor_tensor(
            A["r3"][0:64, 0:n], A["qktmp"][64:128, 0:n], A["sink"][64:128, 0:n], ALU.mult),
           reads=[rg("qktmp", 0, n), rg("sink", 0, n)], writes=[rg("r3", 0, n)],
           tag=f"{kind}rs1{hd}")
        em("vector", lambda h, n=n: h.tensor_tensor(
            A["r3"][64:128, 0:n], A["qktmp"][0:64, 0:n], A["sink"][0:64, 0:n], ALU.mult),
           reads=[rg("qktmp", 0, n), rg("sink", 0, n)], writes=[rg("r3", 0, n)],
           tag=f"{kind}rs2{hd}")
        em("vector", lambda h, dst=dst, n=n: h.tensor_tensor(
            A[dst][:, 0:n], A[dst][:, 0:n], A["r3"][:, 0:n], ALU.add),
           reads=[rg(dst, 0, n), rg("r3", 0, n)], writes=[rg(dst, 0, n)],
           tag=f"{kind}ra{hd}")

    def proj_v(hd):
        g = hd // 2
        pv = g % 2
        for tc in range(NCK):
            ps = ("a", "b")[tc % 2]
            for ic in range(NCK):
                em("tensor", lambda h, ic=ic, tc=tc, pv=pv, ps=ps: h.matmul(
                    PS[ps][:, 0:256], A["xb"][:, ic * S + tc * 128: ic * S + (tc + 1) * 128],
                    A[f"wvg{pv}"][:, ic * 256:(ic + 1) * 256],
                    start=(ic == 0), stop=(ic == NCK - 1)),
                   reads=[rg("xb", ic * S + tc * 128, ic * S + (tc + 1) * 128),
                          rg(f"wvg{pv}", ic * 256, (ic + 1) * 256)],
                   writes=[prg(ps, 0, 256)], tag=f"vmm{g}_{tc}_{ic}")
            for sub in range(2):
                hh = g * 2 + sub
                pw = hh % 2
                em("vector", lambda h, tc=tc, sub=sub, hh=hh, pw=pw, ps=ps: h.tensor_tensor(
                    A[f"vp{pw}"][:, tc * 129: tc * 129 + 128],
                    PS[ps][:, sub * 128:(sub + 1) * 128],
                    A["bvbc"][:, hh * 128:(hh + 1) * 128], ALU.add),
                   reads=[prg(ps, sub * 128, (sub + 1) * 128),
                          rg("bvbc", hh * 128, (hh + 1) * 128)],
                   writes=[rg(f"vp{pw}", tc * 129, tc * 129 + 128)],
                   tag=f"vev{hh}_{tc}")

    def attn(hd):
        p = hd % 2
        ctx_slot = [("c0", 0), ("c1", 0)]
        tilectr = [0]
        for grp in range(4):
            slots = list(range(grp * 2, grp * 2 + 2))
            kcs = [kc for kc in range(16) if (kc % 8) <= slots[-1]]
            for kc in kcs:
                j0 = max(kc % 8, slots[0])
                if j0 > slots[-1]:
                    continue
                n = (slots[-1] - j0 + 1) * 128
                q0 = j0 * 128
                pp = f"pt{kc % 2}"
                c = 0
                while c < n:
                    w = min(512, n - c)
                    ps = ("s0", "s1")[tilectr[0] % 2]
                    tilectr[0] += 1
                    em("tensor", lambda h, kc=kc, q0=q0, c=c, w=w, p=p, ps=ps: h.matmul(
                        PS[ps][:, 0:w], A[f"kp{p}"][:, kc * 128:(kc + 1) * 128],
                        A[f"qp{p}"][:, q0 + c: q0 + c + w], start=True, stop=True),
                       reads=[rg(f"kp{p}", kc * 128, (kc + 1) * 128),
                              rg(f"qp{p}", q0 + c, q0 + c + w)],
                       writes=[prg(ps, 0, w)], tag=f"sc{hd}_{kc}_{c}")
                    if kc >= 8 and c == 0 and j0 == kc - 8:
                        em("scalar", lambda h, ps=ps, pp=pp, q0=q0: h.activation(
                            A[pp][:, q0: q0 + 128], PS[ps][:, 0:128], AF.Exp,
                            bias=A["mcol"][:, 0:1], scale=RSQ),
                           reads=[prg(ps, 0, 128), rg("mcol")],
                           writes=[rg(pp, q0, q0 + 128)], tag=f"exm{hd}_{kc}")
                        if w > 128:
                            em("scalar", lambda h, ps=ps, pp=pp, q0=q0, w=w: h.activation(
                                A[pp][:, q0 + 128: q0 + w], PS[ps][:, 128:w], AF.Exp,
                                bias=0.0, scale=RSQ),
                               reads=[prg(ps, 128, w)], writes=[rg(pp, q0 + 128, q0 + w)],
                               tag=f"exr{hd}_{kc}")
                    else:
                        em("scalar", lambda h, ps=ps, pp=pp, q0=q0, c=c, w=w: h.activation(
                            A[pp][:, q0 + c: q0 + c + w], PS[ps][:, 0:w], AF.Exp,
                            bias=0.0, scale=RSQ),
                           reads=[prg(ps, 0, w)], writes=[rg(pp, q0 + c, q0 + c + w)],
                           tag=f"ex{hd}_{kc}_{c}")
                    if kc < 8 and c == 0 and j0 == kc:
                        em("vector", lambda h, pp=pp, q0=q0: h.tensor_tensor(
                            A[pp][:, q0: q0 + 128], A[pp][:, q0: q0 + 128],
                            A["tri"][:, :], ALU.mult),
                           reads=[rg(pp, q0, q0 + 128), rg("tri")],
                           writes=[rg(pp, q0, q0 + 128)], tag=f"tri{hd}_{kc}")
                    for j in range(j0 + c // 128, j0 + (c + w) // 128):
                        bank, boff = ctx_slot[j % 2]
                        em("tensor", lambda h, j=j, kc=kc, p=p, pp=pp, bank=bank,
                           boff=boff: h.matmul(
                            PS[bank][:, boff:boff + 129], A[pp][:, j * 128:(j + 1) * 128],
                            A[f"vp{p}"][:, kc * 129:(kc + 1) * 129],
                            start=(kc == 0), stop=(kc == 8 + j)),
                           reads=[rg(pp, j * 128, (j + 1) * 128),
                                  rg(f"vp{p}", kc * 129, (kc + 1) * 129)],
                           writes=[prg(bank, boff, boff + 129)], tag=f"cx{hd}_{kc}_{j}")
                    c += w
            for j in slots:
                bank, boff = ctx_slot[j % 2]
                em("vector", lambda h, bank=bank, boff=boff: h.reciprocal(
                    A["rcol"][:, 0:1], PS[bank][:, boff + 128:boff + 129]),
                   reads=[prg(bank, boff + 128, boff + 129)], writes=[rg("rcol", 0, 1)],
                   tag=f"rc{hd}_{j}")
                em("vector", lambda h, bank=bank, boff=boff, j=j: h.tensor_scalar(
                    A["ctxq"][:, (j % 2) * 128:(j % 2) * 128 + 128],
                    PS[bank][:, boff:boff + 128], A["rcol"][:, 0:1], None, ALU.mult),
                   reads=[prg(bank, boff, boff + 128), rg("rcol", 0, 1)],
                   writes=[rg("ctxq", (j % 2) * 128, (j % 2) * 128 + 128)],
                   tag=f"nm{hd}_{j}")
                em("tensor", lambda h, j=j: h.transpose(
                    PS["t"][:, 0:128], A["ctxq"][:, (j % 2) * 128:(j % 2) * 128 + 128],
                    A["id128"][:, :]),
                   reads=[rg("ctxq", (j % 2) * 128, (j % 2) * 128 + 128), rg("id128")],
                   writes=[prg("t")], tag=f"tp{hd}_{j}")
                em("scalar", lambda h, hd=hd, j=j: h.copy(
                    A["ctxsb"][:, hd * TQ + j * 128: hd * TQ + (j + 1) * 128],
                    PS["t"][:, 0:128]),
                   reads=[prg("t")],
                   writes=[rg("ctxsb", hd * TQ + j * 128, hd * TQ + (j + 1) * 128)],
                   tag=f"cte{hd}_{j}")

    load_head_weights(0)
    for hd in range(NH):
        if hd + 1 < NH:
            load_head_weights(hd + 1)
        proj_qk(hd, "q")
        proj_qk(hd, "k")
        if hd % 2 == 0:
            proj_v(hd)
        attn(hd)
        if hd == 0:
            if "qp" in dbg:
                em("sync", lambda h: h.dma_start(out=dbg["qp"][:, :], in_=A["qp0"][:, :]),
                   reads=[rg("qp0")], dma=True, tag="dbg_qp")
            if "kp" in dbg:
                em("sync", lambda h: h.dma_start(out=dbg["kp"][:, :], in_=A["kp0"][:, :]),
                   reads=[rg("kp0")], dma=True, tag="dbg_kp")
            if "vp" in dbg:
                em("sync", lambda h: h.dma_start(out=dbg["vp"][:, :], in_=A["vp0"][:, :]),
                   reads=[rg("vp0")], dma=True, tag="dbg_vp")
    if "ctx" in dbg:
        for ic in range(NCK):
            em("sync", lambda h, ic=ic: h.dma_start(
                out=dbg["ctx"][ic * 128:(ic + 1) * 128, :],
                in_=A["ctxsb"][:, ic * TQ:(ic + 1) * TQ]),
               reads=[rg("ctxsb", ic * TQ, (ic + 1) * TQ)], dma=True, tag=f"dbg_ctx{ic}")

    if stop_after(2):
        return materialize()

    def load_band(dst, dram, band, tagp):
        em("gpsimd", lambda h, dst=dst, dram=dram, band=band: h.dma_start(
            out=A[dst][:, :], in_=dram[band * 128:(band + 1) * 128, :]),
           writes=[rg(dst)], dma=True, tag=tagp)

    load_band("wch0", wo_img, 0, "ldwo0")
    for hoc in range(NCK):
        pw = hoc % 2
        if hoc + 1 < NCK:
            load_band(f"wch{(hoc + 1) % 2}", wo_img, hoc + 1, f"ldwo{hoc + 1}")
        em("sync", lambda h, hoc=hoc, pw=pw: h.dma_start(
            out=A[f"xqs{pw}"][:, :], in_=xT[hoc * 128:(hoc + 1) * 128, 0:TQ]),
           writes=[rg(f"xqs{pw}")], dma=True, tag=f"ldxq{hoc}")
        for tt in range(2):
            ps = ("a", "b")[tt]
            pf = tt
            for ic in range(NCK):
                em("tensor", lambda h, ic=ic, tt=tt, pw=pw, ps=ps: h.matmul(
                    PS[ps][:, :], A[f"wch{pw}"][:, ic * 128:(ic + 1) * 128],
                    A["ctxsb"][:, ic * TQ + tt * 512: ic * TQ + (tt + 1) * 512],
                    start=(ic == 0), stop=(ic == NCK - 1)),
                   reads=[rg(f"wch{pw}", ic * 128, (ic + 1) * 128),
                          rg("ctxsb", ic * TQ + tt * 512, ic * TQ + (tt + 1) * 512)],
                   writes=[prg(ps)], tag=f"omm{hoc}_{tt}_{ic}")
            em("scalar", lambda h, hoc=hoc, pf=pf, ps=ps: h.activation(
                A[f"ytmp{pf}"][:, :], PS[ps][:, :], AF.Identity,
                bias=A["boc"][:, hoc:hoc + 1], scale=1.0),
               reads=[prg(ps), rg("boc", hoc, hoc + 1)],
               writes=[rg(f"ytmp{pf}")], tag=f"oev{hoc}_{tt}")
            em("vector", lambda h, hoc=hoc, tt=tt, pw=pw, pf=pf: h.tensor_tensor(
                A["y1"][:, hoc * TQ + tt * 512: hoc * TQ + (tt + 1) * 512],
                A[f"ytmp{pf}"][:, :], A[f"xqs{pw}"][:, tt * 512:(tt + 1) * 512], ALU.add),
               reads=[rg(f"ytmp{pf}"), rg(f"xqs{pw}", tt * 512, (tt + 1) * 512)],
               writes=[rg("y1", hoc * TQ + tt * 512, hoc * TQ + (tt + 1) * 512)],
               tag=f"ores{hoc}_{tt}")
    if "y1" in dbg:
        for ic in range(NCK):
            em("sync", lambda h, ic=ic: h.dma_start(
                out=dbg["y1"][ic * 128:(ic + 1) * 128, :],
                in_=A["y1"][:, ic * TQ:(ic + 1) * TQ]),
               reads=[rg("y1", ic * TQ, (ic + 1) * TQ)], dma=True, tag=f"dbg_y1{ic}")

    if stop_after(3):
        return materialize()

    for ic in range(NCK):
        em("vector", lambda h, ic=ic: h.tensor_copy(
            A["ln2"][:, ic * TQ:(ic + 1) * TQ], A["y1"][:, ic * TQ:(ic + 1) * TQ]),
           reads=[rg("y1", ic * TQ, (ic + 1) * TQ)],
           writes=[rg("ln2", ic * TQ, (ic + 1) * TQ)], tag=f"l2c{ic}")
    ln_stats_center("ln2", TQ, TQ, "mbc", "ln2")
    if "ln2" in dbg:
        em("sync", lambda h: h.dma_start(out=dbg["ln2"][:, :], in_=A["ln2"][:, :]),
           reads=[rg("ln2")], dma=True, tag="dbg_ln2")

    if stop_after(4):
        return materialize()

    TK = A["tk"]
    for tt in range(2):
        ps = ("a", "b")[tt]
        for ic in range(NCK):
            em("tensor", lambda h, ic=ic, tt=tt, ps=ps: h.matmul(
                PS[ps][0:8, :], A["Gb"][:, ic * 8:(ic + 1) * 8],
                A["ln2"][:, ic * TQ + tt * 512: ic * TQ + (tt + 1) * 512],
                start=(ic == 0), stop=(ic == NCK - 1)),
               reads=[rg("Gb", ic * 8, (ic + 1) * 8),
                      rg("ln2", ic * TQ + tt * 512, ic * TQ + (tt + 1) * 512)],
               writes=[prg(ps, 0, 512)], tag=f"lg{tt}_{ic}")
        em("scalar", lambda h, ps=ps: h.copy(A["logsb"][0:8, :], PS[ps][0:8, :]),
           reads=[prg(ps, 0, 512)], writes=[rg("logsb")], tag=f"lge{tt}")
        for tg4 in range(4):
            tg = tt * 4 + tg4
            em("tensor", lambda h, tg4=tg4: h.transpose(
                PS["t2"][:, 0:8], A["logsb"][0:8, tg4 * 128:(tg4 + 1) * 128],
                A["id8f"][0:8, :]),
               reads=[rg("logsb", tg4 * 128, (tg4 + 1) * 128), rg("id8f")],
               writes=[prg("t2", 0, 8)], tag=f"rtp{tg}")
            em("vector", lambda h: h.tensor_copy(TK[:, 0:8], PS["t2"][:, 0:8]),
               reads=[prg("t2", 0, 8)], writes=[rg("tk", 0, 8)], tag=f"rtc{tg}")
            em("vector", lambda h: h.reduce_max(TK[:, 16:17], TK[:, 0:8], axis=AX.X,
                                                negate=True),
               reads=[rg("tk", 0, 8)], writes=[rg("tk", 16, 17)], tag=f"rmx{tg}")
            em("scalar", lambda h: h.activation(TK[:, 8:16], TK[:, 0:8], AF.Exp,
                                                bias=TK[:, 16:17], scale=1.0),
               reads=[rg("tk", 0, 8), rg("tk", 16, 17)], writes=[rg("tk", 8, 16)],
               tag=f"rex{tg}")
            em("vector", lambda h: h.reduce_sum(TK[:, 17:18], TK[:, 8:16], axis=AX.X),
               reads=[rg("tk", 8, 16)], writes=[rg("tk", 17, 18)], tag=f"rsm{tg}")
            em("vector", lambda h: h.reciprocal(TK[:, 18:19], TK[:, 17:18]),
               reads=[rg("tk", 17, 18)], writes=[rg("tk", 18, 19)], tag=f"rrc{tg}")
            em("vector", lambda h: h.tensor_scalar(TK[:, 24:32], TK[:, 8:16],
                                                   TK[:, 18:19], None, ALU.mult),
               reads=[rg("tk", 8, 16), rg("tk", 18, 19)], writes=[rg("tk", 24, 32)],
               tag=f"rp{tg}")
            em("vector", lambda h: h.reduce_max(TK[:, 19:20], TK[:, 24:32], axis=AX.X),
               reads=[rg("tk", 24, 32)], writes=[rg("tk", 19, 20)], tag=f"rpm{tg}")
            em("vector", lambda h: h.tensor_scalar(TK[:, 32:40], TK[:, 24:32],
                                                   TK[:, 19:20], None, ALU.is_lt),
               reads=[rg("tk", 24, 32), rg("tk", 19, 20)], writes=[rg("tk", 32, 40)],
               tag=f"rlt{tg}")
            em("vector", lambda h: h.tensor_tensor(TK[:, 32:40], TK[:, 32:40],
                                                   TK[:, 24:32], ALU.mult),
               reads=[rg("tk", 32, 40), rg("tk", 24, 32)], writes=[rg("tk", 32, 40)],
               tag=f"rl2{tg}")
            em("vector", lambda h: h.reduce_max(TK[:, 20:21], TK[:, 32:40], axis=AX.X),
               reads=[rg("tk", 32, 40)], writes=[rg("tk", 20, 21)], tag=f"rsc{tg}")
            em("vector", lambda h: h.tensor_scalar(TK[:, 40:48], TK[:, 24:32],
                                                   TK[:, 20:21], None, ALU.is_ge),
               reads=[rg("tk", 24, 32), rg("tk", 20, 21)], writes=[rg("tk", 40, 48)],
               tag=f"rge{tg}")
            em("vector", lambda h: h.tensor_tensor(TK[:, 48:56], TK[:, 40:48],
                                                   TK[:, 24:32], ALU.mult),
               reads=[rg("tk", 40, 48), rg("tk", 24, 32)], writes=[rg("tk", 48, 56)],
               tag=f"rw{tg}")
            em("vector", lambda h: h.tensor_copy(A["wtk"][:, 0:8], TK[:, 48:56]),
               reads=[rg("tk", 48, 56)], writes=[rg("wtk")], tag=f"rwb{tg}")
            em("tensor", lambda h: h.transpose(PS["t"][0:8, 0:128], A["wtk"][:, 0:8],
                                               A["id128"][:, :]),
               reads=[rg("wtk"), rg("id128")], writes=[prg("t")], tag=f"rtb{tg}")
            em("scalar", lambda h, tg=tg: h.copy(
                A["w8sb"][0:8, tg * 128:(tg + 1) * 128], PS["t"][0:8, 0:128]),
               reads=[prg("t")], writes=[rg("w8sb", tg * 128, (tg + 1) * 128)],
               tag=f"rwe{tg}")
    if "w8" in dbg:
        em("sync", lambda h: h.dma_start(out=dbg["w8"][:, :], in_=A["w8sb"][0:8, :]),
           reads=[rg("w8sb")], dma=True, tag="dbg_w8")
    for g in range(2):
        ps = ("c0", "c1")[g]
        em("tensor", lambda h, g=g, ps=ps: h.matmul(
            PS[ps][:, :], A["eer"][0:8, :], A["w8sb"][0:8, g * 512:(g + 1) * 512],
            start=True, stop=True),
           reads=[rg("eer"), rg("w8sb", g * 512, (g + 1) * 512)], writes=[prg(ps)],
           tag=f"wer{g}")
        em("scalar", lambda h, g=g, ps=ps: h.copy(
            A["wer"][:, g * 512:(g + 1) * 512], PS[ps][:, :]),
           reads=[prg(ps)], writes=[rg("wer", g * 512, (g + 1) * 512)], tag=f"were{g}")
    for tt in range(2):
        ps = ("a", "b")[tt]
        for ic in range(NCK):
            em("tensor", lambda h, ic=ic, tt=tt, ps=ps: h.matmul(
                PS[ps][:, :], A["lATb"][:, ic * 128:(ic + 1) * 128],
                A["ln2"][:, ic * TQ + tt * 512: ic * TQ + (tt + 1) * 512],
                start=(ic == 0), stop=(ic == NCK - 1)),
               reads=[rg("lATb", ic * 128, (ic + 1) * 128),
                      rg("ln2", ic * TQ + tt * 512, ic * TQ + (tt + 1) * 512)],
               writes=[prg(ps)], tag=f"lrmm{tt}_{ic}")
        em("vector", lambda h, tt=tt, ps=ps: h.tensor_tensor(
            A["lrw"][:, tt * 512:(tt + 1) * 512], PS[ps][:, :],
            A["wer"][:, tt * 512:(tt + 1) * 512], ALU.mult),
           reads=[prg(ps), rg("wer", tt * 512, (tt + 1) * 512)],
           writes=[rg("lrw", tt * 512, (tt + 1) * 512)], tag=f"lrw{tt}")
    if "lrw" in dbg:
        em("sync", lambda h: h.dma_start(out=dbg["lrw"][:, :], in_=A["lrw"][:, :]),
           reads=[rg("lrw")], dma=True, tag="dbg_lrw")

    if stop_after(5):
        return materialize()

    for tt in range(2):
        load_band("wgh0", g_img, 0, f"ldg{tt}_0")
        load_band("wuh0", u_img, 0, f"ldu{tt}_0")
        for efc in range(NCK):
            pw = efc % 2
            e_ = efc // 2
            if efc + 1 < NCK:
                load_band(f"wgh{(efc + 1) % 2}", g_img, efc + 1, f"ldg{tt}_{efc + 1}")
                load_band(f"wuh{(efc + 1) % 2}", u_img, efc + 1, f"ldu{tt}_{efc + 1}")
            if efc % 2 == 0:
                pb = e_ % 2
                em("tensor", lambda h, e_=e_, tt=tt: h.matmul(
                    PS["t2"][:, :], A["e8s"][0:8, e_ * 128:(e_ + 1) * 128],
                    A["w8sb"][0:8, tt * 512:(tt + 1) * 512], start=True, stop=True),
                   reads=[rg("e8s", e_ * 128, (e_ + 1) * 128),
                          rg("w8sb", tt * 512, (tt + 1) * 512)],
                   writes=[prg("t2")], tag=f"wbc{tt}_{e_}")
                em("scalar", lambda h, pb=pb: h.copy(A[f"wbce{pb}"][:, :], PS["t2"][:, 0:512]),
                   reads=[prg("t2")], writes=[rg(f"wbce{pb}")], tag=f"wbe{tt}_{e_}")
            psg = ("a", "b")[efc % 2]
            psu = ("c0", "c1")[efc % 2]
            for ic in range(NCK):
                em("tensor", lambda h, ic=ic, tt=tt, pw=pw, psg=psg: h.matmul(
                    PS[psg][:, :], A[f"wgh{pw}"][:, ic * 128:(ic + 1) * 128],
                    A["ln2"][:, ic * TQ + tt * 512: ic * TQ + (tt + 1) * 512],
                    start=(ic == 0), stop=(ic == NCK - 1)),
                   reads=[rg(f"wgh{pw}", ic * 128, (ic + 1) * 128),
                          rg("ln2", ic * TQ + tt * 512, ic * TQ + (tt + 1) * 512)],
                   writes=[prg(psg)], tag=f"gmm{tt}_{efc}_{ic}")
                em("tensor", lambda h, ic=ic, tt=tt, pw=pw, psu=psu: h.matmul(
                    PS[psu][:, :], A[f"wuh{pw}"][:, ic * 128:(ic + 1) * 128],
                    A["ln2"][:, ic * TQ + tt * 512: ic * TQ + (tt + 1) * 512],
                    start=(ic == 0), stop=(ic == NCK - 1)),
                   reads=[rg(f"wuh{pw}", ic * 128, (ic + 1) * 128),
                          rg("ln2", ic * TQ + tt * 512, ic * TQ + (tt + 1) * 512)],
                   writes=[prg(psu)], tag=f"umm{tt}_{efc}_{ic}")
            sc0 = (efc % 2) * 512
            em("scalar", lambda h, sc0=sc0, psg=psg: h.activation(
                A["sil"][:, sc0:sc0 + 512], PS[psg][:, :], AF.Silu, bias=0.0, scale=1.0),
               reads=[prg(psg)], writes=[rg("sil", sc0, sc0 + 512)], tag=f"sil{tt}_{efc}")
            em("vector", lambda h, sc0=sc0, psu=psu: h.tensor_tensor(
                A["sil"][:, sc0:sc0 + 512], PS[psu][:, :], A["sil"][:, sc0:sc0 + 512],
                ALU.mult),
               reads=[prg(psu), rg("sil", sc0, sc0 + 512)],
               writes=[rg("sil", sc0, sc0 + 512)], tag=f"gu{tt}_{efc}")
            pb = (efc // 2) % 2
            em("vector", lambda h, efc=efc, sc0=sc0, pb=pb: h.tensor_tensor(
                A["gated"][:, efc * 512:(efc + 1) * 512], A["sil"][:, sc0:sc0 + 512],
                A[f"wbce{pb}"][:, :], ALU.mult),
               reads=[rg("sil", sc0, sc0 + 512), rg(f"wbce{pb}")],
               writes=[rg("gated", efc * 512, (efc + 1) * 512)], tag=f"gw{tt}_{efc}")
        if "gated" in dbg and tt == 0:
            em("sync", lambda h: h.dma_start(out=dbg["gated"][:, :], in_=A["gated"][:, :]),
               reads=[rg("gated")], dma=True, tag="dbg_gated")
        load_band("wch0", d_img, 0, f"ldd{tt}_0")
        for hoc in range(NCK):
            pw = hoc % 2
            if hoc + 1 < NCK:
                load_band(f"wch{(hoc + 1) % 2}", d_img, hoc + 1, f"ldd{tt}_{hoc + 1}")
            ps = ("a", "b")[hoc % 2]
            pf = hoc % 2
            for efc in range(NCK):
                em("tensor", lambda h, efc=efc, pw=pw, ps=ps: h.matmul(
                    PS[ps][:, :], A[f"wch{pw}"][:, efc * 128:(efc + 1) * 128],
                    A["gated"][:, efc * 512:(efc + 1) * 512],
                    start=(efc == 0), stop=False),
                   reads=[rg(f"wch{pw}", efc * 128, (efc + 1) * 128),
                          rg("gated", efc * 512, (efc + 1) * 512)],
                   writes=[prg(ps)], tag=f"dmm{tt}_{hoc}_{efc}")
            em("tensor", lambda h, hoc=hoc, tt=tt, ps=ps: h.matmul(
                PS[ps][:, :], A["lBTb"][:, hoc * 128:(hoc + 1) * 128],
                A["lrw"][:, tt * 512:(tt + 1) * 512], start=False, stop=True),
               reads=[rg("lBTb", hoc * 128, (hoc + 1) * 128),
                      rg("lrw", tt * 512, (tt + 1) * 512)],
               writes=[prg(ps)], tag=f"lbmm{tt}_{hoc}")
            em("vector", lambda h, hoc=hoc, tt=tt, pf=pf, ps=ps: h.tensor_tensor(
                A[f"fout{pf}"][:, :], PS[ps][:, :],
                A["y1"][:, hoc * TQ + tt * 512: hoc * TQ + (tt + 1) * 512], ALU.add),
               reads=[prg(ps), rg("y1", hoc * TQ + tt * 512, hoc * TQ + (tt + 1) * 512)],
               writes=[rg(f"fout{pf}")], tag=f"fres{tt}_{hoc}")
            em("sync", lambda h, hoc=hoc, tt=tt, pf=pf: h.dma_start(
                out=y_out[hoc * 128:(hoc + 1) * 128, tt * 512:(tt + 1) * 512],
                in_=A[f"fout{pf}"][:, :]),
               reads=[rg(f"fout{pf}")],
               writes=[("yout", (hoc * 2 + tt) * 512, (hoc * 2 + tt + 1) * 512)],
               dma=True, tag=f"st{tt}_{hoc}")

    em("gpsimd", lambda h: h.memset(A["joinpad"][:, 0:1], 0.0),
       reads=[("yout", 0, NCK * 2 * 512)], writes=[rg("joinpad")], tag="join")

    return materialize()


def _rope_tables(pos):
    j = np.arange(0, HD, 2, dtype=np.float32) / HD
    inv = (1.0 / 10000.0 ** j).astype(np.float32)
    fr = pos[:, None].astype(np.float32) * inv[None, :]
    cos = np.cos(fr).astype(np.float32)
    sin = np.sin(fr).astype(np.float32)
    cosT = np.ascontiguousarray(np.concatenate([cos, cos], axis=1).T)
    sinS = np.ascontiguousarray(np.concatenate([sin, -sin], axis=1).T)
    return cosT, sinS


def _band_img(T):
    nb = T.shape[1] // 128
    return np.ascontiguousarray(
        T.reshape(16, 128, nb, 128).transpose(2, 1, 0, 3).reshape(nb * 128, 2048))


def _vgroup_img(T):
    return np.ascontiguousarray(
        T.reshape(16, 128, 8, 256).transpose(2, 1, 0, 3).reshape(1024, 4096))


_NC_CACHE = [None]


def kernel(L0_attn_hidden_states, L0_attn_Wq, L0_attn_bq, L0_attn_Wk, L0_attn_bk,
           L0_attn_Wv, L0_attn_bv, L0_attn_Wo, L0_attn_bo, L0_ffn_G, L0_ffn_gate_w,
           L0_ffn_up_w, L0_ffn_down_w, L0_ffn_lora_A, L0_ffn_lora_B):
    x = np.ascontiguousarray(np.asarray(L0_attn_hidden_states, np.float32))
    wq_img = _band_img(np.asarray(L0_attn_Wq, np.float32).T)
    wk_img = _band_img(np.asarray(L0_attn_Wk, np.float32).T)
    wv_img = _vgroup_img(np.asarray(L0_attn_Wv, np.float32).T)
    wo_img = _band_img(np.asarray(L0_attn_Wo, np.float32).T)
    g_img = _band_img(np.asarray(L0_ffn_gate_w, np.float32).transpose(2, 0, 1).reshape(H, EF))
    u_img = _band_img(np.asarray(L0_ffn_up_w, np.float32).transpose(2, 0, 1).reshape(H, EF))
    d_img = _band_img(np.asarray(L0_ffn_down_w, np.float32).transpose(0, 2, 1).reshape(EF, H))
    bq = np.ascontiguousarray(np.asarray(L0_attn_bq, np.float32).reshape(16, 128).T)
    bk = np.ascontiguousarray(np.asarray(L0_attn_bk, np.float32).reshape(16, 128).T)
    bo = np.ascontiguousarray(np.asarray(L0_attn_bo, np.float32).reshape(16, 128).T)
    bv = np.ascontiguousarray(np.asarray(L0_attn_bv, np.float32).reshape(1, H))
    G = np.asarray(L0_ffn_G, np.float32)
    G_img = np.ascontiguousarray(G.reshape(16, 128, 8).transpose(1, 0, 2).reshape(128, 128))
    lAT = np.asarray(L0_ffn_lora_A, np.float32).transpose(2, 0, 1).reshape(H, 128)
    lAT_img = np.ascontiguousarray(
        lAT.reshape(16, 128, 128).transpose(1, 0, 2).reshape(128, 2048))
    lBT = np.ascontiguousarray(
        np.asarray(L0_ffn_lora_B, np.float32).transpose(0, 2, 1).reshape(128, H))

    if _NC_CACHE[0] is None or _NC_CACHE[0][0] != (PHASE_LIMIT, tuple(DEBUG_DUMP)):
        _NC_CACHE[0] = ((PHASE_LIMIT, tuple(DEBUG_DUMP)), build_nc())
    nc = _NC_CACHE[0][1]

    in_maps = []
    perms = []
    for c in range(8):
        b, h2 = c // 2, c % 2
        chunks = [2 * j + h2 for j in range(8)] + [2 * j + (1 - h2) for j in range(8)]
        cols = np.concatenate([np.arange(t * 128, (t + 1) * 128) for t in chunks])
        perms.append(cols)
        xTc = np.ascontiguousarray(x[b].T[:, cols])
        cosk, sink = _rope_tables(cols.astype(np.float32))
        mcol = np.full((128, 1), MASKNEG if h2 == 0 else 0.0, np.float32)
        in_maps.append({
            "xT": xTc, "wq_img": wq_img, "wk_img": wk_img, "wv_img": wv_img,
            "wo_img": wo_img, "g_img": g_img, "u_img": u_img, "d_img": d_img,
            "lAT": lAT_img, "lBT": lBT, "G_img": G_img,
            "bq_img": bq, "bk_img": bk, "bo_img": bo, "bv_row": bv,
            "cosk": cosk, "sink": sink, "maskcol": mcol,
        })
    res = run_bass_kernel_spmd(nc, in_maps, core_ids=list(range(8)))
    y = np.zeros((B, S, H), np.float32)
    for c in range(8):
        b = c // 2
        yq = res.results[c]["y_out"]
        y[b, perms[c][:TQ], :] = yq.T
    if DEBUG_DUMP:
        kernel._dbg = [{k: v for k, v in r.items() if k.startswith("dbg_")}
                       for r in res.results]
    return y


# version 10
# speedup vs baseline: 1.3678x; 1.3678x over previous
import numpy as np
import ml_dtypes
import concourse.bass as bass
import concourse.mybir as mybir
from concourse.bass_utils import run_bass_kernel_spmd

F32 = mybir.dt.float32
BF16 = mybir.dt.bfloat16
AF = mybir.ActivationFunctionType
ALU = mybir.AluOpType
AX = mybir.AxisListType
bfd = ml_dtypes.bfloat16

H = 2048
S = 2048
B = 4
NH = 16
HD = 128
NCK = 16
E = 8
EF = 2048
EPS = 1e-5
TQ = 1024
RSQ = float(1.0 / np.sqrt(np.float32(HD)))
MASKNEG = -1.0e9

ENGS = ("sync", "scalar", "vector", "tensor", "gpsimd")

DEBUG_DUMP = []
PHASE_LIMIT = 99
OP_LIMIT = None


LANES = {"gpsimd": 12, "sync": 6}


class Op:
    __slots__ = ("eng", "fn", "dma", "eidx", "deps", "inc", "waits", "tag", "chan",
                 "cidx")

    def __init__(self, eng, fn, dma, eidx, tag):
        self.eng = eng
        self.fn = fn
        self.dma = dma
        self.eidx = eidx
        self.deps = set()
        self.inc = False
        self.waits = []
        self.tag = tag
        self.chan = eng
        self.cidx = 0


class Prog:
    def __init__(self):
        self.eops = {e: [] for e in ENGS}
        self.segs = {}
        self.nemit = 0

    def emit(self, eng, fn, reads=(), writes=(), dma=False, tag=""):
        self.nemit += 1
        if OP_LIMIT is not None and self.nemit > OP_LIMIT and tag not in ("stub_out", "join"):
            return None
        op = Op(eng, fn, dma, len(self.eops[eng]), tag)
        self.eops[eng].append(op)
        for (sp, lo, hi) in reads:
            segs = self.segs.setdefault(sp, [])
            for seg in segs:
                if seg[0] < hi and lo < seg[1]:
                    if seg[2] is not None:
                        op.deps.add(seg[2])
                    seg[3].add(op)
        for (sp, lo, hi) in writes:
            segs = self.segs.setdefault(sp, [])
            new = []
            for seg in segs:
                if seg[0] < hi and lo < seg[1]:
                    if seg[2] is not None:
                        op.deps.add(seg[2])
                    op.deps.update(seg[3])
                    if seg[0] < lo:
                        new.append([seg[0], lo, seg[2], set(seg[3])])
                    if hi < seg[1]:
                        new.append([hi, seg[1], seg[2], set(seg[3])])
                else:
                    new.append(seg)
            new.append([lo, hi, op, set()])
            self.segs[sp] = new
        op.deps.discard(op)
        return op

    def finalize(self):
        lane_prev = {}
        for e in ENGS:
            nl = LANES.get(e, 0)
            ctr = 0
            for op in self.eops[e]:
                if op.dma and nl:
                    lane = ctr % nl
                    ctr += 1
                    op.chan = f"{e}:L{lane}"
                    prev = lane_prev.get(op.chan)
                    if prev is not None:
                        op.deps.add(prev)
                    lane_prev[op.chan] = op
        chans = list(ENGS) + sorted({op.chan for e in ENGS for op in self.eops[e]
                                     if op.chan not in ENGS})
        cctr = {c: 0 for c in chans}
        for e in ENGS:
            for op in self.eops[e]:
                op.cidx = cctr[op.chan]
                cctr[op.chan] += 1
        for e in ENGS:
            for op in self.eops[e]:
                best = {}
                for d in op.deps:
                    if d.eng == "tensor" and op.eng == "tensor":
                        continue
                    if d is op:
                        continue
                    cur = best.get(d.chan)
                    if cur is None or d.cidx > cur.cidx:
                        best[d.chan] = d
                op.deps = best
        wm = {e: {c: -1 for c in chans} for e in ENGS}
        for e in ENGS:
            w = wm[e]
            for op in self.eops[e]:
                kept = []
                for d in sorted(op.deps.values(), key=lambda d: d.chan):
                    if d.cidx <= w[d.chan]:
                        continue
                    if d.chan == op.chan and d.cidx >= op.cidx:
                        raise RuntimeError(f"self-dep ahead: {op.tag} <- {d.tag}")
                    w[d.chan] = d.cidx
                    kept.append(d)
                    d.inc = True
                op.deps = kept
        for e in ENGS:
            for op in self.eops[e]:
                if op.dma:
                    op.inc = True
        ccnt = {c: 0 for c in chans}
        cumv = {}
        for e in ENGS:
            for op in self.eops[e]:
                if op.inc:
                    ccnt[op.chan] += 16 if op.dma else 1
                cumv[(op.chan, op.cidx)] = ccnt[op.chan]
        for e in ENGS:
            for op in self.eops[e]:
                op.waits = [(d.chan, cumv[(d.chan, d.cidx)]) for d in op.deps]
        self.chans = chans
        pos = {e: 0 for e in ENGS}
        cnt = {c: 0 for c in chans}
        progress = True
        while progress:
            progress = False
            for e in ENGS:
                while pos[e] < len(self.eops[e]):
                    op = self.eops[e][pos[e]]
                    if all(cnt[se] >= v for (se, v) in op.waits):
                        if op.inc:
                            cnt[op.chan] += 16 if op.dma else 1
                        pos[e] += 1
                        progress = True
                    else:
                        break
        stuck = [e for e in ENGS if pos[e] < len(self.eops[e])]
        if stuck:
            msgs = []
            for e in stuck:
                op = self.eops[e][pos[e]]
                msgs.append(f"{e}[{pos[e]}] {op.tag} waits {op.waits}")
            raise RuntimeError("DEADLOCK:\n" + "\n".join(msgs))
        return cumv


def build_nc(phase_limit=None):
    if phase_limit is None:
        phase_limit = PHASE_LIMIT
    nc = bass.Bass("TRN2", target_bir_lowering=False, debug=False)

    xT = nc.dram_tensor("xT", [H, S], F32, kind="ExternalInput").ap()
    wq_img = nc.dram_tensor("wq_img", [H, H], F32, kind="ExternalInput").ap()
    wk_img = nc.dram_tensor("wk_img", [H, H], F32, kind="ExternalInput").ap()
    wv_img = nc.dram_tensor("wv_img", [1024, 4096], F32, kind="ExternalInput").ap()
    wo_img = nc.dram_tensor("wo_img", [H, H], F32, kind="ExternalInput").ap()
    g_img = nc.dram_tensor("g_img", [H, H], F32, kind="ExternalInput").ap()
    u_img = nc.dram_tensor("u_img", [H, H], F32, kind="ExternalInput").ap()
    d_img = nc.dram_tensor("d_img", [H, H], F32, kind="ExternalInput").ap()
    lATd = nc.dram_tensor("lAT", [128, 2048], F32, kind="ExternalInput").ap()
    lBTd = nc.dram_tensor("lBT", [128, H], F32, kind="ExternalInput").ap()
    Gd = nc.dram_tensor("G_img", [128, 128], F32, kind="ExternalInput").ap()
    bqd = nc.dram_tensor("bq_img", [128, 16], F32, kind="ExternalInput").ap()
    bkd = nc.dram_tensor("bk_img", [128, 16], F32, kind="ExternalInput").ap()
    bod = nc.dram_tensor("bo_img", [128, 16], F32, kind="ExternalInput").ap()
    bvd = nc.dram_tensor("bv_row", [1, H], F32, kind="ExternalInput").ap()
    coskd = nc.dram_tensor("cosk", [128, S], F32, kind="ExternalInput").ap()
    sinkd = nc.dram_tensor("sink", [128, S], F32, kind="ExternalInput").ap()
    mcold = nc.dram_tensor("maskcol", [128, 1], F32, kind="ExternalInput").ap()
    y_out = nc.dram_tensor("y_out", [H, TQ], F32, kind="ExternalOutput").ap()

    def bfc(a):
        return np.ascontiguousarray(np.asarray(a, np.float32).astype(bfd))

    tri_np = (np.arange(128)[:, None] <= np.arange(128)[None, :]).astype(np.float32)
    tri_d = nc.inline_tensor(bfc(tri_np), "c_tri")
    id128_d = nc.inline_tensor(bfc(np.eye(128)), "c_id128")
    id8f_d = nc.inline_tensor(np.ascontiguousarray(np.eye(8, dtype=np.float32)), "c_id8f")
    eer_np = np.zeros((8, 128), np.float32)
    for e_ in range(8):
        eer_np[e_, e_ * 16:(e_ + 1) * 16] = 1.0
    eer_d = nc.inline_tensor(bfc(eer_np), "c_eer")
    e8s_np = np.zeros((8, 8 * 128), np.float32)
    for e_ in range(8):
        e8s_np[e_, e_ * 128:(e_ + 1) * 128] = 1.0
    e8s_d = nc.inline_tensor(bfc(e8s_np), "c_e8s")

    dbg = {}
    for name, shape, dt in [
        ("xb", [128, NCK * S], BF16), ("qp", [128, TQ], BF16),
        ("kp", [128, S], BF16), ("vp", [128, 16 * 129], BF16),
        ("ctx", [H, TQ], BF16), ("y1", [H, TQ], F32),
        ("ln2", [128, NCK * TQ], BF16), ("w8", [8, TQ], BF16),
        ("gated", [128, NCK * 512], BF16), ("lrw", [128, TQ], BF16),
    ]:
        if name in DEBUG_DUMP:
            dbg[name] = nc.dram_tensor("dbg_" + name, shape, dt, kind="ExternalOutput").ap()

    SBUF_BASE = 16512
    SBUF_LIMIT = 212864
    A = {}
    meta = {}

    def place(name, ncols, dt, off):
        esz = 2 if dt == BF16 else 4
        off = (off + 31) // 32 * 32
        nb = ncols * esz
        assert off + nb <= SBUF_LIMIT, (name, off, nb)
        A[name] = nc.alloc_sbuf_tensor_at(name, [128, ncols], dt, offset=SBUF_BASE + off)
        meta[name] = (off, esz, ncols)
        return off + nb

    o = 0
    for nm, ncols, dt in [
        ("cosk", S, BF16), ("sink", S, BF16), ("bvbc", H, BF16),
        ("lATb", 2048, BF16), ("lBTb", H, BF16), ("Gb", 128, BF16),
        ("tri", 128, BF16), ("id128", 128, BF16), ("id8f", 8, F32),
        ("eer", 128, BF16), ("e8s", 8 * 128, BF16), ("bqc", 16, F32), ("bkc", 16, F32), ("boc", 16, F32),
        ("mcol", 1, F32), ("onecb", 1, BF16), ("onerf", 128, F32),
        ("onerb", 128, BF16), ("mbc", S, BF16), ("ibc", S, BF16),
        ("joinpad", 16, F32),
    ]:
        o = place(nm, ncols, dt, o)
    REG_P_END = o
    REG_X = (REG_P_END + 31) // 32 * 32
    place("xb", NCK * S, BF16, REG_X)
    place("y1", NCK * TQ, F32, REG_X)
    REG_X_END = REG_X + NCK * TQ * 4
    REG_C = (REG_X_END + 31) // 32 * 32
    place("ctxsb", NCK * TQ, BF16, REG_C)
    place("ln2", NCK * TQ, BF16, REG_C)
    REG_C_END = REG_C + NCK * TQ * 2
    o = (REG_C_END + 31) // 32 * 32
    REG_W = o
    for nm, ncols, dt in [
        ("wqh0", 2048, BF16), ("wqh1", 2048, BF16),
        ("wkh0", 2048, BF16), ("wkh1", 2048, BF16),
        ("wvg0", 4096, BF16), ("wvg1", 4096, BF16),
        ("vp0", 16 * 129, BF16), ("vp1", 16 * 129, BF16),
        ("qp0", TQ, BF16), ("qp1", TQ, BF16),
        ("kp0", S, BF16), ("kp1", S, BF16),
        ("pt0", TQ, BF16), ("pt1", TQ, BF16), ("pt2", TQ, BF16),
        ("qktmp", S, BF16), ("r3", S, BF16),
        ("ctxq", 256, BF16), ("rcol", 8, F32),
        ("rows", 1024, F32), ("rows2", 1024, F32), ("rsc", 512, F32),
        ("sq0", 512, BF16), ("sq1", 512, BF16), ("sq2", 512, BF16),
    ]:
        o = place(nm, ncols, dt, o)
    W12_END = o
    o = REG_W
    for nm, ncols, dt in [
        ("gated", NCK * 512, BF16),
        ("wch0", 2048, BF16), ("wch1", 2048, BF16),
        ("wgh0", 2048, BF16), ("wgh1", 2048, BF16),
        ("wuh0", 2048, BF16), ("wuh1", 2048, BF16),
        ("xqs0", TQ, F32), ("xqs1", TQ, F32),
        ("fout0", 512, F32), ("fout1", 512, F32),
        ("ytmp0", 512, F32), ("ytmp1", 512, F32),
        ("wbce0", 512, BF16), ("wbce1", 512, BF16),
        ("wer", TQ, BF16), ("lrw", TQ, BF16),
        ("logsb", 1024, F32), ("tk", 64, F32), ("wtk", 8, BF16),
        ("w8sb", TQ, BF16), ("sil", TQ, BF16),
    ]:
        o = place(nm, ncols, dt, o)
    W36_END = o
    assert max(W12_END, W36_END) <= SBUF_LIMIT, (W12_END, W36_END)

    PS = {}
    psmeta = {}
    psoff = [0]

    def psalloc(name, ncols, dt):
        t = nc.alloc_psum_tensor("ps_" + name, [128, ncols], dt)
        PS[name] = t
        esz = 2 if dt == BF16 else 4
        psmeta[name] = (psoff[0], esz)
        psoff[0] += 2048 * ((ncols * esz + 2047) // 2048)

    psalloc("a", 512, F32)
    psalloc("b", 512, F32)
    psalloc("s0", 512, F32)
    psalloc("s1", 512, F32)
    psalloc("c0", 512, F32)
    psalloc("c1", 512, F32)
    psalloc("t", 128, BF16)
    psalloc("t2", 512, F32)
    assert psoff[0] <= 8 * 2048, psoff[0]

    def rg(name, c0=None, c1=None):
        off, esz, ncols = meta[name]
        if c0 is None:
            c0, c1 = 0, ncols
        return ("sbuf", off + c0 * esz, off + c1 * esz)

    def prg(name, c0=None, c1=None):
        off, esz = psmeta[name]
        if c0 is None:
            return ("psum", off, off + 2048)
        return ("psum", off + c0 * esz, off + c1 * esz)

    P = Prog()
    em = P.emit

    def materialize():
        P.finalize()
        import contextlib
        with contextlib.ExitStack() as stack:
            sems = {}
            for c in P.chans:
                sems[c] = stack.enter_context(nc.semaphore("sem_" + c.replace(":", "_")))
            block = stack.enter_context(nc.Block())

            def runner(eng):
                def run(h):
                    for op in P.eops[eng]:
                        for (se, v) in op.waits:
                            h.wait_ge(sems[se], v)
                        inst = op.fn(h)
                        if op.inc:
                            inst.then_inc(sems[op.chan], 16 if op.dma else 1)
                return run

            block.sync(runner("sync"))
            block.scalar(runner("scalar"))
            block.vector(runner("vector"))
            block.tensor(runner("tensor"))
            block.gpsimd(runner("gpsimd"))
        return nc

    def stop_after(k):
        if phase_limit > k:
            return False
        em("gpsimd", lambda h: h.dma_start(out=y_out[0:128, :], in_=A["mbc"][:, 0:TQ]),
           reads=[rg("mbc", 0, TQ)], writes=[("yout", 0, 512)], dma=True, tag="stub_out")
        em("gpsimd", lambda h: h.memset(A["joinpad"][:, 0:1], 0.0),
           reads=[("yout", 0, 512)], writes=[rg("joinpad")], tag="join")
        return True

    for nm, src, np_ in [("tri", tri_d.ap(), 128), ("id128", id128_d.ap(), 128),
                         ("eer", eer_d.ap(), 8), ("e8s", e8s_d.ap(), 8)]:
        em("sync", lambda h, nm=nm, src=src, np_=np_: h.dma_start(out=A[nm][0:np_, :], in_=src),
           writes=[rg(nm)], dma=True, tag="ld_" + nm)
    em("sync", lambda h: h.dma_start(out=A["id8f"][0:8, :], in_=id8f_d.ap()),
       writes=[rg("id8f")], dma=True, tag="ld_id8f")
    for nm, src in [("bqc", bqd), ("bkc", bkd), ("boc", bod), ("mcol", mcold)]:
        em("sync", lambda h, nm=nm, src=src: h.dma_start(out=A[nm][:], in_=src[:, :]),
           writes=[rg(nm)], dma=True, tag="ld_" + nm)
    em("sync", lambda h: h.dma_start(out=A["rows"][0:1, :], in_=bvd[0:1, 0:1024]),
       writes=[rg("rows")], dma=True, tag="ld_bv0")
    em("sync", lambda h: h.dma_start(out=A["rows2"][0:1, :], in_=bvd[0:1, 1024:2048]),
       writes=[rg("rows2")], dma=True, tag="ld_bv1")
    for nm, src in [("cosk", coskd), ("sink", sinkd), ("Gb", Gd)]:
        em("gpsimd", lambda h, nm=nm, src=src: h.dma_start(out=A[nm][:], in_=src[:, :]),
           writes=[rg(nm)], dma=True, tag="ld_" + nm)
    em("gpsimd", lambda h: h.dma_start(out=A["lATb"][:], in_=lATd[:, :]),
       writes=[rg("lATb")], dma=True, tag="ld_lAT")
    em("gpsimd", lambda h: h.dma_start(out=A["lBTb"][:], in_=lBTd[:, :]),
       writes=[rg("lBTb")], dma=True, tag="ld_lBT")
    em("vector", lambda h: h.memset(A["onecb"][:], 1.0), writes=[rg("onecb")], tag="ms1")
    em("vector", lambda h: h.memset(A["onerf"][0:1, :], 1.0), writes=[rg("onerf")], tag="ms2")
    em("vector", lambda h: h.memset(A["onerb"][0:1, :], 1.0), writes=[rg("onerb")], tag="ms3")
    for p in range(2):
        em("vector", lambda h, p=p: h.memset(A[f"vp{p}"][:, 128:16 * 129:129], 1.0),
           writes=[rg(f"vp{p}")], tag=f"ms_vp{p}")
    for g in range(4):
        src = "rows" if g < 2 else "rows2"
        sc0 = (g % 2) * 512
        ps = ("a", "b")[g % 2]
        em("tensor", lambda h, src=src, sc0=sc0, ps=ps: h.matmul(
            PS[ps][:, :], A["onerf"][0:1, :], A[src][0:1, sc0:sc0 + 512],
            start=True, stop=True),
           reads=[rg("onerf"), rg(src, sc0, sc0 + 512)], writes=[prg(ps)], tag=f"bvb{g}")
        em("scalar", lambda h, g=g, ps=ps: h.copy(
            A["bvbc"][:, g * 512:(g + 1) * 512], PS[ps][:, :]),
           reads=[prg(ps)], writes=[rg("bvbc", g * 512, (g + 1) * 512)], tag=f"bve{g}")

    if stop_after(0):
        return materialize()

    for ic in range(NCK):
        em("gpsimd", lambda h, ic=ic: h.dma_start(
            out=A["xb"][:, ic * S:(ic + 1) * S], in_=xT[ic * 128:(ic + 1) * 128, :]),
           writes=[rg("xb", ic * S, (ic + 1) * S)], dma=True, tag=f"ld_x{ic}")

    def ln_stats_center(buf, bufstride, ntok, mdst, tag, upto=3):
        nhalf = ntok // 1024
        for hf in range(max(nhalf, 1)):
            if upto < 1:
                break
            t0 = hf * 1024
            tn = min(1024, ntok - t0)
            for ic in range(NCK):
                for g in range(2):
                    if g * 512 >= tn:
                        continue
                    sqb = f"sq{(ic * 2 + g) % 3}"
                    ps = ("a", "b")[g]
                    ps2 = ("c0", "c1")[g]
                    c0 = t0 + g * 512
                    em("vector", lambda h, ic=ic, c0=c0, sqb=sqb: h.tensor_tensor(
                        A[sqb][:, 0:512],
                        A[buf][:, ic * bufstride + c0: ic * bufstride + c0 + 512],
                        A[buf][:, ic * bufstride + c0: ic * bufstride + c0 + 512], ALU.mult),
                       reads=[rg(buf, ic * bufstride + c0, ic * bufstride + c0 + 512)],
                       writes=[rg(sqb, 0, 512)], tag=f"{tag}sq{hf}{g}_{ic}")
                    em("tensor", lambda h, ic=ic, c0=c0, ps=ps: h.matmul(
                        PS[ps][0:1, :], A["onecb"][:, :],
                        A[buf][:, ic * bufstride + c0: ic * bufstride + c0 + 512],
                        start=(ic == 0), stop=(ic == NCK - 1)),
                       reads=[rg("onecb"), rg(buf, ic * bufstride + c0, ic * bufstride + c0 + 512)],
                       writes=[prg(ps, 0, 512)], tag=f"{tag}sx{hf}{g}_{ic}")
                    em("tensor", lambda h, ic=ic, sqb=sqb, ps2=ps2: h.matmul(
                        PS[ps2][0:1, :], A["onecb"][:, :], A[sqb][:, 0:512],
                        start=(ic == 0), stop=(ic == NCK - 1)),
                       reads=[rg("onecb"), rg(sqb, 0, 512)],
                       writes=[prg(ps2, 0, 512)], tag=f"{tag}s2{hf}{g}_{ic}")
            if upto < 2:
                continue
            for g in range(2):
                if g * 512 >= tn:
                    continue
                ps = ("a", "b")[g]
                ps2 = ("c0", "c1")[g]
                gc = g * 512
                em("vector", lambda h, gc=gc, ps=ps: h.tensor_scalar(
                    A["rows"][0:1, gc:gc + 512], PS[ps][0:1, :], 1.0 / H, None, ALU.mult),
                   reads=[prg(ps, 0, 512)], writes=[rg("rows", gc, gc + 512)],
                   tag=f"{tag}m{hf}{g}")
                em("vector", lambda h, gc=gc: h.tensor_tensor(
                    A["rsc"][0:1, :], A["rows"][0:1, gc:gc + 512],
                    A["rows"][0:1, gc:gc + 512], ALU.mult),
                   reads=[rg("rows", gc, gc + 512)], writes=[rg("rsc")], tag=f"{tag}ms{hf}{g}")
                em("vector", lambda h, gc=gc, ps2=ps2: h.tensor_scalar(
                    A["rows2"][0:1, gc:gc + 512], PS[ps2][0:1, :], 1.0 / H, None, ALU.mult),
                   reads=[prg(ps2, 0, 512)], writes=[rg("rows2", gc, gc + 512)],
                   tag=f"{tag}v{hf}{g}")
                em("vector", lambda h, gc=gc: h.tensor_tensor(
                    A["rows2"][0:1, gc:gc + 512], A["rows2"][0:1, gc:gc + 512],
                    A["rsc"][0:1, :], ALU.subtract),
                   reads=[rg("rows2", gc, gc + 512), rg("rsc")],
                   writes=[rg("rows2", gc, gc + 512)], tag=f"{tag}v2{hf}{g}")
                em("vector", lambda h, gc=gc: h.tensor_scalar(
                    A["rows2"][0:1, gc:gc + 512], A["rows2"][0:1, gc:gc + 512],
                    float(EPS), None, ALU.add),
                   reads=[rg("rows2", gc, gc + 512)], writes=[rg("rows2", gc, gc + 512)],
                   tag=f"{tag}v3{hf}{g}")
                em("scalar", lambda h, gc=gc: h.activation(
                    A["rsc"][0:1, :], A["rows2"][0:1, gc:gc + 512], AF.Sqrt,
                    bias=0.0, scale=1.0),
                   reads=[rg("rows2", gc, gc + 512)], writes=[rg("rsc")], tag=f"{tag}sd{hf}{g}")
                em("vector", lambda h, gc=gc: h.reciprocal(
                    A["rows2"][0:1, gc:gc + 512], A["rsc"][0:1, :]),
                   reads=[rg("rsc")], writes=[rg("rows2", gc, gc + 512)], tag=f"{tag}i{hf}{g}")
                for (src, dst) in (("rows", "mbc"), ("rows2", "ibc")):
                    ps3 = (("a", "c0"), ("b", "c1"))[g][src == "rows2"]
                    em("tensor", lambda h, gc=gc, src=src, ps3=ps3: h.matmul(
                        PS[ps3][:, :], A["onerf"][0:1, :], A[src][0:1, gc:gc + 512],
                        start=True, stop=True),
                       reads=[rg("onerf"), rg(src, gc, gc + 512)], writes=[prg(ps3)],
                       tag=f"{tag}bc{hf}{g}{dst}")
                    em("scalar", lambda h, t0=t0, gc=gc, dst=dst, ps3=ps3: h.copy(
                        A[dst][:, t0 + gc: t0 + gc + 512], PS[ps3][:, :]),
                       reads=[prg(ps3)], writes=[rg(dst, t0 + gc, t0 + gc + 512)],
                       tag=f"{tag}be{hf}{g}{dst}")
        if upto < 3:
            return
        for ic in range(NCK):
            em("vector", lambda h, ic=ic: h.tensor_tensor(
                A[buf][:, ic * bufstride: ic * bufstride + ntok],
                A[buf][:, ic * bufstride: ic * bufstride + ntok],
                A["mbc"][:, 0:ntok], ALU.subtract),
               reads=[rg(buf, ic * bufstride, ic * bufstride + ntok), rg("mbc", 0, ntok)],
               writes=[rg(buf, ic * bufstride, ic * bufstride + ntok)], tag=f"{tag}ce{ic}")
            em("vector", lambda h, ic=ic: h.tensor_tensor(
                A[buf][:, ic * bufstride: ic * bufstride + ntok],
                A[buf][:, ic * bufstride: ic * bufstride + ntok],
                A["ibc"][:, 0:ntok], ALU.mult),
               reads=[rg(buf, ic * bufstride, ic * bufstride + ntok), rg("ibc", 0, ntok)],
               writes=[rg(buf, ic * bufstride, ic * bufstride + ntok)], tag=f"{tag}cs{ic}")

    if stop_after(0.2):
        return materialize()
    ln_stats_center("xb", S, S, "mbc", "ln1",
                    upto=(0 if phase_limit <= 0.4 else 1 if phase_limit <= 0.6
                          else 2 if phase_limit <= 0.8 else 3))
    if "xb" in dbg:
        em("sync", lambda h: h.dma_start(out=dbg["xb"][:, :], in_=A["xb"][:, :]),
           reads=[rg("xb")], dma=True, tag="dbg_xb")

    if stop_after(1):
        return materialize()

    def load_head_weights(hd):
        p = hd % 2
        em("gpsimd", lambda h, hd=hd, p=p: h.dma_start(
            out=A[f"wqh{p}"][:, :], in_=wq_img[hd * 128:(hd + 1) * 128, :]),
           writes=[rg(f"wqh{p}")], dma=True, tag=f"ldwq{hd}")
        em("gpsimd", lambda h, hd=hd, p=p: h.dma_start(
            out=A[f"wkh{p}"][:, :], in_=wk_img[hd * 128:(hd + 1) * 128, :]),
           writes=[rg(f"wkh{p}")], dma=True, tag=f"ldwk{hd}")
        if hd % 2 == 0:
            g = hd // 2
            pv = g % 2
            em("gpsimd", lambda h, g=g, pv=pv: h.dma_start(
                out=A[f"wvg{pv}"][:, :], in_=wv_img[g * 128:(g + 1) * 128, :]),
               writes=[rg(f"wvg{pv}")], dma=True, tag=f"ldwv{g}")

    PROT = [0]

    def proj_qk(hd, kind):
        p = hd % 2
        wb = f"wqh{p}" if kind == "q" else f"wkh{p}"
        dst = f"qp{p}" if kind == "q" else f"kp{p}"
        bias = "bqc" if kind == "q" else "bkc"
        ntt = 2 if kind == "q" else 4
        n = ntt * 512
        for tt in range(ntt):
            ps = ("a", "b", "t2")[(PROT[0] + tt) % 3]
            for ic in range(NCK):
                em("tensor", lambda h, ic=ic, tt=tt, wb=wb, ps=ps: h.matmul(
                    PS[ps][:, :], A[wb][:, ic * 128:(ic + 1) * 128],
                    A["xb"][:, ic * S + tt * 512: ic * S + (tt + 1) * 512],
                    start=(ic == 0), stop=(ic == NCK - 1)),
                   reads=[rg(wb, ic * 128, (ic + 1) * 128),
                          rg("xb", ic * S + tt * 512, ic * S + (tt + 1) * 512)],
                   writes=[prg(ps)], tag=f"{kind}mm{hd}_{tt}_{ic}")
            em("scalar", lambda h, tt=tt, hd=hd, bias=bias, ps=ps: h.activation(
                A["qktmp"][:, tt * 512:(tt + 1) * 512], PS[ps][:, :], AF.Identity,
                bias=A[bias][:, hd:hd + 1], scale=1.0),
               reads=[prg(ps), rg(bias, hd, hd + 1)],
               writes=[rg("qktmp", tt * 512, (tt + 1) * 512)], tag=f"{kind}ev{hd}_{tt}")
        PROT[0] = (PROT[0] + ntt) % 3
        em("vector", lambda h, dst=dst, n=n: h.tensor_tensor(
            A[dst][:, 0:n], A["qktmp"][:, 0:n], A["cosk"][:, 0:n], ALU.mult),
           reads=[rg("qktmp", 0, n), rg("cosk", 0, n)], writes=[rg(dst, 0, n)],
           tag=f"{kind}rc{hd}")
        em("vector", lambda h, n=n: h.tensor_tensor(
            A["r3"][0:64, 0:n], A["qktmp"][64:128, 0:n], A["sink"][64:128, 0:n], ALU.mult),
           reads=[rg("qktmp", 0, n), rg("sink", 0, n)], writes=[rg("r3", 0, n)],
           tag=f"{kind}rs1{hd}")
        em("vector", lambda h, n=n: h.tensor_tensor(
            A["r3"][64:128, 0:n], A["qktmp"][0:64, 0:n], A["sink"][0:64, 0:n], ALU.mult),
           reads=[rg("qktmp", 0, n), rg("sink", 0, n)], writes=[rg("r3", 0, n)],
           tag=f"{kind}rs2{hd}")
        em("vector", lambda h, dst=dst, n=n: h.tensor_tensor(
            A[dst][:, 0:n], A[dst][:, 0:n], A["r3"][:, 0:n], ALU.add),
           reads=[rg(dst, 0, n), rg("r3", 0, n)], writes=[rg(dst, 0, n)],
           tag=f"{kind}ra{hd}")

    def proj_v(hd):
        g = hd // 2
        pv = g % 2
        for tc in range(NCK):
            ps = ("a", "b", "t2")[(PROT[0] + tc) % 3]
            for ic in range(NCK):
                em("tensor", lambda h, ic=ic, tc=tc, pv=pv, ps=ps: h.matmul(
                    PS[ps][:, 0:256], A["xb"][:, ic * S + tc * 128: ic * S + (tc + 1) * 128],
                    A[f"wvg{pv}"][:, ic * 256:(ic + 1) * 256],
                    start=(ic == 0), stop=(ic == NCK - 1)),
                   reads=[rg("xb", ic * S + tc * 128, ic * S + (tc + 1) * 128),
                          rg(f"wvg{pv}", ic * 256, (ic + 1) * 256)],
                   writes=[prg(ps, 0, 256)], tag=f"vmm{g}_{tc}_{ic}")
            for sub in range(2):
                hh = g * 2 + sub
                pw = hh % 2
                em("vector", lambda h, tc=tc, sub=sub, hh=hh, pw=pw, ps=ps: h.tensor_tensor(
                    A[f"vp{pw}"][:, tc * 129: tc * 129 + 128],
                    PS[ps][:, sub * 128:(sub + 1) * 128],
                    A["bvbc"][:, hh * 128:(hh + 1) * 128], ALU.add),
                   reads=[prg(ps, sub * 128, (sub + 1) * 128),
                          rg("bvbc", hh * 128, (hh + 1) * 128)],
                   writes=[rg(f"vp{pw}", tc * 129, tc * 129 + 128)],
                   tag=f"vev{hh}_{tc}")
        PROT[0] = (PROT[0] + NCK) % 3

    def attn(hd):
        p = hd % 2
        ctx_slot = [("c0", 0), ("c1", 0)]
        tilectr = [0]
        for grp in range(4):
            slots = list(range(grp * 2, grp * 2 + 2))
            kcs = [kc for kc in range(16) if (kc % 8) <= slots[-1]]
            pend = []
            for kc in kcs:
                j0 = max(kc % 8, slots[0])
                if j0 > slots[-1]:
                    continue
                n = (slots[-1] - j0 + 1) * 128
                q0 = j0 * 128
                sl = tilectr[0] % 2
                tilectr[0] += 1
                pp = f"pt{sl}"
                ps, so = (("s0", 0), ("s1", 0))[sl]
                w = n
                em("tensor", lambda h, kc=kc, q0=q0, w=w, p=p, ps=ps, so=so: h.matmul(
                    PS[ps][:, so:so + w], A[f"kp{p}"][:, kc * 128:(kc + 1) * 128],
                    A[f"qp{p}"][:, q0: q0 + w], start=True, stop=True),
                   reads=[rg(f"kp{p}", kc * 128, (kc + 1) * 128),
                          rg(f"qp{p}", q0, q0 + w)],
                   writes=[prg(ps, so, so + w)], tag=f"sc{hd}_{kc}")
                if kc >= 8 and j0 == kc - 8:
                    em("scalar", lambda h, ps=ps, pp=pp, q0=q0, so=so: h.activation(
                        A[pp][:, q0: q0 + 128], PS[ps][:, so:so + 128], AF.Exp,
                        bias=A["mcol"][:, 0:1], scale=RSQ),
                       reads=[prg(ps, so, so + 128), rg("mcol")],
                       writes=[rg(pp, q0, q0 + 128)], tag=f"exm{hd}_{kc}")
                    if w > 128:
                        em("scalar", lambda h, ps=ps, pp=pp, q0=q0, w=w, so=so: h.activation(
                            A[pp][:, q0 + 128: q0 + w], PS[ps][:, so + 128:so + w], AF.Exp,
                            bias=0.0, scale=RSQ),
                           reads=[prg(ps, so + 128, so + w)],
                           writes=[rg(pp, q0 + 128, q0 + w)], tag=f"exr{hd}_{kc}")
                else:
                    em("scalar", lambda h, ps=ps, pp=pp, q0=q0, w=w, so=so: h.activation(
                        A[pp][:, q0: q0 + w], PS[ps][:, so:so + w], AF.Exp,
                        bias=0.0, scale=RSQ),
                       reads=[prg(ps, so, so + w)], writes=[rg(pp, q0, q0 + w)],
                       tag=f"ex{hd}_{kc}")
                if kc < 8 and j0 == kc:
                    em("vector", lambda h, pp=pp, q0=q0: h.tensor_tensor(
                        A[pp][:, q0: q0 + 128], A[pp][:, q0: q0 + 128],
                        A["tri"][:, :], ALU.mult),
                       reads=[rg(pp, q0, q0 + 128), rg("tri")],
                       writes=[rg(pp, q0, q0 + 128)], tag=f"tri{hd}_{kc}")

                def ctx_emit(kc=kc, j0=j0, w=w, pp=pp):
                    for j in range(j0, j0 + w // 128):
                        bank, boff = ctx_slot[j % 2]
                        em("tensor", lambda h, j=j, kc=kc, p=p, pp=pp, bank=bank,
                           boff=boff: h.matmul(
                            PS[bank][:, boff:boff + 129], A[pp][:, j * 128:(j + 1) * 128],
                            A[f"vp{p}"][:, kc * 129:(kc + 1) * 129],
                            start=(kc == 0), stop=(kc == 8 + j)),
                           reads=[rg(pp, j * 128, (j + 1) * 128),
                                  rg(f"vp{p}", kc * 129, (kc + 1) * 129)],
                           writes=[prg(bank, boff, boff + 129)], tag=f"cx{hd}_{kc}_{j}")
                pend.append(ctx_emit)
                if len(pend) > 1:
                    pend.pop(0)()
            for f in pend:
                f()
            for j in slots:
                bank, boff = ctx_slot[j % 2]
                em("vector", lambda h, bank=bank, boff=boff: h.reciprocal(
                    A["rcol"][:, 0:1], PS[bank][:, boff + 128:boff + 129]),
                   reads=[prg(bank, boff + 128, boff + 129)], writes=[rg("rcol", 0, 1)],
                   tag=f"rc{hd}_{j}")
                em("vector", lambda h, bank=bank, boff=boff, j=j: h.tensor_scalar(
                    A["ctxq"][:, (j % 2) * 128:(j % 2) * 128 + 128],
                    PS[bank][:, boff:boff + 128], A["rcol"][:, 0:1], None, ALU.mult),
                   reads=[prg(bank, boff, boff + 128), rg("rcol", 0, 1)],
                   writes=[rg("ctxq", (j % 2) * 128, (j % 2) * 128 + 128)],
                   tag=f"nm{hd}_{j}")
                em("tensor", lambda h, j=j: h.transpose(
                    PS["t"][:, 0:128], A["ctxq"][:, (j % 2) * 128:(j % 2) * 128 + 128],
                    A["id128"][:, :]),
                   reads=[rg("ctxq", (j % 2) * 128, (j % 2) * 128 + 128), rg("id128")],
                   writes=[prg("t")], tag=f"tp{hd}_{j}")
                em("scalar", lambda h, hd=hd, j=j: h.copy(
                    A["ctxsb"][:, hd * TQ + j * 128: hd * TQ + (j + 1) * 128],
                    PS["t"][:, 0:128]),
                   reads=[prg("t")],
                   writes=[rg("ctxsb", hd * TQ + j * 128, hd * TQ + (j + 1) * 128)],
                   tag=f"cte{hd}_{j}")

    load_head_weights(0)
    load_head_weights(1)
    proj_qk(0, "q")
    proj_qk(0, "k")
    for hd in range(NH):
        if hd + 2 < NH:
            load_head_weights(hd + 2)
        if hd + 1 < NH:
            proj_qk(hd + 1, "q")
            proj_qk(hd + 1, "k")
        if hd % 2 == 0:
            proj_v(hd)
        attn(hd)
        if hd == 0:
            if "qp" in dbg:
                em("sync", lambda h: h.dma_start(out=dbg["qp"][:, :], in_=A["qp0"][:, :]),
                   reads=[rg("qp0")], dma=True, tag="dbg_qp")
            if "kp" in dbg:
                em("sync", lambda h: h.dma_start(out=dbg["kp"][:, :], in_=A["kp0"][:, :]),
                   reads=[rg("kp0")], dma=True, tag="dbg_kp")
            if "vp" in dbg:
                em("sync", lambda h: h.dma_start(out=dbg["vp"][:, :], in_=A["vp0"][:, :]),
                   reads=[rg("vp0")], dma=True, tag="dbg_vp")
    if "ctx" in dbg:
        for ic in range(NCK):
            em("sync", lambda h, ic=ic: h.dma_start(
                out=dbg["ctx"][ic * 128:(ic + 1) * 128, :],
                in_=A["ctxsb"][:, ic * TQ:(ic + 1) * TQ]),
               reads=[rg("ctxsb", ic * TQ, (ic + 1) * TQ)], dma=True, tag=f"dbg_ctx{ic}")

    if stop_after(2):
        return materialize()

    def load_band(dst, dram, band, tagp):
        em("gpsimd", lambda h, dst=dst, dram=dram, band=band: h.dma_start(
            out=A[dst][:, :], in_=dram[band * 128:(band + 1) * 128, :]),
           writes=[rg(dst)], dma=True, tag=tagp)

    load_band("wch0", wo_img, 0, "ldwo0")
    for hoc in range(NCK):
        pw = hoc % 2
        if hoc + 1 < NCK:
            load_band(f"wch{(hoc + 1) % 2}", wo_img, hoc + 1, f"ldwo{hoc + 1}")
        em("sync", lambda h, hoc=hoc, pw=pw: h.dma_start(
            out=A[f"xqs{pw}"][:, :], in_=xT[hoc * 128:(hoc + 1) * 128, 0:TQ]),
           writes=[rg(f"xqs{pw}")], dma=True, tag=f"ldxq{hoc}")
        for tt in range(2):
            ps = ("a", "b")[tt]
            pf = tt
            for ic in range(NCK):
                em("tensor", lambda h, ic=ic, tt=tt, pw=pw, ps=ps: h.matmul(
                    PS[ps][:, :], A[f"wch{pw}"][:, ic * 128:(ic + 1) * 128],
                    A["ctxsb"][:, ic * TQ + tt * 512: ic * TQ + (tt + 1) * 512],
                    start=(ic == 0), stop=(ic == NCK - 1)),
                   reads=[rg(f"wch{pw}", ic * 128, (ic + 1) * 128),
                          rg("ctxsb", ic * TQ + tt * 512, ic * TQ + (tt + 1) * 512)],
                   writes=[prg(ps)], tag=f"omm{hoc}_{tt}_{ic}")
            em("scalar", lambda h, hoc=hoc, pf=pf, ps=ps: h.activation(
                A[f"ytmp{pf}"][:, :], PS[ps][:, :], AF.Identity,
                bias=A["boc"][:, hoc:hoc + 1], scale=1.0),
               reads=[prg(ps), rg("boc", hoc, hoc + 1)],
               writes=[rg(f"ytmp{pf}")], tag=f"oev{hoc}_{tt}")
            em("vector", lambda h, hoc=hoc, tt=tt, pw=pw, pf=pf: h.tensor_tensor(
                A["y1"][:, hoc * TQ + tt * 512: hoc * TQ + (tt + 1) * 512],
                A[f"ytmp{pf}"][:, :], A[f"xqs{pw}"][:, tt * 512:(tt + 1) * 512], ALU.add),
               reads=[rg(f"ytmp{pf}"), rg(f"xqs{pw}", tt * 512, (tt + 1) * 512)],
               writes=[rg("y1", hoc * TQ + tt * 512, hoc * TQ + (tt + 1) * 512)],
               tag=f"ores{hoc}_{tt}")
    if "y1" in dbg:
        for ic in range(NCK):
            em("sync", lambda h, ic=ic: h.dma_start(
                out=dbg["y1"][ic * 128:(ic + 1) * 128, :],
                in_=A["y1"][:, ic * TQ:(ic + 1) * TQ]),
               reads=[rg("y1", ic * TQ, (ic + 1) * TQ)], dma=True, tag=f"dbg_y1{ic}")

    if stop_after(3):
        return materialize()

    for ic in range(NCK):
        em("vector", lambda h, ic=ic: h.tensor_copy(
            A["ln2"][:, ic * TQ:(ic + 1) * TQ], A["y1"][:, ic * TQ:(ic + 1) * TQ]),
           reads=[rg("y1", ic * TQ, (ic + 1) * TQ)],
           writes=[rg("ln2", ic * TQ, (ic + 1) * TQ)], tag=f"l2c{ic}")
    ln_stats_center("ln2", TQ, TQ, "mbc", "ln2")
    if "ln2" in dbg:
        em("sync", lambda h: h.dma_start(out=dbg["ln2"][:, :], in_=A["ln2"][:, :]),
           reads=[rg("ln2")], dma=True, tag="dbg_ln2")

    if stop_after(4):
        return materialize()

    TK = A["tk"]
    for tt in range(2):
        ps = ("a", "b")[tt]
        for ic in range(NCK):
            em("tensor", lambda h, ic=ic, tt=tt, ps=ps: h.matmul(
                PS[ps][0:8, :], A["Gb"][:, ic * 8:(ic + 1) * 8],
                A["ln2"][:, ic * TQ + tt * 512: ic * TQ + (tt + 1) * 512],
                start=(ic == 0), stop=(ic == NCK - 1)),
               reads=[rg("Gb", ic * 8, (ic + 1) * 8),
                      rg("ln2", ic * TQ + tt * 512, ic * TQ + (tt + 1) * 512)],
               writes=[prg(ps, 0, 512)], tag=f"lg{tt}_{ic}")
        em("scalar", lambda h, tt=tt, ps=ps: h.copy(
            A["logsb"][0:8, tt * 512:(tt + 1) * 512], PS[ps][0:8, :]),
           reads=[prg(ps, 0, 512)], writes=[rg("logsb", tt * 512, (tt + 1) * 512)],
           tag=f"lge{tt}")

    def router_tg(tg):
        em("tensor", lambda h, tg=tg: h.transpose(
            PS["t2"][:, 0:8], A["logsb"][0:8, tg * 128:(tg + 1) * 128], A["id8f"][0:8, :]),
           reads=[rg("logsb", tg * 128, (tg + 1) * 128), rg("id8f")],
           writes=[prg("t2", 0, 8)], tag=f"rtp{tg}")
        em("vector", lambda h: h.tensor_copy(TK[:, 0:8], PS["t2"][:, 0:8]),
           reads=[prg("t2", 0, 8)], writes=[rg("tk", 0, 8)], tag=f"rtc{tg}")
        em("vector", lambda h: h.reduce_max(TK[:, 16:17], TK[:, 0:8], axis=AX.X,
                                            negate=True),
           reads=[rg("tk", 0, 8)], writes=[rg("tk", 16, 17)], tag=f"rmx{tg}")
        em("scalar", lambda h: h.activation(TK[:, 8:16], TK[:, 0:8], AF.Exp,
                                            bias=TK[:, 16:17], scale=1.0),
           reads=[rg("tk", 0, 8), rg("tk", 16, 17)], writes=[rg("tk", 8, 16)],
           tag=f"rex{tg}")
        em("vector", lambda h: h.reduce_sum(TK[:, 17:18], TK[:, 8:16], axis=AX.X),
           reads=[rg("tk", 8, 16)], writes=[rg("tk", 17, 18)], tag=f"rsm{tg}")
        em("vector", lambda h: h.reciprocal(TK[:, 18:19], TK[:, 17:18]),
           reads=[rg("tk", 17, 18)], writes=[rg("tk", 18, 19)], tag=f"rrc{tg}")
        em("vector", lambda h: h.tensor_scalar(TK[:, 24:32], TK[:, 8:16],
                                               TK[:, 18:19], None, ALU.mult),
           reads=[rg("tk", 8, 16), rg("tk", 18, 19)], writes=[rg("tk", 24, 32)],
           tag=f"rp{tg}")
        em("vector", lambda h: h.reduce_max(TK[:, 19:20], TK[:, 24:32], axis=AX.X),
           reads=[rg("tk", 24, 32)], writes=[rg("tk", 19, 20)], tag=f"rpm{tg}")
        em("vector", lambda h: h.tensor_scalar(TK[:, 32:40], TK[:, 24:32],
                                               TK[:, 19:20], None, ALU.is_lt),
           reads=[rg("tk", 24, 32), rg("tk", 19, 20)], writes=[rg("tk", 32, 40)],
           tag=f"rlt{tg}")
        em("vector", lambda h: h.tensor_tensor(TK[:, 32:40], TK[:, 32:40],
                                               TK[:, 24:32], ALU.mult),
           reads=[rg("tk", 32, 40), rg("tk", 24, 32)], writes=[rg("tk", 32, 40)],
           tag=f"rl2{tg}")
        em("vector", lambda h: h.reduce_max(TK[:, 20:21], TK[:, 32:40], axis=AX.X),
           reads=[rg("tk", 32, 40)], writes=[rg("tk", 20, 21)], tag=f"rsc{tg}")
        em("vector", lambda h: h.tensor_scalar(TK[:, 40:48], TK[:, 24:32],
                                               TK[:, 20:21], None, ALU.is_ge),
           reads=[rg("tk", 24, 32), rg("tk", 20, 21)], writes=[rg("tk", 40, 48)],
           tag=f"rge{tg}")
        em("vector", lambda h: h.tensor_tensor(TK[:, 48:56], TK[:, 40:48],
                                               TK[:, 24:32], ALU.mult),
           reads=[rg("tk", 40, 48), rg("tk", 24, 32)], writes=[rg("tk", 48, 56)],
           tag=f"rw{tg}")
        em("vector", lambda h: h.tensor_copy(A["wtk"][:, 0:8], TK[:, 48:56]),
           reads=[rg("tk", 48, 56)], writes=[rg("wtk")], tag=f"rwb{tg}")
        em("tensor", lambda h: h.transpose(PS["t"][0:8, 0:128], A["wtk"][:, 0:8],
                                           A["id128"][:, :]),
           reads=[rg("wtk"), rg("id128")], writes=[prg("t")], tag=f"rtb{tg}")
        em("scalar", lambda h, tg=tg: h.copy(
            A["w8sb"][0:8, tg * 128:(tg + 1) * 128], PS["t"][0:8, 0:128]),
           reads=[prg("t")], writes=[rg("w8sb", tg * 128, (tg + 1) * 128)],
           tag=f"rwe{tg}")

    def load_fband(dst, dram, band, tagp):
        em("gpsimd", lambda h, dst=dst, dram=dram, band=band: h.dma_start(
            out=A[dst][:, :], in_=dram[band * 128:(band + 1) * 128, :]),
           writes=[rg(dst)], dma=True, tag=tagp)

    for tt in range(2):
        load_fband("wgh0", g_img, 0, f"ldg{tt}_0")
        load_fband("wuh0", u_img, 0, f"ldu{tt}_0")
        for efc in range(NCK):
            pw = efc % 2
            if efc + 1 < NCK:
                load_fband(f"wgh{(efc + 1) % 2}", g_img, efc + 1, f"ldg{tt}_{efc + 1}")
                load_fband(f"wuh{(efc + 1) % 2}", u_img, efc + 1, f"ldu{tt}_{efc + 1}")
            psg = ("a", "b")[efc % 2]
            psu = ("c0", "c1")[efc % 2]
            for ic in range(NCK):
                em("tensor", lambda h, ic=ic, tt=tt, pw=pw, psg=psg: h.matmul(
                    PS[psg][:, :], A[f"wgh{pw}"][:, ic * 128:(ic + 1) * 128],
                    A["ln2"][:, ic * TQ + tt * 512: ic * TQ + (tt + 1) * 512],
                    start=(ic == 0), stop=(ic == NCK - 1)),
                   reads=[rg(f"wgh{pw}", ic * 128, (ic + 1) * 128),
                          rg("ln2", ic * TQ + tt * 512, ic * TQ + (tt + 1) * 512)],
                   writes=[prg(psg)], tag=f"gmm{tt}_{efc}_{ic}")
                em("tensor", lambda h, ic=ic, tt=tt, pw=pw, psu=psu: h.matmul(
                    PS[psu][:, :], A[f"wuh{pw}"][:, ic * 128:(ic + 1) * 128],
                    A["ln2"][:, ic * TQ + tt * 512: ic * TQ + (tt + 1) * 512],
                    start=(ic == 0), stop=(ic == NCK - 1)),
                   reads=[rg(f"wuh{pw}", ic * 128, (ic + 1) * 128),
                          rg("ln2", ic * TQ + tt * 512, ic * TQ + (tt + 1) * 512)],
                   writes=[prg(psu)], tag=f"umm{tt}_{efc}_{ic}")
            sc0 = (efc % 2) * 512
            em("scalar", lambda h, sc0=sc0, psg=psg: h.activation(
                A["sil"][:, sc0:sc0 + 512], PS[psg][:, :], AF.Silu, bias=0.0, scale=1.0),
               reads=[prg(psg)], writes=[rg("sil", sc0, sc0 + 512)], tag=f"sil{tt}_{efc}")
            em("vector", lambda h, efc=efc, sc0=sc0, psu=psu: h.tensor_tensor(
                A["gated"][:, efc * 512:(efc + 1) * 512], PS[psu][:, :],
                A["sil"][:, sc0:sc0 + 512], ALU.mult),
               reads=[prg(psu), rg("sil", sc0, sc0 + 512)],
               writes=[rg("gated", efc * 512, (efc + 1) * 512)], tag=f"gu{tt}_{efc}")
            if tt == 0 and efc % 2 == 0:
                router_tg(efc // 2)
        if tt == 0:
            for g in range(2):
                ps = ("c0", "c1")[g]
                em("tensor", lambda h, g=g, ps=ps: h.matmul(
                    PS[ps][:, :], A["eer"][0:8, :], A["w8sb"][0:8, g * 512:(g + 1) * 512],
                    start=True, stop=True),
                   reads=[rg("eer"), rg("w8sb", g * 512, (g + 1) * 512)], writes=[prg(ps)],
                   tag=f"wer{g}")
                em("scalar", lambda h, g=g, ps=ps: h.copy(
                    A["wer"][:, g * 512:(g + 1) * 512], PS[ps][:, :]),
                   reads=[prg(ps)], writes=[rg("wer", g * 512, (g + 1) * 512)], tag=f"were{g}")
            for ltt in range(2):
                ps = ("a", "b")[ltt]
                for ic in range(NCK):
                    em("tensor", lambda h, ic=ic, ltt=ltt, ps=ps: h.matmul(
                        PS[ps][:, :], A["lATb"][:, ic * 128:(ic + 1) * 128],
                        A["ln2"][:, ic * TQ + ltt * 512: ic * TQ + (ltt + 1) * 512],
                        start=(ic == 0), stop=(ic == NCK - 1)),
                       reads=[rg("lATb", ic * 128, (ic + 1) * 128),
                              rg("ln2", ic * TQ + ltt * 512, ic * TQ + (ltt + 1) * 512)],
                       writes=[prg(ps)], tag=f"lrmm{ltt}_{ic}")
                em("vector", lambda h, ltt=ltt, ps=ps: h.tensor_tensor(
                    A["lrw"][:, ltt * 512:(ltt + 1) * 512], PS[ps][:, :],
                    A["wer"][:, ltt * 512:(ltt + 1) * 512], ALU.mult),
                   reads=[prg(ps), rg("wer", ltt * 512, (ltt + 1) * 512)],
                   writes=[rg("lrw", ltt * 512, (ltt + 1) * 512)], tag=f"lrw{ltt}")
            if "w8" in dbg:
                em("sync", lambda h: h.dma_start(out=dbg["w8"][:, :], in_=A["w8sb"][0:8, :]),
                   reads=[rg("w8sb")], dma=True, tag="dbg_w8")
            if "lrw" in dbg:
                em("sync", lambda h: h.dma_start(out=dbg["lrw"][:, :], in_=A["lrw"][:, :]),
                   reads=[rg("lrw")], dma=True, tag="dbg_lrw")
        for e_ in range(8):
            pb = e_ % 2
            em("tensor", lambda h, e_=e_, tt=tt: h.matmul(
                PS["t2"][:, :], A["e8s"][0:8, e_ * 128:(e_ + 1) * 128],
                A["w8sb"][0:8, tt * 512:(tt + 1) * 512], start=True, stop=True),
               reads=[rg("e8s", e_ * 128, (e_ + 1) * 128),
                      rg("w8sb", tt * 512, (tt + 1) * 512)],
               writes=[prg("t2")], tag=f"wbc{tt}_{e_}")
            em("scalar", lambda h, pb=pb: h.copy(A[f"wbce{pb}"][:, :], PS["t2"][:, 0:512]),
               reads=[prg("t2")], writes=[rg(f"wbce{pb}")], tag=f"wbe{tt}_{e_}")
            for sub in range(2):
                efc = e_ * 2 + sub
                em("vector", lambda h, efc=efc, pb=pb: h.tensor_tensor(
                    A["gated"][:, efc * 512:(efc + 1) * 512],
                    A["gated"][:, efc * 512:(efc + 1) * 512],
                    A[f"wbce{pb}"][:, :], ALU.mult),
                   reads=[rg("gated", efc * 512, (efc + 1) * 512), rg(f"wbce{pb}")],
                   writes=[rg("gated", efc * 512, (efc + 1) * 512)], tag=f"gw{tt}_{efc}")
        if "gated" in dbg and tt == 0:
            em("sync", lambda h: h.dma_start(out=dbg["gated"][:, :], in_=A["gated"][:, :]),
               reads=[rg("gated")], dma=True, tag="dbg_gated")
        load_fband("wch0", d_img, 0, f"ldd{tt}_0")
        for hoc in range(NCK):
            pw = hoc % 2
            if hoc + 1 < NCK:
                load_fband(f"wch{(hoc + 1) % 2}", d_img, hoc + 1, f"ldd{tt}_{hoc + 1}")
            ps = ("a", "b", "s0", "s1")[hoc % 4]
            pf = hoc % 2
            for efc in range(NCK):
                em("tensor", lambda h, efc=efc, pw=pw, ps=ps: h.matmul(
                    PS[ps][:, :], A[f"wch{pw}"][:, efc * 128:(efc + 1) * 128],
                    A["gated"][:, efc * 512:(efc + 1) * 512],
                    start=(efc == 0), stop=False),
                   reads=[rg(f"wch{pw}", efc * 128, (efc + 1) * 128),
                          rg("gated", efc * 512, (efc + 1) * 512)],
                   writes=[prg(ps)], tag=f"dmm{tt}_{hoc}_{efc}")
            em("tensor", lambda h, hoc=hoc, tt=tt, ps=ps: h.matmul(
                PS[ps][:, :], A["lBTb"][:, hoc * 128:(hoc + 1) * 128],
                A["lrw"][:, tt * 512:(tt + 1) * 512], start=False, stop=True),
               reads=[rg("lBTb", hoc * 128, (hoc + 1) * 128),
                      rg("lrw", tt * 512, (tt + 1) * 512)],
               writes=[prg(ps)], tag=f"lbmm{tt}_{hoc}")
            em("vector", lambda h, hoc=hoc, tt=tt, pf=pf, ps=ps: h.tensor_tensor(
                A[f"fout{pf}"][:, :], PS[ps][:, :],
                A["y1"][:, hoc * TQ + tt * 512: hoc * TQ + (tt + 1) * 512], ALU.add),
               reads=[prg(ps), rg("y1", hoc * TQ + tt * 512, hoc * TQ + (tt + 1) * 512)],
               writes=[rg(f"fout{pf}")], tag=f"fres{tt}_{hoc}")
            em("sync", lambda h, hoc=hoc, tt=tt, pf=pf: h.dma_start(
                out=y_out[hoc * 128:(hoc + 1) * 128, tt * 512:(tt + 1) * 512],
                in_=A[f"fout{pf}"][:, :]),
               reads=[rg(f"fout{pf}")],
               writes=[("yout", (hoc * 2 + tt) * 512, (hoc * 2 + tt + 1) * 512)],
               dma=True, tag=f"st{tt}_{hoc}")

    em("gpsimd", lambda h: h.memset(A["joinpad"][:, 0:1], 0.0),
       reads=[("yout", 0, NCK * 2 * 512)], writes=[rg("joinpad")], tag="join")

    return materialize()


def _rope_tables(pos):
    j = np.arange(0, HD, 2, dtype=np.float32) / HD
    inv = (1.0 / 10000.0 ** j).astype(np.float32)
    fr = pos[:, None].astype(np.float32) * inv[None, :]
    cos = np.cos(fr).astype(np.float32)
    sin = np.sin(fr).astype(np.float32)
    cosT = np.ascontiguousarray(np.concatenate([cos, cos], axis=1).T)
    sinS = np.ascontiguousarray(np.concatenate([sin, -sin], axis=1).T)
    return cosT, sinS


def _band_img(T):
    nb = T.shape[1] // 128
    return np.ascontiguousarray(
        T.reshape(16, 128, nb, 128).transpose(2, 1, 0, 3).reshape(nb * 128, 2048))


def _vgroup_img(T):
    return np.ascontiguousarray(
        T.reshape(16, 128, 8, 256).transpose(2, 1, 0, 3).reshape(1024, 4096))


_NC_CACHE = [None]


def kernel(L0_attn_hidden_states, L0_attn_Wq, L0_attn_bq, L0_attn_Wk, L0_attn_bk,
           L0_attn_Wv, L0_attn_bv, L0_attn_Wo, L0_attn_bo, L0_ffn_G, L0_ffn_gate_w,
           L0_ffn_up_w, L0_ffn_down_w, L0_ffn_lora_A, L0_ffn_lora_B):
    x = np.ascontiguousarray(np.asarray(L0_attn_hidden_states, np.float32))
    wq_img = _band_img(np.asarray(L0_attn_Wq, np.float32).T)
    wk_img = _band_img(np.asarray(L0_attn_Wk, np.float32).T)
    wv_img = _vgroup_img(np.asarray(L0_attn_Wv, np.float32).T)
    wo_img = _band_img(np.asarray(L0_attn_Wo, np.float32).T)
    g_img = _band_img(np.asarray(L0_ffn_gate_w, np.float32).transpose(2, 0, 1).reshape(H, EF))
    u_img = _band_img(np.asarray(L0_ffn_up_w, np.float32).transpose(2, 0, 1).reshape(H, EF))
    d_img = _band_img(np.asarray(L0_ffn_down_w, np.float32).transpose(0, 2, 1).reshape(EF, H))
    bq = np.ascontiguousarray(np.asarray(L0_attn_bq, np.float32).reshape(16, 128).T)
    bk = np.ascontiguousarray(np.asarray(L0_attn_bk, np.float32).reshape(16, 128).T)
    bo = np.ascontiguousarray(np.asarray(L0_attn_bo, np.float32).reshape(16, 128).T)
    bv = np.ascontiguousarray(np.asarray(L0_attn_bv, np.float32).reshape(1, H))
    G = np.asarray(L0_ffn_G, np.float32)
    G_img = np.ascontiguousarray(G.reshape(16, 128, 8).transpose(1, 0, 2).reshape(128, 128))
    lAT = np.asarray(L0_ffn_lora_A, np.float32).transpose(2, 0, 1).reshape(H, 128)
    lAT_img = np.ascontiguousarray(
        lAT.reshape(16, 128, 128).transpose(1, 0, 2).reshape(128, 2048))
    lBT = np.ascontiguousarray(
        np.asarray(L0_ffn_lora_B, np.float32).transpose(0, 2, 1).reshape(128, H))

    if _NC_CACHE[0] is None or _NC_CACHE[0][0] != (PHASE_LIMIT, tuple(DEBUG_DUMP)):
        _NC_CACHE[0] = ((PHASE_LIMIT, tuple(DEBUG_DUMP)), build_nc())
    nc = _NC_CACHE[0][1]

    in_maps = []
    perms = []
    for c in range(8):
        b, h2 = c // 2, c % 2
        chunks = [2 * j + h2 for j in range(8)] + [2 * j + (1 - h2) for j in range(8)]
        cols = np.concatenate([np.arange(t * 128, (t + 1) * 128) for t in chunks])
        perms.append(cols)
        xTc = np.ascontiguousarray(x[b].T[:, cols])
        cosk, sink = _rope_tables(cols.astype(np.float32))
        mcol = np.full((128, 1), MASKNEG if h2 == 0 else 0.0, np.float32)
        in_maps.append({
            "xT": xTc, "wq_img": wq_img, "wk_img": wk_img, "wv_img": wv_img,
            "wo_img": wo_img, "g_img": g_img, "u_img": u_img, "d_img": d_img,
            "lAT": lAT_img, "lBT": lBT, "G_img": G_img,
            "bq_img": bq, "bk_img": bk, "bo_img": bo, "bv_row": bv,
            "cosk": cosk, "sink": sink, "maskcol": mcol,
        })
    res = run_bass_kernel_spmd(nc, in_maps, core_ids=list(range(8)))
    y = np.zeros((B, S, H), np.float32)
    for c in range(8):
        b = c // 2
        yq = res.results[c]["y_out"]
        y[b, perms[c][:TQ], :] = yq.T
    if DEBUG_DUMP:
        kernel._dbg = [{k: v for k, v in r.items() if k.startswith("dbg_")}
                       for r in res.results]
    return y


# version 12
# speedup vs baseline: 1.4246x; 1.0415x over previous
import numpy as np
import ml_dtypes
import concourse.bass as bass
import concourse.mybir as mybir
from concourse.bass_utils import run_bass_kernel_spmd

F32 = mybir.dt.float32
BF16 = mybir.dt.bfloat16
AF = mybir.ActivationFunctionType
ALU = mybir.AluOpType
AX = mybir.AxisListType
bfd = ml_dtypes.bfloat16

H = 2048
S = 2048
B = 4
NH = 16
HD = 128
NCK = 16
E = 8
EF = 2048
EPS = 1e-5
TQ = 1024
RSQ = float(1.0 / np.sqrt(np.float32(HD)))
MASKNEG = -1.0e9

ENGS = ("sync", "scalar", "vector", "tensor", "gpsimd")

DEBUG_DUMP = []
PHASE_LIMIT = 99
OP_LIMIT = None


LANES = {"gpsimd": 12, "sync": 6}


class Op:
    __slots__ = ("eng", "fn", "dma", "eidx", "deps", "inc", "waits", "tag", "chan",
                 "cidx")

    def __init__(self, eng, fn, dma, eidx, tag):
        self.eng = eng
        self.fn = fn
        self.dma = dma
        self.eidx = eidx
        self.deps = set()
        self.inc = False
        self.waits = []
        self.tag = tag
        self.chan = eng
        self.cidx = 0


class Prog:
    def __init__(self):
        self.eops = {e: [] for e in ENGS}
        self.segs = {}
        self.nemit = 0

    def emit(self, eng, fn, reads=(), writes=(), dma=False, tag=""):
        self.nemit += 1
        if OP_LIMIT is not None and self.nemit > OP_LIMIT and tag not in ("stub_out", "join"):
            return None
        op = Op(eng, fn, dma, len(self.eops[eng]), tag)
        self.eops[eng].append(op)
        for (sp, lo, hi) in reads:
            segs = self.segs.setdefault(sp, [])
            for seg in segs:
                if seg[0] < hi and lo < seg[1]:
                    if seg[2] is not None:
                        op.deps.add(seg[2])
                    seg[3].add(op)
        for (sp, lo, hi) in writes:
            segs = self.segs.setdefault(sp, [])
            new = []
            for seg in segs:
                if seg[0] < hi and lo < seg[1]:
                    if seg[2] is not None:
                        op.deps.add(seg[2])
                    op.deps.update(seg[3])
                    if seg[0] < lo:
                        new.append([seg[0], lo, seg[2], set(seg[3])])
                    if hi < seg[1]:
                        new.append([hi, seg[1], seg[2], set(seg[3])])
                else:
                    new.append(seg)
            new.append([lo, hi, op, set()])
            self.segs[sp] = new
        op.deps.discard(op)
        return op

    def finalize(self):
        lane_prev = {}
        for e in ENGS:
            nl = LANES.get(e, 0)
            ctr = 0
            for op in self.eops[e]:
                if op.dma and nl:
                    lane = ctr % nl
                    ctr += 1
                    op.chan = f"{e}:L{lane}"
                    prev = lane_prev.get(op.chan)
                    if prev is not None:
                        op.deps.add(prev)
                    lane_prev[op.chan] = op
        chans = list(ENGS) + sorted({op.chan for e in ENGS for op in self.eops[e]
                                     if op.chan not in ENGS})
        cctr = {c: 0 for c in chans}
        for e in ENGS:
            for op in self.eops[e]:
                op.cidx = cctr[op.chan]
                cctr[op.chan] += 1
        for e in ENGS:
            for op in self.eops[e]:
                best = {}
                for d in op.deps:
                    if d.eng == "tensor" and op.eng == "tensor":
                        continue
                    if d is op:
                        continue
                    cur = best.get(d.chan)
                    if cur is None or d.cidx > cur.cidx:
                        best[d.chan] = d
                op.deps = best
        wm = {e: {c: -1 for c in chans} for e in ENGS}
        for e in ENGS:
            w = wm[e]
            for op in self.eops[e]:
                kept = []
                for d in sorted(op.deps.values(), key=lambda d: d.chan):
                    if d.cidx <= w[d.chan]:
                        continue
                    if d.chan == op.chan and d.cidx >= op.cidx:
                        raise RuntimeError(f"self-dep ahead: {op.tag} <- {d.tag}")
                    w[d.chan] = d.cidx
                    kept.append(d)
                    d.inc = True
                op.deps = kept
        for e in ENGS:
            for op in self.eops[e]:
                if op.dma:
                    op.inc = True
        ccnt = {c: 0 for c in chans}
        cumv = {}
        for e in ENGS:
            for op in self.eops[e]:
                if op.inc:
                    ccnt[op.chan] += 16 if op.dma else 1
                cumv[(op.chan, op.cidx)] = ccnt[op.chan]
        for e in ENGS:
            for op in self.eops[e]:
                op.waits = [(d.chan, cumv[(d.chan, d.cidx)]) for d in op.deps]
        self.chans = chans
        pos = {e: 0 for e in ENGS}
        cnt = {c: 0 for c in chans}
        progress = True
        while progress:
            progress = False
            for e in ENGS:
                while pos[e] < len(self.eops[e]):
                    op = self.eops[e][pos[e]]
                    if all(cnt[se] >= v for (se, v) in op.waits):
                        if op.inc:
                            cnt[op.chan] += 16 if op.dma else 1
                        pos[e] += 1
                        progress = True
                    else:
                        break
        stuck = [e for e in ENGS if pos[e] < len(self.eops[e])]
        if stuck:
            msgs = []
            for e in stuck:
                op = self.eops[e][pos[e]]
                msgs.append(f"{e}[{pos[e]}] {op.tag} waits {op.waits}")
            raise RuntimeError("DEADLOCK:\n" + "\n".join(msgs))
        return cumv


def build_nc(phase_limit=None):
    if phase_limit is None:
        phase_limit = PHASE_LIMIT
    nc = bass.Bass("TRN2", target_bir_lowering=False, debug=False)

    xT = nc.dram_tensor("xT", [H, S], F32, kind="ExternalInput").ap()
    wq_img = nc.dram_tensor("wq_img", [H, H], F32, kind="ExternalInput").ap()
    wk_img = nc.dram_tensor("wk_img", [H, H], F32, kind="ExternalInput").ap()
    wv_img = nc.dram_tensor("wv_img", [1024, 4096], F32, kind="ExternalInput").ap()
    wo_img = nc.dram_tensor("wo_img", [H, H], F32, kind="ExternalInput").ap()
    g_img = nc.dram_tensor("g_img", [H, H], F32, kind="ExternalInput").ap()
    u_img = nc.dram_tensor("u_img", [H, H], F32, kind="ExternalInput").ap()
    d_img = nc.dram_tensor("d_img", [H, H], F32, kind="ExternalInput").ap()
    lATd = nc.dram_tensor("lAT", [128, 2048], F32, kind="ExternalInput").ap()
    lBTd = nc.dram_tensor("lBT", [128, H], F32, kind="ExternalInput").ap()
    Gd = nc.dram_tensor("G_img", [128, 128], F32, kind="ExternalInput").ap()
    bqd = nc.dram_tensor("bq_img", [128, 16], F32, kind="ExternalInput").ap()
    bkd = nc.dram_tensor("bk_img", [128, 16], F32, kind="ExternalInput").ap()
    bod = nc.dram_tensor("bo_img", [128, 16], F32, kind="ExternalInput").ap()
    bvd = nc.dram_tensor("bv_row", [1, H], F32, kind="ExternalInput").ap()
    coskd = nc.dram_tensor("cosk", [128, S], F32, kind="ExternalInput").ap()
    sinkd = nc.dram_tensor("sink", [128, S], F32, kind="ExternalInput").ap()
    mcold = nc.dram_tensor("maskcol", [128, 1], F32, kind="ExternalInput").ap()
    y_out = nc.dram_tensor("y_out", [H, TQ], F32, kind="ExternalOutput").ap()

    def bfc(a):
        return np.ascontiguousarray(np.asarray(a, np.float32).astype(bfd))

    tri_np = (np.arange(128)[:, None] <= np.arange(128)[None, :]).astype(np.float32)
    tri_d = nc.inline_tensor(bfc(tri_np), "c_tri")
    id128_d = nc.inline_tensor(bfc(np.eye(128)), "c_id128")
    id8f_d = nc.inline_tensor(np.ascontiguousarray(np.eye(8, dtype=np.float32)), "c_id8f")
    eer_np = np.zeros((8, 128), np.float32)
    for e_ in range(8):
        eer_np[e_, e_ * 16:(e_ + 1) * 16] = 1.0
    eer_d = nc.inline_tensor(bfc(eer_np), "c_eer")
    e8s_np = np.zeros((8, 8 * 128), np.float32)
    for e_ in range(8):
        e8s_np[e_, e_ * 128:(e_ + 1) * 128] = 1.0
    e8s_d = nc.inline_tensor(bfc(e8s_np), "c_e8s")

    dbg = {}
    for name, shape, dt in [
        ("xb", [128, NCK * S], BF16), ("qp", [128, TQ], BF16),
        ("kp", [128, S], BF16), ("vp", [128, 16 * 129], BF16),
        ("ctx", [H, TQ], BF16), ("y1", [H, TQ], F32),
        ("ln2", [128, NCK * TQ], BF16), ("w8", [8, TQ], BF16),
        ("gated", [128, NCK * 512], BF16), ("lrw", [128, TQ], BF16),
    ]:
        if name in DEBUG_DUMP:
            dbg[name] = nc.dram_tensor("dbg_" + name, shape, dt, kind="ExternalOutput").ap()

    SBUF_BASE = 16512
    SBUF_LIMIT = 212864
    A = {}
    meta = {}

    def place(name, ncols, dt, off):
        esz = 2 if dt == BF16 else 4
        off = (off + 31) // 32 * 32
        nb = ncols * esz
        assert off + nb <= SBUF_LIMIT, (name, off, nb)
        A[name] = nc.alloc_sbuf_tensor_at(name, [128, ncols], dt, offset=SBUF_BASE + off)
        meta[name] = (off, esz, ncols)
        return off + nb

    o = 0
    for nm, ncols, dt in [
        ("cosk", S, BF16), ("sink", S, BF16), ("bvbc", H, BF16),
        ("lATb", 2048, BF16), ("lBTb", H, BF16), ("Gb", 128, BF16),
        ("tri", 128, BF16), ("id128", 128, BF16), ("id8f", 8, F32),
        ("eer", 128, BF16), ("e8s", 8 * 128, BF16), ("bqc", 16, F32), ("bkc", 16, F32), ("boc", 16, F32),
        ("mcol", 1, F32), ("onecb", 1, BF16), ("onerf", 128, F32),
        ("onerb", 128, BF16), ("mbc", S, BF16), ("ibc", S, BF16),
        ("joinpad", 16, F32),
    ]:
        o = place(nm, ncols, dt, o)
    REG_P_END = o
    REG_X = (REG_P_END + 31) // 32 * 32
    place("xb", NCK * S, BF16, REG_X)
    place("y1", NCK * TQ, F32, REG_X)
    REG_X_END = REG_X + NCK * TQ * 4
    REG_C = (REG_X_END + 31) // 32 * 32
    place("ctxsb", NCK * TQ, BF16, REG_C)
    place("ln2", NCK * TQ, BF16, REG_C)
    REG_C_END = REG_C + NCK * TQ * 2
    o = (REG_C_END + 31) // 32 * 32
    REG_W = o
    for nm, ncols, dt in [
        ("wqh0", 2048, BF16), ("wqh1", 2048, BF16),
        ("wkh0", 2048, BF16), ("wkh1", 2048, BF16),
        ("wvg0", 4096, BF16), ("wvg1", 4096, BF16),
        ("vp0", 16 * 129, BF16), ("vp1", 16 * 129, BF16),
        ("qp0", TQ, BF16), ("qp1", TQ, BF16),
        ("kp0", S, BF16), ("kp1", S, BF16),
        ("pt0", TQ, BF16), ("pt1", TQ, BF16), ("pt2", TQ, BF16),
        ("qktmp", S, BF16), ("r3", S, BF16),
        ("ctxq", 256, BF16), ("rcol", 8, F32),
        ("rows", 1024, F32), ("rows2", 1024, F32), ("rsc", 512, F32),
        ("sq0", 512, BF16), ("sq1", 512, BF16), ("sq2", 512, BF16),
    ]:
        o = place(nm, ncols, dt, o)
    W12_END = o
    o = REG_W
    for nm, ncols, dt in [
        ("gated", NCK * 512, BF16),
        ("wch0", 2048, BF16), ("wch1", 2048, BF16),
        ("wgh0", 2048, BF16), ("wgh1", 2048, BF16),
        ("wuh0", 2048, BF16), ("wuh1", 2048, BF16),
        ("xqs0", TQ, F32), ("xqs1", TQ, F32),
        ("fout0", 512, F32), ("fout1", 512, F32),
        ("ytmp0", 512, F32), ("ytmp1", 512, F32),
        ("wbce0", 512, BF16), ("wbce1", 512, BF16),
        ("wer", TQ, BF16), ("lrw", TQ, BF16),
        ("logsb", 1024, F32), ("tk", 64, F32), ("wtk", 8, BF16),
        ("w8sb", TQ, BF16), ("sil", TQ, BF16),
    ]:
        o = place(nm, ncols, dt, o)
    W36_END = o
    assert max(W12_END, W36_END) <= SBUF_LIMIT, (W12_END, W36_END)

    PS = {}
    psmeta = {}
    psoff = [0]

    def psalloc(name, ncols, dt):
        t = nc.alloc_psum_tensor("ps_" + name, [128, ncols], dt)
        PS[name] = t
        esz = 2 if dt == BF16 else 4
        psmeta[name] = (psoff[0], esz)
        psoff[0] += 2048 * ((ncols * esz + 2047) // 2048)

    psalloc("a", 512, F32)
    psalloc("b", 512, F32)
    psalloc("s0", 512, F32)
    psalloc("s1", 512, F32)
    psalloc("c0", 512, F32)
    psalloc("c1", 512, F32)
    psalloc("t", 128, BF16)
    psalloc("t2", 512, F32)
    assert psoff[0] <= 8 * 2048, psoff[0]

    def rg(name, c0=None, c1=None):
        off, esz, ncols = meta[name]
        if c0 is None:
            c0, c1 = 0, ncols
        return ("sbuf", off + c0 * esz, off + c1 * esz)

    def prg(name, c0=None, c1=None):
        off, esz = psmeta[name]
        if c0 is None:
            return ("psum", off, off + 2048)
        return ("psum", off + c0 * esz, off + c1 * esz)

    P = Prog()
    em = P.emit

    def materialize():
        P.finalize()
        import contextlib
        with contextlib.ExitStack() as stack:
            sems = {}
            for c in P.chans:
                sems[c] = stack.enter_context(nc.semaphore("sem_" + c.replace(":", "_")))
            block = stack.enter_context(nc.Block())

            def runner(eng):
                def run(h):
                    for op in P.eops[eng]:
                        for (se, v) in op.waits:
                            h.wait_ge(sems[se], v)
                        inst = op.fn(h)
                        if op.inc:
                            inst.then_inc(sems[op.chan], 16 if op.dma else 1)
                return run

            block.sync(runner("sync"))
            block.scalar(runner("scalar"))
            block.vector(runner("vector"))
            block.tensor(runner("tensor"))
            block.gpsimd(runner("gpsimd"))
        return nc

    def stop_after(k):
        if phase_limit > k:
            return False
        em("gpsimd", lambda h: h.dma_start(out=y_out[0:128, :], in_=A["mbc"][:, 0:TQ]),
           reads=[rg("mbc", 0, TQ)], writes=[("yout", 0, 512)], dma=True, tag="stub_out")
        em("gpsimd", lambda h: h.memset(A["joinpad"][:, 0:1], 0.0),
           reads=[("yout", 0, 512)], writes=[rg("joinpad")], tag="join")
        return True

    for nm, src, np_ in [("tri", tri_d.ap(), 128), ("id128", id128_d.ap(), 128),
                         ("eer", eer_d.ap(), 8), ("e8s", e8s_d.ap(), 8)]:
        em("sync", lambda h, nm=nm, src=src, np_=np_: h.dma_start(out=A[nm][0:np_, :], in_=src),
           writes=[rg(nm)], dma=True, tag="ld_" + nm)
    em("sync", lambda h: h.dma_start(out=A["id8f"][0:8, :], in_=id8f_d.ap()),
       writes=[rg("id8f")], dma=True, tag="ld_id8f")
    for nm, src in [("bqc", bqd), ("bkc", bkd), ("boc", bod), ("mcol", mcold)]:
        em("sync", lambda h, nm=nm, src=src: h.dma_start(out=A[nm][:], in_=src[:, :]),
           writes=[rg(nm)], dma=True, tag="ld_" + nm)
    em("sync", lambda h: h.dma_start(out=A["rows"][0:1, :], in_=bvd[0:1, 0:1024]),
       writes=[rg("rows")], dma=True, tag="ld_bv0")
    em("sync", lambda h: h.dma_start(out=A["rows2"][0:1, :], in_=bvd[0:1, 1024:2048]),
       writes=[rg("rows2")], dma=True, tag="ld_bv1")
    for nm, src in [("cosk", coskd), ("sink", sinkd), ("Gb", Gd)]:
        em("gpsimd", lambda h, nm=nm, src=src: h.dma_start(out=A[nm][:], in_=src[:, :]),
           writes=[rg(nm)], dma=True, tag="ld_" + nm)
    em("gpsimd", lambda h: h.dma_start(out=A["lATb"][:], in_=lATd[:, :]),
       writes=[rg("lATb")], dma=True, tag="ld_lAT")
    em("gpsimd", lambda h: h.dma_start(out=A["lBTb"][:], in_=lBTd[:, :]),
       writes=[rg("lBTb")], dma=True, tag="ld_lBT")
    em("vector", lambda h: h.memset(A["onecb"][:], 1.0), writes=[rg("onecb")], tag="ms1")
    em("vector", lambda h: h.memset(A["onerf"][0:1, :], 1.0), writes=[rg("onerf")], tag="ms2")
    em("vector", lambda h: h.memset(A["onerb"][0:1, :], 1.0), writes=[rg("onerb")], tag="ms3")
    for p in range(2):
        em("vector", lambda h, p=p: h.memset(A[f"vp{p}"][:, 128:16 * 129:129], 1.0),
           writes=[rg(f"vp{p}")], tag=f"ms_vp{p}")
    for g in range(4):
        src = "rows" if g < 2 else "rows2"
        sc0 = (g % 2) * 512
        ps = ("a", "b")[g % 2]
        em("tensor", lambda h, src=src, sc0=sc0, ps=ps: h.matmul(
            PS[ps][:, :], A["onerf"][0:1, :], A[src][0:1, sc0:sc0 + 512],
            start=True, stop=True),
           reads=[rg("onerf"), rg(src, sc0, sc0 + 512)], writes=[prg(ps)], tag=f"bvb{g}")
        em("scalar", lambda h, g=g, ps=ps: h.copy(
            A["bvbc"][:, g * 512:(g + 1) * 512], PS[ps][:, :]),
           reads=[prg(ps)], writes=[rg("bvbc", g * 512, (g + 1) * 512)], tag=f"bve{g}")

    if stop_after(0):
        return materialize()

    for ic in range(NCK):
        em("gpsimd", lambda h, ic=ic: h.dma_start(
            out=A["xb"][:, ic * S:(ic + 1) * S], in_=xT[ic * 128:(ic + 1) * 128, :]),
           writes=[rg("xb", ic * S, (ic + 1) * S)], dma=True, tag=f"ld_x{ic}")

    def ln_stats_center(buf, bufstride, ntok, mdst, tag, upto=3):
        nhalf = ntok // 1024
        for hf in range(max(nhalf, 1)):
            if upto < 1:
                break
            t0 = hf * 1024
            tn = min(1024, ntok - t0)
            for ic in range(NCK):
                for g in range(2):
                    if g * 512 >= tn:
                        continue
                    sqb = f"sq{(ic * 2 + g) % 3}"
                    ps = ("a", "b")[g]
                    ps2 = ("c0", "c1")[g]
                    c0 = t0 + g * 512
                    em("vector", lambda h, ic=ic, c0=c0, sqb=sqb: h.tensor_tensor(
                        A[sqb][:, 0:512],
                        A[buf][:, ic * bufstride + c0: ic * bufstride + c0 + 512],
                        A[buf][:, ic * bufstride + c0: ic * bufstride + c0 + 512], ALU.mult),
                       reads=[rg(buf, ic * bufstride + c0, ic * bufstride + c0 + 512)],
                       writes=[rg(sqb, 0, 512)], tag=f"{tag}sq{hf}{g}_{ic}")
                    em("tensor", lambda h, ic=ic, c0=c0, ps=ps: h.matmul(
                        PS[ps][0:1, :], A["onecb"][:, :],
                        A[buf][:, ic * bufstride + c0: ic * bufstride + c0 + 512],
                        start=(ic == 0), stop=(ic == NCK - 1)),
                       reads=[rg("onecb"), rg(buf, ic * bufstride + c0, ic * bufstride + c0 + 512)],
                       writes=[prg(ps, 0, 512)], tag=f"{tag}sx{hf}{g}_{ic}")
                    em("tensor", lambda h, ic=ic, sqb=sqb, ps2=ps2: h.matmul(
                        PS[ps2][0:1, :], A["onecb"][:, :], A[sqb][:, 0:512],
                        start=(ic == 0), stop=(ic == NCK - 1)),
                       reads=[rg("onecb"), rg(sqb, 0, 512)],
                       writes=[prg(ps2, 0, 512)], tag=f"{tag}s2{hf}{g}_{ic}")
            if upto < 2:
                continue
            for g in range(2):
                if g * 512 >= tn:
                    continue
                ps = ("a", "b")[g]
                ps2 = ("c0", "c1")[g]
                gc = g * 512
                em("vector", lambda h, gc=gc, ps=ps: h.tensor_scalar(
                    A["rows"][0:1, gc:gc + 512], PS[ps][0:1, :], 1.0 / H, None, ALU.mult),
                   reads=[prg(ps, 0, 512)], writes=[rg("rows", gc, gc + 512)],
                   tag=f"{tag}m{hf}{g}")
                em("vector", lambda h, gc=gc: h.tensor_tensor(
                    A["rsc"][0:1, :], A["rows"][0:1, gc:gc + 512],
                    A["rows"][0:1, gc:gc + 512], ALU.mult),
                   reads=[rg("rows", gc, gc + 512)], writes=[rg("rsc")], tag=f"{tag}ms{hf}{g}")
                em("vector", lambda h, gc=gc, ps2=ps2: h.tensor_scalar(
                    A["rows2"][0:1, gc:gc + 512], PS[ps2][0:1, :], 1.0 / H, None, ALU.mult),
                   reads=[prg(ps2, 0, 512)], writes=[rg("rows2", gc, gc + 512)],
                   tag=f"{tag}v{hf}{g}")
                em("vector", lambda h, gc=gc: h.tensor_tensor(
                    A["rows2"][0:1, gc:gc + 512], A["rows2"][0:1, gc:gc + 512],
                    A["rsc"][0:1, :], ALU.subtract),
                   reads=[rg("rows2", gc, gc + 512), rg("rsc")],
                   writes=[rg("rows2", gc, gc + 512)], tag=f"{tag}v2{hf}{g}")
                em("vector", lambda h, gc=gc: h.tensor_scalar(
                    A["rows2"][0:1, gc:gc + 512], A["rows2"][0:1, gc:gc + 512],
                    float(EPS), None, ALU.add),
                   reads=[rg("rows2", gc, gc + 512)], writes=[rg("rows2", gc, gc + 512)],
                   tag=f"{tag}v3{hf}{g}")
                em("scalar", lambda h, gc=gc: h.activation(
                    A["rsc"][0:1, :], A["rows2"][0:1, gc:gc + 512], AF.Sqrt,
                    bias=0.0, scale=1.0),
                   reads=[rg("rows2", gc, gc + 512)], writes=[rg("rsc")], tag=f"{tag}sd{hf}{g}")
                em("vector", lambda h, gc=gc: h.reciprocal(
                    A["rows2"][0:1, gc:gc + 512], A["rsc"][0:1, :]),
                   reads=[rg("rsc")], writes=[rg("rows2", gc, gc + 512)], tag=f"{tag}i{hf}{g}")
                for (src, dst) in (("rows", "mbc"), ("rows2", "ibc")):
                    ps3 = (("a", "c0"), ("b", "c1"))[g][src == "rows2"]
                    em("tensor", lambda h, gc=gc, src=src, ps3=ps3: h.matmul(
                        PS[ps3][:, :], A["onerf"][0:1, :], A[src][0:1, gc:gc + 512],
                        start=True, stop=True),
                       reads=[rg("onerf"), rg(src, gc, gc + 512)], writes=[prg(ps3)],
                       tag=f"{tag}bc{hf}{g}{dst}")
                    em("scalar", lambda h, t0=t0, gc=gc, dst=dst, ps3=ps3: h.copy(
                        A[dst][:, t0 + gc: t0 + gc + 512], PS[ps3][:, :]),
                       reads=[prg(ps3)], writes=[rg(dst, t0 + gc, t0 + gc + 512)],
                       tag=f"{tag}be{hf}{g}{dst}")
        if upto < 3:
            return
        for ic in range(NCK):
            em("vector", lambda h, ic=ic: h.tensor_tensor(
                A[buf][:, ic * bufstride: ic * bufstride + ntok],
                A[buf][:, ic * bufstride: ic * bufstride + ntok],
                A["mbc"][:, 0:ntok], ALU.subtract),
               reads=[rg(buf, ic * bufstride, ic * bufstride + ntok), rg("mbc", 0, ntok)],
               writes=[rg(buf, ic * bufstride, ic * bufstride + ntok)], tag=f"{tag}ce{ic}")
            em("vector", lambda h, ic=ic: h.tensor_tensor(
                A[buf][:, ic * bufstride: ic * bufstride + ntok],
                A[buf][:, ic * bufstride: ic * bufstride + ntok],
                A["ibc"][:, 0:ntok], ALU.mult),
               reads=[rg(buf, ic * bufstride, ic * bufstride + ntok), rg("ibc", 0, ntok)],
               writes=[rg(buf, ic * bufstride, ic * bufstride + ntok)], tag=f"{tag}cs{ic}")

    if stop_after(0.2):
        return materialize()
    ln_stats_center("xb", S, S, "mbc", "ln1",
                    upto=(0 if phase_limit <= 0.4 else 1 if phase_limit <= 0.6
                          else 2 if phase_limit <= 0.8 else 3))
    if "xb" in dbg:
        em("sync", lambda h: h.dma_start(out=dbg["xb"][:, :], in_=A["xb"][:, :]),
           reads=[rg("xb")], dma=True, tag="dbg_xb")

    if stop_after(1):
        return materialize()

    def load_head_weights(hd):
        p = hd % 2
        em("gpsimd", lambda h, hd=hd, p=p: h.dma_start(
            out=A[f"wqh{p}"][:, :], in_=wq_img[hd * 128:(hd + 1) * 128, :]),
           writes=[rg(f"wqh{p}")], dma=True, tag=f"ldwq{hd}")
        em("gpsimd", lambda h, hd=hd, p=p: h.dma_start(
            out=A[f"wkh{p}"][:, :], in_=wk_img[hd * 128:(hd + 1) * 128, :]),
           writes=[rg(f"wkh{p}")], dma=True, tag=f"ldwk{hd}")
        if hd % 2 == 0:
            g = hd // 2
            pv = g % 2
            em("gpsimd", lambda h, g=g, pv=pv: h.dma_start(
                out=A[f"wvg{pv}"][:, :], in_=wv_img[g * 128:(g + 1) * 128, :]),
               writes=[rg(f"wvg{pv}")], dma=True, tag=f"ldwv{g}")

    PROT = [0]
    PEND_EVAC = [None]

    def proj_qk(hd, kind):
        p = hd % 2
        wb = f"wqh{p}" if kind == "q" else f"wkh{p}"
        dst = f"qp{p}" if kind == "q" else f"kp{p}"
        bias = "bqc" if kind == "q" else "bkc"
        ntt = 2 if kind == "q" else 4
        n = ntt * 512
        for tt in range(ntt):
            ps = ("a", "b", "t2")[(PROT[0] + tt) % 3]
            for ic in range(NCK):
                em("tensor", lambda h, ic=ic, tt=tt, wb=wb, ps=ps: h.matmul(
                    PS[ps][:, :], A[wb][:, ic * 128:(ic + 1) * 128],
                    A["xb"][:, ic * S + tt * 512: ic * S + (tt + 1) * 512],
                    start=(ic == 0), stop=(ic == NCK - 1)),
                   reads=[rg(wb, ic * 128, (ic + 1) * 128),
                          rg("xb", ic * S + tt * 512, ic * S + (tt + 1) * 512)],
                   writes=[prg(ps)], tag=f"{kind}mm{hd}_{tt}_{ic}")
            if tt % 2 == 0:
                em("vector", lambda h, tt=tt, hd=hd, bias=bias, ps=ps: h.tensor_scalar(
                    A["qktmp"][:, tt * 512:(tt + 1) * 512], PS[ps][:, :],
                    A[bias][:, hd:hd + 1], None, ALU.add),
                   reads=[prg(ps), rg(bias, hd, hd + 1)],
                   writes=[rg("qktmp", tt * 512, (tt + 1) * 512)], tag=f"{kind}ev{hd}_{tt}")
            else:
                em("scalar", lambda h, tt=tt, hd=hd, bias=bias, ps=ps: h.activation(
                    A["qktmp"][:, tt * 512:(tt + 1) * 512], PS[ps][:, :], AF.Identity,
                    bias=A[bias][:, hd:hd + 1], scale=1.0),
                   reads=[prg(ps), rg(bias, hd, hd + 1)],
                   writes=[rg("qktmp", tt * 512, (tt + 1) * 512)], tag=f"{kind}ev{hd}_{tt}")
        PROT[0] = (PROT[0] + ntt) % 3
        em("vector", lambda h, dst=dst, n=n: h.tensor_tensor(
            A[dst][:, 0:n], A["qktmp"][:, 0:n], A["cosk"][:, 0:n], ALU.mult),
           reads=[rg("qktmp", 0, n), rg("cosk", 0, n)], writes=[rg(dst, 0, n)],
           tag=f"{kind}rc{hd}")
        em("vector", lambda h, n=n: h.tensor_tensor(
            A["r3"][0:64, 0:n], A["qktmp"][64:128, 0:n], A["sink"][64:128, 0:n], ALU.mult),
           reads=[rg("qktmp", 0, n), rg("sink", 0, n)], writes=[rg("r3", 0, n)],
           tag=f"{kind}rs1{hd}")
        em("vector", lambda h, n=n: h.tensor_tensor(
            A["r3"][64:128, 0:n], A["qktmp"][0:64, 0:n], A["sink"][0:64, 0:n], ALU.mult),
           reads=[rg("qktmp", 0, n), rg("sink", 0, n)], writes=[rg("r3", 0, n)],
           tag=f"{kind}rs2{hd}")
        em("vector", lambda h, dst=dst, n=n: h.tensor_tensor(
            A[dst][:, 0:n], A[dst][:, 0:n], A["r3"][:, 0:n], ALU.add),
           reads=[rg(dst, 0, n), rg("r3", 0, n)], writes=[rg(dst, 0, n)],
           tag=f"{kind}ra{hd}")

    def proj_v(hd):
        g = hd // 2
        pv = g % 2
        for tc in range(NCK):
            ps = ("a", "b", "t2")[(PROT[0] + tc) % 3]
            for ic in range(NCK):
                em("tensor", lambda h, ic=ic, tc=tc, pv=pv, ps=ps: h.matmul(
                    PS[ps][:, 0:256], A["xb"][:, ic * S + tc * 128: ic * S + (tc + 1) * 128],
                    A[f"wvg{pv}"][:, ic * 256:(ic + 1) * 256],
                    start=(ic == 0), stop=(ic == NCK - 1)),
                   reads=[rg("xb", ic * S + tc * 128, ic * S + (tc + 1) * 128),
                          rg(f"wvg{pv}", ic * 256, (ic + 1) * 256)],
                   writes=[prg(ps, 0, 256)], tag=f"vmm{g}_{tc}_{ic}")
            for sub in range(2):
                hh = g * 2 + sub
                pw = hh % 2
                em("vector", lambda h, tc=tc, sub=sub, hh=hh, pw=pw, ps=ps: h.tensor_tensor(
                    A[f"vp{pw}"][:, tc * 129: tc * 129 + 128],
                    PS[ps][:, sub * 128:(sub + 1) * 128],
                    A["bvbc"][:, hh * 128:(hh + 1) * 128], ALU.add),
                   reads=[prg(ps, sub * 128, (sub + 1) * 128),
                          rg("bvbc", hh * 128, (hh + 1) * 128)],
                   writes=[rg(f"vp{pw}", tc * 129, tc * 129 + 128)],
                   tag=f"vev{hh}_{tc}")
        PROT[0] = (PROT[0] + NCK) % 3

    def attn(hd):
        p = hd % 2
        ctx_slot = [("c0", 0), ("c1", 0)]
        tilectr = [0]
        for grp in range(4):
            slots = list(range(grp * 2, grp * 2 + 2))
            kcs = [kc for kc in range(16) if (kc % 8) <= slots[-1]]
            pend = []
            kcount = 0
            for kc in kcs:
                j0 = max(kc % 8, slots[0])
                if j0 > slots[-1]:
                    continue
                n = (slots[-1] - j0 + 1) * 128
                q0 = j0 * 128
                sl = tilectr[0] % 2
                tilectr[0] += 1
                pp = f"pt{sl}"
                ps, so = (("s0", 0), ("s1", 0))[sl]
                w = n
                em("tensor", lambda h, kc=kc, q0=q0, w=w, p=p, ps=ps, so=so: h.matmul(
                    PS[ps][:, so:so + w], A[f"kp{p}"][:, kc * 128:(kc + 1) * 128],
                    A[f"qp{p}"][:, q0: q0 + w], start=True, stop=True),
                   reads=[rg(f"kp{p}", kc * 128, (kc + 1) * 128),
                          rg(f"qp{p}", q0, q0 + w)],
                   writes=[prg(ps, so, so + w)], tag=f"sc{hd}_{kc}")
                if kc >= 8 and j0 == kc - 8:
                    em("scalar", lambda h, ps=ps, pp=pp, q0=q0, so=so: h.activation(
                        A[pp][:, q0: q0 + 128], PS[ps][:, so:so + 128], AF.Exp,
                        bias=A["mcol"][:, 0:1], scale=RSQ),
                       reads=[prg(ps, so, so + 128), rg("mcol")],
                       writes=[rg(pp, q0, q0 + 128)], tag=f"exm{hd}_{kc}")
                    if w > 128:
                        em("scalar", lambda h, ps=ps, pp=pp, q0=q0, w=w, so=so: h.activation(
                            A[pp][:, q0 + 128: q0 + w], PS[ps][:, so + 128:so + w], AF.Exp,
                            bias=0.0, scale=RSQ),
                           reads=[prg(ps, so + 128, so + w)],
                           writes=[rg(pp, q0 + 128, q0 + w)], tag=f"exr{hd}_{kc}")
                else:
                    em("scalar", lambda h, ps=ps, pp=pp, q0=q0, w=w, so=so: h.activation(
                        A[pp][:, q0: q0 + w], PS[ps][:, so:so + w], AF.Exp,
                        bias=0.0, scale=RSQ),
                       reads=[prg(ps, so, so + w)], writes=[rg(pp, q0, q0 + w)],
                       tag=f"ex{hd}_{kc}")
                if kc < 8 and j0 == kc:
                    em("vector", lambda h, pp=pp, q0=q0: h.tensor_tensor(
                        A[pp][:, q0: q0 + 128], A[pp][:, q0: q0 + 128],
                        A["tri"][:, :], ALU.mult),
                       reads=[rg(pp, q0, q0 + 128), rg("tri")],
                       writes=[rg(pp, q0, q0 + 128)], tag=f"tri{hd}_{kc}")

                def ctx_emit(kc=kc, j0=j0, w=w, pp=pp):
                    for j in range(j0, j0 + w // 128):
                        bank, boff = ctx_slot[j % 2]
                        em("tensor", lambda h, j=j, kc=kc, p=p, pp=pp, bank=bank,
                           boff=boff: h.matmul(
                            PS[bank][:, boff:boff + 129], A[pp][:, j * 128:(j + 1) * 128],
                            A[f"vp{p}"][:, kc * 129:(kc + 1) * 129],
                            start=(kc == 0), stop=(kc == 8 + j)),
                           reads=[rg(pp, j * 128, (j + 1) * 128),
                                  rg(f"vp{p}", kc * 129, (kc + 1) * 129)],
                           writes=[prg(bank, boff, boff + 129)], tag=f"cx{hd}_{kc}_{j}")
                kcount += 1
                if kcount == 2 and PEND_EVAC[0] is not None:
                    PEND_EVAC[0]()
                    PEND_EVAC[0] = None
                pend.append(ctx_emit)
                if len(pend) > 1:
                    pend.pop(0)()
            for f in pend:
                f()
            if PEND_EVAC[0] is not None:
                PEND_EVAC[0]()
                PEND_EVAC[0] = None

            def grp_evac(slots=slots, hd=hd):
                for j in slots:
                    bank, boff = ctx_slot[j % 2]
                    em("vector", lambda h, bank=bank, boff=boff: h.reciprocal(
                        A["rcol"][:, 0:1], PS[bank][:, boff + 128:boff + 129]),
                       reads=[prg(bank, boff + 128, boff + 129)], writes=[rg("rcol", 0, 1)],
                       tag=f"rc{hd}_{j}")
                    em("scalar", lambda h, bank=bank, boff=boff, j=j: h.activation(
                        A["ctxq"][:, (j % 2) * 128:(j % 2) * 128 + 128],
                        PS[bank][:, boff:boff + 128], AF.Copy,
                        bias=0.0, scale=A["rcol"][:, 0:1]),
                       reads=[prg(bank, boff, boff + 128), rg("rcol", 0, 1)],
                       writes=[rg("ctxq", (j % 2) * 128, (j % 2) * 128 + 128)],
                       tag=f"nm{hd}_{j}")
                    em("tensor", lambda h, j=j: h.transpose(
                        PS["t"][:, 0:128], A["ctxq"][:, (j % 2) * 128:(j % 2) * 128 + 128],
                        A["id128"][:, :]),
                       reads=[rg("ctxq", (j % 2) * 128, (j % 2) * 128 + 128), rg("id128")],
                       writes=[prg("t")], tag=f"tp{hd}_{j}")
                    em("vector", lambda h, hd=hd, j=j: h.tensor_copy(
                        A["ctxsb"][:, hd * TQ + j * 128: hd * TQ + (j + 1) * 128],
                        PS["t"][:, 0:128]),
                       reads=[prg("t")],
                       writes=[rg("ctxsb", hd * TQ + j * 128, hd * TQ + (j + 1) * 128)],
                       tag=f"cte{hd}_{j}")
            PEND_EVAC[0] = grp_evac

    load_head_weights(0)
    load_head_weights(1)
    proj_qk(0, "q")
    proj_qk(0, "k")
    for hd in range(NH):
        if hd + 2 < NH:
            load_head_weights(hd + 2)
        if hd + 1 < NH:
            proj_qk(hd + 1, "q")
            proj_qk(hd + 1, "k")
        if hd % 2 == 0:
            proj_v(hd)
        attn(hd)
        if hd == 0:
            if "qp" in dbg:
                em("sync", lambda h: h.dma_start(out=dbg["qp"][:, :], in_=A["qp0"][:, :]),
                   reads=[rg("qp0")], dma=True, tag="dbg_qp")
            if "kp" in dbg:
                em("sync", lambda h: h.dma_start(out=dbg["kp"][:, :], in_=A["kp0"][:, :]),
                   reads=[rg("kp0")], dma=True, tag="dbg_kp")
            if "vp" in dbg:
                em("sync", lambda h: h.dma_start(out=dbg["vp"][:, :], in_=A["vp0"][:, :]),
                   reads=[rg("vp0")], dma=True, tag="dbg_vp")
    PEND_EVAC[0]()
    PEND_EVAC[0] = None
    if "ctx" in dbg:
        for ic in range(NCK):
            em("sync", lambda h, ic=ic: h.dma_start(
                out=dbg["ctx"][ic * 128:(ic + 1) * 128, :],
                in_=A["ctxsb"][:, ic * TQ:(ic + 1) * TQ]),
               reads=[rg("ctxsb", ic * TQ, (ic + 1) * TQ)], dma=True, tag=f"dbg_ctx{ic}")

    if stop_after(2):
        return materialize()

    def load_band(dst, dram, band, tagp):
        em("gpsimd", lambda h, dst=dst, dram=dram, band=band: h.dma_start(
            out=A[dst][:, :], in_=dram[band * 128:(band + 1) * 128, :]),
           writes=[rg(dst)], dma=True, tag=tagp)

    load_band("wch0", wo_img, 0, "ldwo0")
    for hoc in range(NCK):
        pw = hoc % 2
        if hoc + 1 < NCK:
            load_band(f"wch{(hoc + 1) % 2}", wo_img, hoc + 1, f"ldwo{hoc + 1}")
        em("sync", lambda h, hoc=hoc, pw=pw: h.dma_start(
            out=A[f"xqs{pw}"][:, :], in_=xT[hoc * 128:(hoc + 1) * 128, 0:TQ]),
           writes=[rg(f"xqs{pw}")], dma=True, tag=f"ldxq{hoc}")
        for tt in range(2):
            ps = ("a", "b")[tt]
            pf = tt
            for ic in range(NCK):
                em("tensor", lambda h, ic=ic, tt=tt, pw=pw, ps=ps: h.matmul(
                    PS[ps][:, :], A[f"wch{pw}"][:, ic * 128:(ic + 1) * 128],
                    A["ctxsb"][:, ic * TQ + tt * 512: ic * TQ + (tt + 1) * 512],
                    start=(ic == 0), stop=(ic == NCK - 1)),
                   reads=[rg(f"wch{pw}", ic * 128, (ic + 1) * 128),
                          rg("ctxsb", ic * TQ + tt * 512, ic * TQ + (tt + 1) * 512)],
                   writes=[prg(ps)], tag=f"omm{hoc}_{tt}_{ic}")
            em("scalar", lambda h, hoc=hoc, pf=pf, ps=ps: h.activation(
                A[f"ytmp{pf}"][:, :], PS[ps][:, :], AF.Identity,
                bias=A["boc"][:, hoc:hoc + 1], scale=1.0),
               reads=[prg(ps), rg("boc", hoc, hoc + 1)],
               writes=[rg(f"ytmp{pf}")], tag=f"oev{hoc}_{tt}")
            em("vector", lambda h, hoc=hoc, tt=tt, pw=pw, pf=pf: h.tensor_tensor(
                A["y1"][:, hoc * TQ + tt * 512: hoc * TQ + (tt + 1) * 512],
                A[f"ytmp{pf}"][:, :], A[f"xqs{pw}"][:, tt * 512:(tt + 1) * 512], ALU.add),
               reads=[rg(f"ytmp{pf}"), rg(f"xqs{pw}", tt * 512, (tt + 1) * 512)],
               writes=[rg("y1", hoc * TQ + tt * 512, hoc * TQ + (tt + 1) * 512)],
               tag=f"ores{hoc}_{tt}")
    if "y1" in dbg:
        for ic in range(NCK):
            em("sync", lambda h, ic=ic: h.dma_start(
                out=dbg["y1"][ic * 128:(ic + 1) * 128, :],
                in_=A["y1"][:, ic * TQ:(ic + 1) * TQ]),
               reads=[rg("y1", ic * TQ, (ic + 1) * TQ)], dma=True, tag=f"dbg_y1{ic}")

    if stop_after(3):
        return materialize()

    for ic in range(NCK):
        em("vector", lambda h, ic=ic: h.tensor_copy(
            A["ln2"][:, ic * TQ:(ic + 1) * TQ], A["y1"][:, ic * TQ:(ic + 1) * TQ]),
           reads=[rg("y1", ic * TQ, (ic + 1) * TQ)],
           writes=[rg("ln2", ic * TQ, (ic + 1) * TQ)], tag=f"l2c{ic}")
    ln_stats_center("ln2", TQ, TQ, "mbc", "ln2")
    if "ln2" in dbg:
        em("sync", lambda h: h.dma_start(out=dbg["ln2"][:, :], in_=A["ln2"][:, :]),
           reads=[rg("ln2")], dma=True, tag="dbg_ln2")

    if stop_after(4):
        return materialize()

    TK = A["tk"]
    for tt in range(2):
        ps = ("a", "b")[tt]
        for ic in range(NCK):
            em("tensor", lambda h, ic=ic, tt=tt, ps=ps: h.matmul(
                PS[ps][0:8, :], A["Gb"][:, ic * 8:(ic + 1) * 8],
                A["ln2"][:, ic * TQ + tt * 512: ic * TQ + (tt + 1) * 512],
                start=(ic == 0), stop=(ic == NCK - 1)),
               reads=[rg("Gb", ic * 8, (ic + 1) * 8),
                      rg("ln2", ic * TQ + tt * 512, ic * TQ + (tt + 1) * 512)],
               writes=[prg(ps, 0, 512)], tag=f"lg{tt}_{ic}")
        em("scalar", lambda h, tt=tt, ps=ps: h.copy(
            A["logsb"][0:8, tt * 512:(tt + 1) * 512], PS[ps][0:8, :]),
           reads=[prg(ps, 0, 512)], writes=[rg("logsb", tt * 512, (tt + 1) * 512)],
           tag=f"lge{tt}")

    def router_tg(tg):
        em("tensor", lambda h, tg=tg: h.transpose(
            PS["t2"][:, 0:8], A["logsb"][0:8, tg * 128:(tg + 1) * 128], A["id8f"][0:8, :]),
           reads=[rg("logsb", tg * 128, (tg + 1) * 128), rg("id8f")],
           writes=[prg("t2", 0, 8)], tag=f"rtp{tg}")
        em("vector", lambda h: h.tensor_copy(TK[:, 0:8], PS["t2"][:, 0:8]),
           reads=[prg("t2", 0, 8)], writes=[rg("tk", 0, 8)], tag=f"rtc{tg}")
        em("vector", lambda h: h.reduce_max(TK[:, 16:17], TK[:, 0:8], axis=AX.X,
                                            negate=True),
           reads=[rg("tk", 0, 8)], writes=[rg("tk", 16, 17)], tag=f"rmx{tg}")
        em("scalar", lambda h: h.activation(TK[:, 8:16], TK[:, 0:8], AF.Exp,
                                            bias=TK[:, 16:17], scale=1.0),
           reads=[rg("tk", 0, 8), rg("tk", 16, 17)], writes=[rg("tk", 8, 16)],
           tag=f"rex{tg}")
        em("vector", lambda h: h.reduce_sum(TK[:, 17:18], TK[:, 8:16], axis=AX.X),
           reads=[rg("tk", 8, 16)], writes=[rg("tk", 17, 18)], tag=f"rsm{tg}")
        em("vector", lambda h: h.reciprocal(TK[:, 18:19], TK[:, 17:18]),
           reads=[rg("tk", 17, 18)], writes=[rg("tk", 18, 19)], tag=f"rrc{tg}")
        em("vector", lambda h: h.tensor_scalar(TK[:, 24:32], TK[:, 8:16],
                                               TK[:, 18:19], None, ALU.mult),
           reads=[rg("tk", 8, 16), rg("tk", 18, 19)], writes=[rg("tk", 24, 32)],
           tag=f"rp{tg}")
        em("vector", lambda h: h.reduce_max(TK[:, 19:20], TK[:, 24:32], axis=AX.X),
           reads=[rg("tk", 24, 32)], writes=[rg("tk", 19, 20)], tag=f"rpm{tg}")
        em("vector", lambda h: h.tensor_scalar(TK[:, 32:40], TK[:, 24:32],
                                               TK[:, 19:20], None, ALU.is_lt),
           reads=[rg("tk", 24, 32), rg("tk", 19, 20)], writes=[rg("tk", 32, 40)],
           tag=f"rlt{tg}")
        em("vector", lambda h: h.tensor_tensor(TK[:, 32:40], TK[:, 32:40],
                                               TK[:, 24:32], ALU.mult),
           reads=[rg("tk", 32, 40), rg("tk", 24, 32)], writes=[rg("tk", 32, 40)],
           tag=f"rl2{tg}")
        em("vector", lambda h: h.reduce_max(TK[:, 20:21], TK[:, 32:40], axis=AX.X),
           reads=[rg("tk", 32, 40)], writes=[rg("tk", 20, 21)], tag=f"rsc{tg}")
        em("vector", lambda h: h.tensor_scalar(TK[:, 40:48], TK[:, 24:32],
                                               TK[:, 20:21], None, ALU.is_ge),
           reads=[rg("tk", 24, 32), rg("tk", 20, 21)], writes=[rg("tk", 40, 48)],
           tag=f"rge{tg}")
        em("vector", lambda h: h.tensor_tensor(TK[:, 48:56], TK[:, 40:48],
                                               TK[:, 24:32], ALU.mult),
           reads=[rg("tk", 40, 48), rg("tk", 24, 32)], writes=[rg("tk", 48, 56)],
           tag=f"rw{tg}")
        em("vector", lambda h: h.tensor_copy(A["wtk"][:, 0:8], TK[:, 48:56]),
           reads=[rg("tk", 48, 56)], writes=[rg("wtk")], tag=f"rwb{tg}")

    def router_tg2(tg):
        em("tensor", lambda h: h.transpose(PS["t"][0:8, 0:128], A["wtk"][:, 0:8],
                                           A["id128"][:, :]),
           reads=[rg("wtk"), rg("id128")], writes=[prg("t")], tag=f"rtb{tg}")
        em("scalar", lambda h, tg=tg: h.copy(
            A["w8sb"][0:8, tg * 128:(tg + 1) * 128], PS["t"][0:8, 0:128]),
           reads=[prg("t")], writes=[rg("w8sb", tg * 128, (tg + 1) * 128)],
           tag=f"rwe{tg}")

    def load_fband(dst, dram, band, tagp):
        em("gpsimd", lambda h, dst=dst, dram=dram, band=band: h.dma_start(
            out=A[dst][:, :], in_=dram[band * 128:(band + 1) * 128, :]),
           writes=[rg(dst)], dma=True, tag=tagp)

    for tt in range(2):
        load_fband("wgh0", g_img, 0, f"ldg{tt}_0")
        load_fband("wuh0", u_img, 0, f"ldu{tt}_0")
        for efc in range(NCK):
            pw = efc % 2
            if efc + 1 < NCK:
                load_fband(f"wgh{(efc + 1) % 2}", g_img, efc + 1, f"ldg{tt}_{efc + 1}")
                load_fband(f"wuh{(efc + 1) % 2}", u_img, efc + 1, f"ldu{tt}_{efc + 1}")
            psg = ("a", "b")[efc % 2]
            psu = ("c0", "c1")[efc % 2]
            for ic in range(NCK):
                em("tensor", lambda h, ic=ic, tt=tt, pw=pw, psg=psg: h.matmul(
                    PS[psg][:, :], A[f"wgh{pw}"][:, ic * 128:(ic + 1) * 128],
                    A["ln2"][:, ic * TQ + tt * 512: ic * TQ + (tt + 1) * 512],
                    start=(ic == 0), stop=(ic == NCK - 1)),
                   reads=[rg(f"wgh{pw}", ic * 128, (ic + 1) * 128),
                          rg("ln2", ic * TQ + tt * 512, ic * TQ + (tt + 1) * 512)],
                   writes=[prg(psg)], tag=f"gmm{tt}_{efc}_{ic}")
                em("tensor", lambda h, ic=ic, tt=tt, pw=pw, psu=psu: h.matmul(
                    PS[psu][:, :], A[f"wuh{pw}"][:, ic * 128:(ic + 1) * 128],
                    A["ln2"][:, ic * TQ + tt * 512: ic * TQ + (tt + 1) * 512],
                    start=(ic == 0), stop=(ic == NCK - 1)),
                   reads=[rg(f"wuh{pw}", ic * 128, (ic + 1) * 128),
                          rg("ln2", ic * TQ + tt * 512, ic * TQ + (tt + 1) * 512)],
                   writes=[prg(psu)], tag=f"umm{tt}_{efc}_{ic}")
            sc0 = (efc % 2) * 512
            em("scalar", lambda h, sc0=sc0, psg=psg: h.activation(
                A["sil"][:, sc0:sc0 + 512], PS[psg][:, :], AF.Silu, bias=0.0, scale=1.0),
               reads=[prg(psg)], writes=[rg("sil", sc0, sc0 + 512)], tag=f"sil{tt}_{efc}")
            em("vector", lambda h, efc=efc, sc0=sc0, psu=psu: h.tensor_tensor(
                A["gated"][:, efc * 512:(efc + 1) * 512], PS[psu][:, :],
                A["sil"][:, sc0:sc0 + 512], ALU.mult),
               reads=[prg(psu), rg("sil", sc0, sc0 + 512)],
               writes=[rg("gated", efc * 512, (efc + 1) * 512)], tag=f"gu{tt}_{efc}")
            if tt == 0 and efc % 2 == 0:
                router_tg(efc // 2)
            if tt == 0 and efc % 2 == 1:
                router_tg2(efc // 2)
        if tt == 0:
            for g in range(2):
                ps = ("c0", "c1")[g]
                em("tensor", lambda h, g=g, ps=ps: h.matmul(
                    PS[ps][:, :], A["eer"][0:8, :], A["w8sb"][0:8, g * 512:(g + 1) * 512],
                    start=True, stop=True),
                   reads=[rg("eer"), rg("w8sb", g * 512, (g + 1) * 512)], writes=[prg(ps)],
                   tag=f"wer{g}")
                em("scalar", lambda h, g=g, ps=ps: h.copy(
                    A["wer"][:, g * 512:(g + 1) * 512], PS[ps][:, :]),
                   reads=[prg(ps)], writes=[rg("wer", g * 512, (g + 1) * 512)], tag=f"were{g}")
            for ltt in range(2):
                ps = ("a", "b")[ltt]
                for ic in range(NCK):
                    em("tensor", lambda h, ic=ic, ltt=ltt, ps=ps: h.matmul(
                        PS[ps][:, :], A["lATb"][:, ic * 128:(ic + 1) * 128],
                        A["ln2"][:, ic * TQ + ltt * 512: ic * TQ + (ltt + 1) * 512],
                        start=(ic == 0), stop=(ic == NCK - 1)),
                       reads=[rg("lATb", ic * 128, (ic + 1) * 128),
                              rg("ln2", ic * TQ + ltt * 512, ic * TQ + (ltt + 1) * 512)],
                       writes=[prg(ps)], tag=f"lrmm{ltt}_{ic}")
                em("vector", lambda h, ltt=ltt, ps=ps: h.tensor_tensor(
                    A["lrw"][:, ltt * 512:(ltt + 1) * 512], PS[ps][:, :],
                    A["wer"][:, ltt * 512:(ltt + 1) * 512], ALU.mult),
                   reads=[prg(ps), rg("wer", ltt * 512, (ltt + 1) * 512)],
                   writes=[rg("lrw", ltt * 512, (ltt + 1) * 512)], tag=f"lrw{ltt}")
            if "w8" in dbg:
                em("sync", lambda h: h.dma_start(out=dbg["w8"][:, :], in_=A["w8sb"][0:8, :]),
                   reads=[rg("w8sb")], dma=True, tag="dbg_w8")
            if "lrw" in dbg:
                em("sync", lambda h: h.dma_start(out=dbg["lrw"][:, :], in_=A["lrw"][:, :]),
                   reads=[rg("lrw")], dma=True, tag="dbg_lrw")
        for e_ in range(8):
            pb = e_ % 2
            em("tensor", lambda h, e_=e_, tt=tt: h.matmul(
                PS["t2"][:, :], A["e8s"][0:8, e_ * 128:(e_ + 1) * 128],
                A["w8sb"][0:8, tt * 512:(tt + 1) * 512], start=True, stop=True),
               reads=[rg("e8s", e_ * 128, (e_ + 1) * 128),
                      rg("w8sb", tt * 512, (tt + 1) * 512)],
               writes=[prg("t2")], tag=f"wbc{tt}_{e_}")
            em("scalar", lambda h, pb=pb: h.copy(A[f"wbce{pb}"][:, :], PS["t2"][:, 0:512]),
               reads=[prg("t2")], writes=[rg(f"wbce{pb}")], tag=f"wbe{tt}_{e_}")
            for sub in range(2):
                efc = e_ * 2 + sub
                em("vector", lambda h, efc=efc, pb=pb: h.tensor_tensor(
                    A["gated"][:, efc * 512:(efc + 1) * 512],
                    A["gated"][:, efc * 512:(efc + 1) * 512],
                    A[f"wbce{pb}"][:, :], ALU.mult),
                   reads=[rg("gated", efc * 512, (efc + 1) * 512), rg(f"wbce{pb}")],
                   writes=[rg("gated", efc * 512, (efc + 1) * 512)], tag=f"gw{tt}_{efc}")
        if "gated" in dbg and tt == 0:
            em("sync", lambda h: h.dma_start(out=dbg["gated"][:, :], in_=A["gated"][:, :]),
               reads=[rg("gated")], dma=True, tag="dbg_gated")
        load_fband("wch0", d_img, 0, f"ldd{tt}_0")
        for hoc in range(NCK):
            pw = hoc % 2
            if hoc + 1 < NCK:
                load_fband(f"wch{(hoc + 1) % 2}", d_img, hoc + 1, f"ldd{tt}_{hoc + 1}")
            ps = ("a", "b", "s0", "s1")[hoc % 4]
            pf = hoc % 2
            for efc in range(NCK):
                em("tensor", lambda h, efc=efc, pw=pw, ps=ps: h.matmul(
                    PS[ps][:, :], A[f"wch{pw}"][:, efc * 128:(efc + 1) * 128],
                    A["gated"][:, efc * 512:(efc + 1) * 512],
                    start=(efc == 0), stop=False),
                   reads=[rg(f"wch{pw}", efc * 128, (efc + 1) * 128),
                          rg("gated", efc * 512, (efc + 1) * 512)],
                   writes=[prg(ps)], tag=f"dmm{tt}_{hoc}_{efc}")
            em("tensor", lambda h, hoc=hoc, tt=tt, ps=ps: h.matmul(
                PS[ps][:, :], A["lBTb"][:, hoc * 128:(hoc + 1) * 128],
                A["lrw"][:, tt * 512:(tt + 1) * 512], start=False, stop=True),
               reads=[rg("lBTb", hoc * 128, (hoc + 1) * 128),
                      rg("lrw", tt * 512, (tt + 1) * 512)],
               writes=[prg(ps)], tag=f"lbmm{tt}_{hoc}")
            em("vector", lambda h, hoc=hoc, tt=tt, pf=pf, ps=ps: h.tensor_tensor(
                A[f"fout{pf}"][:, :], PS[ps][:, :],
                A["y1"][:, hoc * TQ + tt * 512: hoc * TQ + (tt + 1) * 512], ALU.add),
               reads=[prg(ps), rg("y1", hoc * TQ + tt * 512, hoc * TQ + (tt + 1) * 512)],
               writes=[rg(f"fout{pf}")], tag=f"fres{tt}_{hoc}")
            em("sync", lambda h, hoc=hoc, tt=tt, pf=pf: h.dma_start(
                out=y_out[hoc * 128:(hoc + 1) * 128, tt * 512:(tt + 1) * 512],
                in_=A[f"fout{pf}"][:, :]),
               reads=[rg(f"fout{pf}")],
               writes=[("yout", (hoc * 2 + tt) * 512, (hoc * 2 + tt + 1) * 512)],
               dma=True, tag=f"st{tt}_{hoc}")

    em("gpsimd", lambda h: h.memset(A["joinpad"][:, 0:1], 0.0),
       reads=[("yout", 0, NCK * 2 * 512)], writes=[rg("joinpad")], tag="join")

    return materialize()


def _rope_tables(pos):
    j = np.arange(0, HD, 2, dtype=np.float32) / HD
    inv = (1.0 / 10000.0 ** j).astype(np.float32)
    fr = pos[:, None].astype(np.float32) * inv[None, :]
    cos = np.cos(fr).astype(np.float32)
    sin = np.sin(fr).astype(np.float32)
    cosT = np.ascontiguousarray(np.concatenate([cos, cos], axis=1).T)
    sinS = np.ascontiguousarray(np.concatenate([sin, -sin], axis=1).T)
    return cosT, sinS


def _band_img(T):
    nb = T.shape[1] // 128
    return np.ascontiguousarray(
        T.reshape(16, 128, nb, 128).transpose(2, 1, 0, 3).reshape(nb * 128, 2048))


def _vgroup_img(T):
    return np.ascontiguousarray(
        T.reshape(16, 128, 8, 256).transpose(2, 1, 0, 3).reshape(1024, 4096))


_NC_CACHE = [None]


def kernel(L0_attn_hidden_states, L0_attn_Wq, L0_attn_bq, L0_attn_Wk, L0_attn_bk,
           L0_attn_Wv, L0_attn_bv, L0_attn_Wo, L0_attn_bo, L0_ffn_G, L0_ffn_gate_w,
           L0_ffn_up_w, L0_ffn_down_w, L0_ffn_lora_A, L0_ffn_lora_B):
    x = np.ascontiguousarray(np.asarray(L0_attn_hidden_states, np.float32))
    wq_img = _band_img(np.asarray(L0_attn_Wq, np.float32).T)
    wk_img = _band_img(np.asarray(L0_attn_Wk, np.float32).T)
    wv_img = _vgroup_img(np.asarray(L0_attn_Wv, np.float32).T)
    wo_img = _band_img(np.asarray(L0_attn_Wo, np.float32).T)
    g_img = _band_img(np.asarray(L0_ffn_gate_w, np.float32).transpose(2, 0, 1).reshape(H, EF))
    u_img = _band_img(np.asarray(L0_ffn_up_w, np.float32).transpose(2, 0, 1).reshape(H, EF))
    d_img = _band_img(np.asarray(L0_ffn_down_w, np.float32).transpose(0, 2, 1).reshape(EF, H))
    bq = np.ascontiguousarray(np.asarray(L0_attn_bq, np.float32).reshape(16, 128).T)
    bk = np.ascontiguousarray(np.asarray(L0_attn_bk, np.float32).reshape(16, 128).T)
    bo = np.ascontiguousarray(np.asarray(L0_attn_bo, np.float32).reshape(16, 128).T)
    bv = np.ascontiguousarray(np.asarray(L0_attn_bv, np.float32).reshape(1, H))
    G = np.asarray(L0_ffn_G, np.float32)
    G_img = np.ascontiguousarray(G.reshape(16, 128, 8).transpose(1, 0, 2).reshape(128, 128))
    lAT = np.asarray(L0_ffn_lora_A, np.float32).transpose(2, 0, 1).reshape(H, 128)
    lAT_img = np.ascontiguousarray(
        lAT.reshape(16, 128, 128).transpose(1, 0, 2).reshape(128, 2048))
    lBT = np.ascontiguousarray(
        np.asarray(L0_ffn_lora_B, np.float32).transpose(0, 2, 1).reshape(128, H))

    if _NC_CACHE[0] is None or _NC_CACHE[0][0] != (PHASE_LIMIT, tuple(DEBUG_DUMP)):
        _NC_CACHE[0] = ((PHASE_LIMIT, tuple(DEBUG_DUMP)), build_nc())
    nc = _NC_CACHE[0][1]

    in_maps = []
    perms = []
    for c in range(8):
        b, h2 = c // 2, c % 2
        chunks = [2 * j + h2 for j in range(8)] + [2 * j + (1 - h2) for j in range(8)]
        cols = np.concatenate([np.arange(t * 128, (t + 1) * 128) for t in chunks])
        perms.append(cols)
        xTc = np.ascontiguousarray(x[b].T[:, cols])
        cosk, sink = _rope_tables(cols.astype(np.float32))
        mcol = np.full((128, 1), MASKNEG if h2 == 0 else 0.0, np.float32)
        in_maps.append({
            "xT": xTc, "wq_img": wq_img, "wk_img": wk_img, "wv_img": wv_img,
            "wo_img": wo_img, "g_img": g_img, "u_img": u_img, "d_img": d_img,
            "lAT": lAT_img, "lBT": lBT, "G_img": G_img,
            "bq_img": bq, "bk_img": bk, "bo_img": bo, "bv_row": bv,
            "cosk": cosk, "sink": sink, "maskcol": mcol,
        })
    res = run_bass_kernel_spmd(nc, in_maps, core_ids=list(range(8)))
    y = np.zeros((B, S, H), np.float32)
    for c in range(8):
        b = c // 2
        yq = res.results[c]["y_out"]
        y[b, perms[c][:TQ], :] = yq.T
    if DEBUG_DUMP:
        kernel._dbg = [{k: v for k, v in r.items() if k.startswith("dbg_")}
                       for r in res.results]
    return y


# version 14
# speedup vs baseline: 1.4340x; 1.0066x over previous
import numpy as np
import ml_dtypes
import concourse.bass as bass
import concourse.mybir as mybir
from concourse.bass_utils import run_bass_kernel_spmd

F32 = mybir.dt.float32
BF16 = mybir.dt.bfloat16
AF = mybir.ActivationFunctionType
ALU = mybir.AluOpType
AX = mybir.AxisListType
bfd = ml_dtypes.bfloat16

H = 2048
S = 2048
B = 4
NH = 16
HD = 128
NCK = 16
E = 8
EF = 2048
EPS = 1e-5
TQ = 1024
RSQ = float(1.0 / np.sqrt(np.float32(HD)))
MASKNEG = -1.0e9

ENGS = ("sync", "scalar", "vector", "tensor", "gpsimd")

DEBUG_DUMP = []
PHASE_LIMIT = 99
OP_LIMIT = None


LANES = {"gpsimd": 12, "sync": 6}


class Op:
    __slots__ = ("eng", "fn", "dma", "eidx", "deps", "inc", "waits", "tag", "chan",
                 "cidx")

    def __init__(self, eng, fn, dma, eidx, tag):
        self.eng = eng
        self.fn = fn
        self.dma = dma
        self.eidx = eidx
        self.deps = set()
        self.inc = False
        self.waits = []
        self.tag = tag
        self.chan = eng
        self.cidx = 0


class Prog:
    def __init__(self):
        self.eops = {e: [] for e in ENGS}
        self.segs = {}
        self.nemit = 0

    def emit(self, eng, fn, reads=(), writes=(), dma=False, tag=""):
        self.nemit += 1
        if OP_LIMIT is not None and self.nemit > OP_LIMIT and tag not in ("stub_out", "join"):
            return None
        op = Op(eng, fn, dma, len(self.eops[eng]), tag)
        self.eops[eng].append(op)
        for (sp, lo, hi) in reads:
            segs = self.segs.setdefault(sp, [])
            for seg in segs:
                if seg[0] < hi and lo < seg[1]:
                    if seg[2] is not None:
                        op.deps.add(seg[2])
                    seg[3].add(op)
        for (sp, lo, hi) in writes:
            segs = self.segs.setdefault(sp, [])
            new = []
            for seg in segs:
                if seg[0] < hi and lo < seg[1]:
                    if seg[2] is not None:
                        op.deps.add(seg[2])
                    op.deps.update(seg[3])
                    if seg[0] < lo:
                        new.append([seg[0], lo, seg[2], set(seg[3])])
                    if hi < seg[1]:
                        new.append([hi, seg[1], seg[2], set(seg[3])])
                else:
                    new.append(seg)
            new.append([lo, hi, op, set()])
            self.segs[sp] = new
        op.deps.discard(op)
        return op

    def finalize(self):
        lane_prev = {}
        for e in ENGS:
            nl = LANES.get(e, 0)
            ctr = 0
            for op in self.eops[e]:
                if op.dma and nl:
                    lane = ctr % nl
                    ctr += 1
                    op.chan = f"{e}:L{lane}"
                    prev = lane_prev.get(op.chan)
                    if prev is not None:
                        op.deps.add(prev)
                    lane_prev[op.chan] = op
        chans = list(ENGS) + sorted({op.chan for e in ENGS for op in self.eops[e]
                                     if op.chan not in ENGS})
        cctr = {c: 0 for c in chans}
        for e in ENGS:
            for op in self.eops[e]:
                op.cidx = cctr[op.chan]
                cctr[op.chan] += 1
        for e in ENGS:
            for op in self.eops[e]:
                best = {}
                for d in op.deps:
                    if d.eng == "tensor" and op.eng == "tensor":
                        continue
                    if d is op:
                        continue
                    cur = best.get(d.chan)
                    if cur is None or d.cidx > cur.cidx:
                        best[d.chan] = d
                op.deps = best
        wm = {e: {c: -1 for c in chans} for e in ENGS}
        for e in ENGS:
            w = wm[e]
            for op in self.eops[e]:
                kept = []
                for d in sorted(op.deps.values(), key=lambda d: d.chan):
                    if d.cidx <= w[d.chan]:
                        continue
                    if d.chan == op.chan and d.cidx >= op.cidx:
                        raise RuntimeError(f"self-dep ahead: {op.tag} <- {d.tag}")
                    w[d.chan] = d.cidx
                    kept.append(d)
                    d.inc = True
                op.deps = kept
        for e in ENGS:
            for op in self.eops[e]:
                if op.dma:
                    op.inc = True
        ccnt = {c: 0 for c in chans}
        cumv = {}
        for e in ENGS:
            for op in self.eops[e]:
                if op.inc:
                    ccnt[op.chan] += 16 if op.dma else 1
                cumv[(op.chan, op.cidx)] = ccnt[op.chan]
        for e in ENGS:
            for op in self.eops[e]:
                op.waits = [(d.chan, cumv[(d.chan, d.cidx)]) for d in op.deps]
        self.chans = chans
        pos = {e: 0 for e in ENGS}
        cnt = {c: 0 for c in chans}
        progress = True
        while progress:
            progress = False
            for e in ENGS:
                while pos[e] < len(self.eops[e]):
                    op = self.eops[e][pos[e]]
                    if all(cnt[se] >= v for (se, v) in op.waits):
                        if op.inc:
                            cnt[op.chan] += 16 if op.dma else 1
                        pos[e] += 1
                        progress = True
                    else:
                        break
        stuck = [e for e in ENGS if pos[e] < len(self.eops[e])]
        if stuck:
            msgs = []
            for e in stuck:
                op = self.eops[e][pos[e]]
                msgs.append(f"{e}[{pos[e]}] {op.tag} waits {op.waits}")
            raise RuntimeError("DEADLOCK:\n" + "\n".join(msgs))
        return cumv


def build_nc(phase_limit=None):
    if phase_limit is None:
        phase_limit = PHASE_LIMIT
    nc = bass.Bass("TRN2", target_bir_lowering=False, debug=False)

    xT = nc.dram_tensor("xT", [H, S], F32, kind="ExternalInput").ap()
    wq_img = nc.dram_tensor("wq_img", [H, H], F32, kind="ExternalInput").ap()
    wk_img = nc.dram_tensor("wk_img", [H, H], F32, kind="ExternalInput").ap()
    wv_img = nc.dram_tensor("wv_img", [1024, 4096], F32, kind="ExternalInput").ap()
    wo_img = nc.dram_tensor("wo_img", [H, H], F32, kind="ExternalInput").ap()
    g_img = nc.dram_tensor("g_img", [H, H], F32, kind="ExternalInput").ap()
    u_img = nc.dram_tensor("u_img", [H, H], F32, kind="ExternalInput").ap()
    d_img = nc.dram_tensor("d_img", [H, H], F32, kind="ExternalInput").ap()
    lATd = nc.dram_tensor("lAT", [128, 2048], F32, kind="ExternalInput").ap()
    lBTd = nc.dram_tensor("lBT", [128, H], F32, kind="ExternalInput").ap()
    Gd = nc.dram_tensor("G_img", [128, 128], F32, kind="ExternalInput").ap()
    bqd = nc.dram_tensor("bq_img", [128, 16], F32, kind="ExternalInput").ap()
    bkd = nc.dram_tensor("bk_img", [128, 16], F32, kind="ExternalInput").ap()
    bod = nc.dram_tensor("bo_img", [128, 16], F32, kind="ExternalInput").ap()
    bvd = nc.dram_tensor("bv_row", [1, H], F32, kind="ExternalInput").ap()
    coskd = nc.dram_tensor("cosk", [128, S], F32, kind="ExternalInput").ap()
    sinkd = nc.dram_tensor("sink", [128, S], F32, kind="ExternalInput").ap()
    mcold = nc.dram_tensor("maskcol", [128, 1], F32, kind="ExternalInput").ap()
    y_out = nc.dram_tensor("y_out", [H, TQ], F32, kind="ExternalOutput").ap()

    def bfc(a):
        return np.ascontiguousarray(np.asarray(a, np.float32).astype(bfd))

    tri_np = (np.arange(128)[:, None] <= np.arange(128)[None, :]).astype(np.float32)
    tri_d = nc.inline_tensor(bfc(tri_np), "c_tri")
    id128_d = nc.inline_tensor(bfc(np.eye(128)), "c_id128")
    id8f_d = nc.inline_tensor(np.ascontiguousarray(np.eye(8, dtype=np.float32)), "c_id8f")
    eer_np = np.zeros((8, 128), np.float32)
    for e_ in range(8):
        eer_np[e_, e_ * 16:(e_ + 1) * 16] = 1.0
    eer_d = nc.inline_tensor(bfc(eer_np), "c_eer")
    e8s_np = np.zeros((8, 8 * 128), np.float32)
    for e_ in range(8):
        e8s_np[e_, e_ * 128:(e_ + 1) * 128] = 1.0
    e8s_d = nc.inline_tensor(bfc(e8s_np), "c_e8s")

    dbg = {}
    for name, shape, dt in [
        ("xb", [128, NCK * S], BF16), ("qp", [128, TQ], BF16),
        ("kp", [128, S], BF16), ("vp", [128, 16 * 129], BF16),
        ("ctx", [H, TQ], BF16), ("y1", [H, TQ], F32),
        ("ln2", [128, NCK * TQ], BF16), ("w8", [8, TQ], BF16),
        ("gated", [128, NCK * 512], BF16), ("lrw", [128, TQ], BF16),
    ]:
        if name in DEBUG_DUMP:
            dbg[name] = nc.dram_tensor("dbg_" + name, shape, dt, kind="ExternalOutput").ap()

    SBUF_BASE = 16512
    SBUF_LIMIT = 212864
    A = {}
    meta = {}

    def place(name, ncols, dt, off):
        esz = 2 if dt == BF16 else 4
        off = (off + 31) // 32 * 32
        nb = ncols * esz
        assert off + nb <= SBUF_LIMIT, (name, off, nb)
        A[name] = nc.alloc_sbuf_tensor_at(name, [128, ncols], dt, offset=SBUF_BASE + off)
        meta[name] = (off, esz, ncols)
        return off + nb

    o = 0
    for nm, ncols, dt in [
        ("cosk", S, BF16), ("sink", S, BF16), ("bvbc", H, BF16),
        ("lATb", 2048, BF16), ("lBTb", H, BF16), ("Gb", 128, BF16),
        ("tri", 128, BF16), ("id128", 128, BF16), ("id8f", 8, F32),
        ("eer", 128, BF16), ("e8s", 8 * 128, BF16), ("bqc", 16, F32), ("bkc", 16, F32), ("boc", 16, F32),
        ("mcol", 1, F32), ("onecb", 1, BF16), ("onerf", 128, F32),
        ("onerb", 128, BF16), ("mbc", S, BF16), ("ibc", S, BF16),
        ("joinpad", 16, F32),
    ]:
        o = place(nm, ncols, dt, o)
    REG_P_END = o
    REG_X = (REG_P_END + 31) // 32 * 32
    place("xb", NCK * S, BF16, REG_X)
    place("y1", NCK * TQ, F32, REG_X)
    REG_X_END = REG_X + NCK * TQ * 4
    REG_C = (REG_X_END + 31) // 32 * 32
    place("ctxsb", NCK * TQ, BF16, REG_C)
    place("ln2", NCK * TQ, BF16, REG_C)
    REG_C_END = REG_C + NCK * TQ * 2
    o = (REG_C_END + 31) // 32 * 32
    REG_W = o
    for nm, ncols, dt in [
        ("wqh0", 2048, BF16), ("wqh1", 2048, BF16),
        ("wkh0", 2048, BF16), ("wkh1", 2048, BF16),
        ("wvg0", 4096, BF16), ("wvg1", 4096, BF16),
        ("vp0", 16 * 129, BF16), ("vp1", 16 * 129, BF16),
        ("qp0", TQ, BF16), ("qp1", TQ, BF16),
        ("kp0", S, BF16), ("kp1", S, BF16),
        ("pt0", TQ, BF16), ("pt1", TQ, BF16), ("pt2", TQ, BF16),
        ("qktmp", S, BF16), ("r3", S, BF16),
        ("ctxq", 256, BF16), ("rcol", 8, F32),
        ("rows", 1024, F32), ("rows2", 1024, F32), ("rsc", 512, F32),
        ("sq0", 512, BF16), ("sq1", 512, BF16), ("sq2", 512, BF16),
    ]:
        o = place(nm, ncols, dt, o)
    W12_END = o
    o = REG_W
    for nm, ncols, dt in [
        ("gated", NCK * 512, BF16),
        ("wch0", 2048, BF16), ("wch1", 2048, BF16),
        ("wgh0", 2048, BF16), ("wgh1", 2048, BF16),
        ("wuh0", 2048, BF16), ("wuh1", 2048, BF16),
        ("xqs0", TQ, F32), ("xqs1", TQ, F32),
        ("fout0", 512, F32), ("fout1", 512, F32),
        ("ytmp0", 512, F32), ("ytmp1", 512, F32),
        ("wbce0", 512, BF16), ("wbce1", 512, BF16),
        ("wer", TQ, BF16), ("lrw", TQ, BF16),
        ("logsb", 1024, F32), ("tk", 64, F32), ("wtk", 8, BF16),
        ("w8sb", TQ, BF16), ("sil", TQ, BF16),
    ]:
        o = place(nm, ncols, dt, o)
    W36_END = o
    assert max(W12_END, W36_END) <= SBUF_LIMIT, (W12_END, W36_END)

    PS = {}
    psmeta = {}
    psoff = [0]

    def psalloc(name, ncols, dt):
        t = nc.alloc_psum_tensor("ps_" + name, [128, ncols], dt)
        PS[name] = t
        esz = 2 if dt == BF16 else 4
        psmeta[name] = (psoff[0], esz)
        psoff[0] += 2048 * ((ncols * esz + 2047) // 2048)

    psalloc("a", 512, F32)
    psalloc("b", 512, F32)
    psalloc("s0", 512, F32)
    psalloc("s1", 512, F32)
    psalloc("c0", 512, F32)
    psalloc("c1", 512, F32)
    psalloc("t", 128, BF16)
    psalloc("t2", 512, F32)
    assert psoff[0] <= 8 * 2048, psoff[0]

    def rg(name, c0=None, c1=None):
        off, esz, ncols = meta[name]
        if c0 is None:
            c0, c1 = 0, ncols
        return ("sbuf", off + c0 * esz, off + c1 * esz)

    def prg(name, c0=None, c1=None):
        off, esz = psmeta[name]
        if c0 is None:
            return ("psum", off, off + 2048)
        return ("psum", off + c0 * esz, off + c1 * esz)

    P = Prog()
    em = P.emit

    def materialize():
        P.finalize()
        import contextlib
        with contextlib.ExitStack() as stack:
            sems = {}
            for c in P.chans:
                sems[c] = stack.enter_context(nc.semaphore("sem_" + c.replace(":", "_")))
            block = stack.enter_context(nc.Block())

            def runner(eng):
                def run(h):
                    for op in P.eops[eng]:
                        for (se, v) in op.waits:
                            h.wait_ge(sems[se], v)
                        inst = op.fn(h)
                        if op.inc:
                            inst.then_inc(sems[op.chan], 16 if op.dma else 1)
                return run

            block.sync(runner("sync"))
            block.scalar(runner("scalar"))
            block.vector(runner("vector"))
            block.tensor(runner("tensor"))
            block.gpsimd(runner("gpsimd"))
        return nc

    def stop_after(k):
        if phase_limit > k:
            return False
        em("gpsimd", lambda h: h.dma_start(out=y_out[0:128, :], in_=A["mbc"][:, 0:TQ]),
           reads=[rg("mbc", 0, TQ)], writes=[("yout", 0, 512)], dma=True, tag="stub_out")
        em("gpsimd", lambda h: h.memset(A["joinpad"][:, 0:1], 0.0),
           reads=[("yout", 0, 512)], writes=[rg("joinpad")], tag="join")
        return True

    for nm, src, np_ in [("tri", tri_d.ap(), 128), ("id128", id128_d.ap(), 128),
                         ("eer", eer_d.ap(), 8), ("e8s", e8s_d.ap(), 8)]:
        em("sync", lambda h, nm=nm, src=src, np_=np_: h.dma_start(out=A[nm][0:np_, :], in_=src),
           writes=[rg(nm)], dma=True, tag="ld_" + nm)
    em("sync", lambda h: h.dma_start(out=A["id8f"][0:8, :], in_=id8f_d.ap()),
       writes=[rg("id8f")], dma=True, tag="ld_id8f")
    for nm, src in [("bqc", bqd), ("bkc", bkd), ("boc", bod), ("mcol", mcold)]:
        em("sync", lambda h, nm=nm, src=src: h.dma_start(out=A[nm][:], in_=src[:, :]),
           writes=[rg(nm)], dma=True, tag="ld_" + nm)
    em("sync", lambda h: h.dma_start(out=A["rows"][0:1, :], in_=bvd[0:1, 0:1024]),
       writes=[rg("rows")], dma=True, tag="ld_bv0")
    em("sync", lambda h: h.dma_start(out=A["rows2"][0:1, :], in_=bvd[0:1, 1024:2048]),
       writes=[rg("rows2")], dma=True, tag="ld_bv1")
    for nm, src in [("cosk", coskd), ("sink", sinkd), ("Gb", Gd)]:
        em("gpsimd", lambda h, nm=nm, src=src: h.dma_start(out=A[nm][:], in_=src[:, :]),
           writes=[rg(nm)], dma=True, tag="ld_" + nm)
    em("gpsimd", lambda h: h.dma_start(out=A["lATb"][:], in_=lATd[:, :]),
       writes=[rg("lATb")], dma=True, tag="ld_lAT")
    em("gpsimd", lambda h: h.dma_start(out=A["lBTb"][:], in_=lBTd[:, :]),
       writes=[rg("lBTb")], dma=True, tag="ld_lBT")
    em("vector", lambda h: h.memset(A["onecb"][:], 1.0), writes=[rg("onecb")], tag="ms1")
    em("vector", lambda h: h.memset(A["onerf"][0:1, :], 1.0), writes=[rg("onerf")], tag="ms2")
    em("vector", lambda h: h.memset(A["onerb"][0:1, :], 1.0), writes=[rg("onerb")], tag="ms3")
    for p in range(2):
        em("vector", lambda h, p=p: h.memset(A[f"vp{p}"][:, 128:16 * 129:129], 1.0),
           writes=[rg(f"vp{p}")], tag=f"ms_vp{p}")
    for g in range(4):
        src = "rows" if g < 2 else "rows2"
        sc0 = (g % 2) * 512
        ps = ("a", "b")[g % 2]
        em("tensor", lambda h, src=src, sc0=sc0, ps=ps: h.matmul(
            PS[ps][:, :], A["onerf"][0:1, :], A[src][0:1, sc0:sc0 + 512],
            start=True, stop=True),
           reads=[rg("onerf"), rg(src, sc0, sc0 + 512)], writes=[prg(ps)], tag=f"bvb{g}")
        em("scalar", lambda h, g=g, ps=ps: h.copy(
            A["bvbc"][:, g * 512:(g + 1) * 512], PS[ps][:, :]),
           reads=[prg(ps)], writes=[rg("bvbc", g * 512, (g + 1) * 512)], tag=f"bve{g}")

    if stop_after(0):
        return materialize()

    for ic in range(NCK):
        em("gpsimd", lambda h, ic=ic: h.dma_start(
            out=A["xb"][:, ic * S:(ic + 1) * S], in_=xT[ic * 128:(ic + 1) * 128, :]),
           writes=[rg("xb", ic * S, (ic + 1) * S)], dma=True, tag=f"ld_x{ic}")

    def ln_stats_center(buf, bufstride, ntok, mdst, tag, upto=3):
        nhalf = ntok // 1024
        for hf in range(max(nhalf, 1)):
            if upto < 1:
                break
            t0 = hf * 1024
            tn = min(1024, ntok - t0)
            for ic in range(NCK):
                for g in range(2):
                    if g * 512 >= tn:
                        continue
                    sqb = f"sq{(ic * 2 + g) % 3}"
                    ps = ("a", "b")[g]
                    ps2 = ("c0", "c1")[g]
                    c0 = t0 + g * 512
                    em("vector", lambda h, ic=ic, c0=c0, sqb=sqb: h.tensor_tensor(
                        A[sqb][:, 0:512],
                        A[buf][:, ic * bufstride + c0: ic * bufstride + c0 + 512],
                        A[buf][:, ic * bufstride + c0: ic * bufstride + c0 + 512], ALU.mult),
                       reads=[rg(buf, ic * bufstride + c0, ic * bufstride + c0 + 512)],
                       writes=[rg(sqb, 0, 512)], tag=f"{tag}sq{hf}{g}_{ic}")
                    em("tensor", lambda h, ic=ic, c0=c0, ps=ps: h.matmul(
                        PS[ps][0:1, :], A["onecb"][:, :],
                        A[buf][:, ic * bufstride + c0: ic * bufstride + c0 + 512],
                        start=(ic == 0), stop=(ic == NCK - 1)),
                       reads=[rg("onecb"), rg(buf, ic * bufstride + c0, ic * bufstride + c0 + 512)],
                       writes=[prg(ps, 0, 512)], tag=f"{tag}sx{hf}{g}_{ic}")
                    em("tensor", lambda h, ic=ic, sqb=sqb, ps2=ps2: h.matmul(
                        PS[ps2][0:1, :], A["onecb"][:, :], A[sqb][:, 0:512],
                        start=(ic == 0), stop=(ic == NCK - 1)),
                       reads=[rg("onecb"), rg(sqb, 0, 512)],
                       writes=[prg(ps2, 0, 512)], tag=f"{tag}s2{hf}{g}_{ic}")
            if upto < 2:
                continue
            for g in range(2):
                if g * 512 >= tn:
                    continue
                ps = ("a", "b")[g]
                ps2 = ("c0", "c1")[g]
                gc = g * 512
                em("vector", lambda h, gc=gc, ps=ps: h.tensor_scalar(
                    A["rows"][0:1, gc:gc + 512], PS[ps][0:1, :], 1.0 / H, None, ALU.mult),
                   reads=[prg(ps, 0, 512)], writes=[rg("rows", gc, gc + 512)],
                   tag=f"{tag}m{hf}{g}")
                em("vector", lambda h, gc=gc: h.tensor_tensor(
                    A["rsc"][0:1, :], A["rows"][0:1, gc:gc + 512],
                    A["rows"][0:1, gc:gc + 512], ALU.mult),
                   reads=[rg("rows", gc, gc + 512)], writes=[rg("rsc")], tag=f"{tag}ms{hf}{g}")
                em("vector", lambda h, gc=gc, ps2=ps2: h.tensor_scalar(
                    A["rows2"][0:1, gc:gc + 512], PS[ps2][0:1, :], 1.0 / H, None, ALU.mult),
                   reads=[prg(ps2, 0, 512)], writes=[rg("rows2", gc, gc + 512)],
                   tag=f"{tag}v{hf}{g}")
                em("vector", lambda h, gc=gc: h.tensor_tensor(
                    A["rows2"][0:1, gc:gc + 512], A["rows2"][0:1, gc:gc + 512],
                    A["rsc"][0:1, :], ALU.subtract),
                   reads=[rg("rows2", gc, gc + 512), rg("rsc")],
                   writes=[rg("rows2", gc, gc + 512)], tag=f"{tag}v2{hf}{g}")
                em("vector", lambda h, gc=gc: h.tensor_scalar(
                    A["rows2"][0:1, gc:gc + 512], A["rows2"][0:1, gc:gc + 512],
                    float(EPS), None, ALU.add),
                   reads=[rg("rows2", gc, gc + 512)], writes=[rg("rows2", gc, gc + 512)],
                   tag=f"{tag}v3{hf}{g}")
                em("scalar", lambda h, gc=gc: h.activation(
                    A["rsc"][0:1, :], A["rows2"][0:1, gc:gc + 512], AF.Sqrt,
                    bias=0.0, scale=1.0),
                   reads=[rg("rows2", gc, gc + 512)], writes=[rg("rsc")], tag=f"{tag}sd{hf}{g}")
                em("vector", lambda h, gc=gc: h.reciprocal(
                    A["rows2"][0:1, gc:gc + 512], A["rsc"][0:1, :]),
                   reads=[rg("rsc")], writes=[rg("rows2", gc, gc + 512)], tag=f"{tag}i{hf}{g}")
                for (src, dst) in (("rows", "mbc"), ("rows2", "ibc")):
                    ps3 = (("a", "c0"), ("b", "c1"))[g][src == "rows2"]
                    em("tensor", lambda h, gc=gc, src=src, ps3=ps3: h.matmul(
                        PS[ps3][:, :], A["onerf"][0:1, :], A[src][0:1, gc:gc + 512],
                        start=True, stop=True),
                       reads=[rg("onerf"), rg(src, gc, gc + 512)], writes=[prg(ps3)],
                       tag=f"{tag}bc{hf}{g}{dst}")
                    em("scalar", lambda h, t0=t0, gc=gc, dst=dst, ps3=ps3: h.copy(
                        A[dst][:, t0 + gc: t0 + gc + 512], PS[ps3][:, :]),
                       reads=[prg(ps3)], writes=[rg(dst, t0 + gc, t0 + gc + 512)],
                       tag=f"{tag}be{hf}{g}{dst}")
        if upto < 3:
            return
        for ic in range(NCK):
            em("vector", lambda h, ic=ic: h.tensor_tensor(
                A[buf][:, ic * bufstride: ic * bufstride + ntok],
                A[buf][:, ic * bufstride: ic * bufstride + ntok],
                A["mbc"][:, 0:ntok], ALU.subtract),
               reads=[rg(buf, ic * bufstride, ic * bufstride + ntok), rg("mbc", 0, ntok)],
               writes=[rg(buf, ic * bufstride, ic * bufstride + ntok)], tag=f"{tag}ce{ic}")
            em("vector", lambda h, ic=ic: h.tensor_tensor(
                A[buf][:, ic * bufstride: ic * bufstride + ntok],
                A[buf][:, ic * bufstride: ic * bufstride + ntok],
                A["ibc"][:, 0:ntok], ALU.mult),
               reads=[rg(buf, ic * bufstride, ic * bufstride + ntok), rg("ibc", 0, ntok)],
               writes=[rg(buf, ic * bufstride, ic * bufstride + ntok)], tag=f"{tag}cs{ic}")

    if stop_after(0.2):
        return materialize()
    ln_stats_center("xb", S, S, "mbc", "ln1",
                    upto=(0 if phase_limit <= 0.4 else 1 if phase_limit <= 0.6
                          else 2 if phase_limit <= 0.8 else 3))
    if "xb" in dbg:
        em("sync", lambda h: h.dma_start(out=dbg["xb"][:, :], in_=A["xb"][:, :]),
           reads=[rg("xb")], dma=True, tag="dbg_xb")

    if stop_after(1):
        return materialize()

    def load_head_weights(hd):
        p = hd % 2
        em("gpsimd", lambda h, hd=hd, p=p: h.dma_start(
            out=A[f"wqh{p}"][:, :], in_=wq_img[hd * 128:(hd + 1) * 128, :]),
           writes=[rg(f"wqh{p}")], dma=True, tag=f"ldwq{hd}")
        em("gpsimd", lambda h, hd=hd, p=p: h.dma_start(
            out=A[f"wkh{p}"][:, :], in_=wk_img[hd * 128:(hd + 1) * 128, :]),
           writes=[rg(f"wkh{p}")], dma=True, tag=f"ldwk{hd}")
        if hd % 2 == 0:
            g = hd // 2
            pv = g % 2
            em("gpsimd", lambda h, g=g, pv=pv: h.dma_start(
                out=A[f"wvg{pv}"][:, :], in_=wv_img[g * 128:(g + 1) * 128, :]),
               writes=[rg(f"wvg{pv}")], dma=True, tag=f"ldwv{g}")

    PROT = [0]
    PEND_EVAC = [None]
    CTX_ROT = [0]

    def proj_qk(hd, kind):
        p = hd % 2
        wb = f"wqh{p}" if kind == "q" else f"wkh{p}"
        dst = f"qp{p}" if kind == "q" else f"kp{p}"
        bias = "bqc" if kind == "q" else "bkc"
        ntt = 2 if kind == "q" else 4
        n = ntt * 512
        for tt in range(ntt):
            ps = ("a", "b")[(PROT[0] + tt) % 2]
            for ic in range(NCK):
                em("tensor", lambda h, ic=ic, tt=tt, wb=wb, ps=ps: h.matmul(
                    PS[ps][:, :], A[wb][:, ic * 128:(ic + 1) * 128],
                    A["xb"][:, ic * S + tt * 512: ic * S + (tt + 1) * 512],
                    start=(ic == 0), stop=(ic == NCK - 1)),
                   reads=[rg(wb, ic * 128, (ic + 1) * 128),
                          rg("xb", ic * S + tt * 512, ic * S + (tt + 1) * 512)],
                   writes=[prg(ps)], tag=f"{kind}mm{hd}_{tt}_{ic}")
            em("vector", lambda h, tt=tt, hd=hd, bias=bias, ps=ps: h.tensor_scalar(
                A["qktmp"][:, tt * 512:(tt + 1) * 512], PS[ps][:, :],
                A[bias][:, hd:hd + 1], None, ALU.add),
               reads=[prg(ps), rg(bias, hd, hd + 1)],
               writes=[rg("qktmp", tt * 512, (tt + 1) * 512)], tag=f"{kind}ev{hd}_{tt}")
        PROT[0] = (PROT[0] + ntt) % 2
        em("vector", lambda h, dst=dst, n=n: h.tensor_tensor(
            A[dst][:, 0:n], A["qktmp"][:, 0:n], A["cosk"][:, 0:n], ALU.mult),
           reads=[rg("qktmp", 0, n), rg("cosk", 0, n)], writes=[rg(dst, 0, n)],
           tag=f"{kind}rc{hd}")
        em("vector", lambda h, n=n: h.tensor_tensor(
            A["r3"][0:64, 0:n], A["qktmp"][64:128, 0:n], A["sink"][64:128, 0:n], ALU.mult),
           reads=[rg("qktmp", 0, n), rg("sink", 0, n)], writes=[rg("r3", 0, n)],
           tag=f"{kind}rs1{hd}")
        em("vector", lambda h, n=n: h.tensor_tensor(
            A["r3"][64:128, 0:n], A["qktmp"][0:64, 0:n], A["sink"][0:64, 0:n], ALU.mult),
           reads=[rg("qktmp", 0, n), rg("sink", 0, n)], writes=[rg("r3", 0, n)],
           tag=f"{kind}rs2{hd}")
        em("vector", lambda h, dst=dst, n=n: h.tensor_tensor(
            A[dst][:, 0:n], A[dst][:, 0:n], A["r3"][:, 0:n], ALU.add),
           reads=[rg(dst, 0, n), rg("r3", 0, n)], writes=[rg(dst, 0, n)],
           tag=f"{kind}ra{hd}")

    def proj_v(hd):
        g = hd // 2
        pv = g % 2
        for tc in range(NCK):
            ps = ("a", "b")[(PROT[0] + tc) % 2]
            for ic in range(NCK):
                em("tensor", lambda h, ic=ic, tc=tc, pv=pv, ps=ps: h.matmul(
                    PS[ps][:, 0:256], A["xb"][:, ic * S + tc * 128: ic * S + (tc + 1) * 128],
                    A[f"wvg{pv}"][:, ic * 256:(ic + 1) * 256],
                    start=(ic == 0), stop=(ic == NCK - 1)),
                   reads=[rg("xb", ic * S + tc * 128, ic * S + (tc + 1) * 128),
                          rg(f"wvg{pv}", ic * 256, (ic + 1) * 256)],
                   writes=[prg(ps, 0, 256)], tag=f"vmm{g}_{tc}_{ic}")
            for sub in range(2):
                hh = g * 2 + sub
                pw = hh % 2
                em("vector", lambda h, tc=tc, sub=sub, hh=hh, pw=pw, ps=ps: h.tensor_tensor(
                    A[f"vp{pw}"][:, tc * 129: tc * 129 + 128],
                    PS[ps][:, sub * 128:(sub + 1) * 128],
                    A["bvbc"][:, hh * 128:(hh + 1) * 128], ALU.add),
                   reads=[prg(ps, sub * 128, (sub + 1) * 128),
                          rg("bvbc", hh * 128, (hh + 1) * 128)],
                   writes=[rg(f"vp{pw}", tc * 129, tc * 129 + 128)],
                   tag=f"vev{hh}_{tc}")
        PROT[0] = (PROT[0] + NCK) % 2

    def attn(hd):
        p = hd % 2
        tilectr = [0]
        CTXB = ("c0", "c1", "t2")
        for grp in range(4):
            gb = (CTX_ROT[0] + 2 * grp) % 3
            ctx_slot = [(CTXB[gb], 0), (CTXB[(gb + 1) % 3], 0)]
            slots = list(range(grp * 2, grp * 2 + 2))
            kcs = [kc for kc in range(16) if (kc % 8) <= slots[-1]]
            pend = []
            kcount = 0
            for kc in kcs:
                j0 = max(kc % 8, slots[0])
                if j0 > slots[-1]:
                    continue
                n = (slots[-1] - j0 + 1) * 128
                q0 = j0 * 128
                sl = tilectr[0] % 2
                tilectr[0] += 1
                pp = f"pt{sl}"
                ps, so = (("s0", 0), ("s1", 0))[sl]
                w = n
                em("tensor", lambda h, kc=kc, q0=q0, w=w, p=p, ps=ps, so=so: h.matmul(
                    PS[ps][:, so:so + w], A[f"kp{p}"][:, kc * 128:(kc + 1) * 128],
                    A[f"qp{p}"][:, q0: q0 + w], start=True, stop=True),
                   reads=[rg(f"kp{p}", kc * 128, (kc + 1) * 128),
                          rg(f"qp{p}", q0, q0 + w)],
                   writes=[prg(ps, so, so + w)], tag=f"sc{hd}_{kc}")
                if kc >= 8 and j0 == kc - 8:
                    em("scalar", lambda h, ps=ps, pp=pp, q0=q0, so=so: h.activation(
                        A[pp][:, q0: q0 + 128], PS[ps][:, so:so + 128], AF.Exp,
                        bias=A["mcol"][:, 0:1], scale=RSQ),
                       reads=[prg(ps, so, so + 128), rg("mcol")],
                       writes=[rg(pp, q0, q0 + 128)], tag=f"exm{hd}_{kc}")
                    if w > 128:
                        em("scalar", lambda h, ps=ps, pp=pp, q0=q0, w=w, so=so: h.activation(
                            A[pp][:, q0 + 128: q0 + w], PS[ps][:, so + 128:so + w], AF.Exp,
                            bias=0.0, scale=RSQ),
                           reads=[prg(ps, so + 128, so + w)],
                           writes=[rg(pp, q0 + 128, q0 + w)], tag=f"exr{hd}_{kc}")
                else:
                    em("scalar", lambda h, ps=ps, pp=pp, q0=q0, w=w, so=so: h.activation(
                        A[pp][:, q0: q0 + w], PS[ps][:, so:so + w], AF.Exp,
                        bias=0.0, scale=RSQ),
                       reads=[prg(ps, so, so + w)], writes=[rg(pp, q0, q0 + w)],
                       tag=f"ex{hd}_{kc}")
                if kc < 8 and j0 == kc:
                    em("vector", lambda h, pp=pp, q0=q0: h.tensor_tensor(
                        A[pp][:, q0: q0 + 128], A[pp][:, q0: q0 + 128],
                        A["tri"][:, :], ALU.mult),
                       reads=[rg(pp, q0, q0 + 128), rg("tri")],
                       writes=[rg(pp, q0, q0 + 128)], tag=f"tri{hd}_{kc}")

                def ctx_emit(kc=kc, j0=j0, w=w, pp=pp):
                    for j in range(j0, j0 + w // 128):
                        bank, boff = ctx_slot[j % 2]
                        em("tensor", lambda h, j=j, kc=kc, p=p, pp=pp, bank=bank,
                           boff=boff: h.matmul(
                            PS[bank][:, boff:boff + 129], A[pp][:, j * 128:(j + 1) * 128],
                            A[f"vp{p}"][:, kc * 129:(kc + 1) * 129],
                            start=(kc == 0), stop=(kc == 8 + j)),
                           reads=[rg(pp, j * 128, (j + 1) * 128),
                                  rg(f"vp{p}", kc * 129, (kc + 1) * 129)],
                           writes=[prg(bank, boff, boff + 129)], tag=f"cx{hd}_{kc}_{j}")
                kcount += 1
                if kcount == 2 and PEND_EVAC[0] is not None:
                    PEND_EVAC[0]()
                    PEND_EVAC[0] = None
                pend.append(ctx_emit)
                if len(pend) > 1:
                    pend.pop(0)()
            for f in pend:
                f()
            if PEND_EVAC[0] is not None:
                PEND_EVAC[0]()
                PEND_EVAC[0] = None

            def grp_evac(slots=slots, hd=hd, ctx_slot=tuple(ctx_slot)):
                for j in slots:
                    bank, boff = ctx_slot[j % 2]
                    em("vector", lambda h, bank=bank, boff=boff: h.reciprocal(
                        A["rcol"][:, 0:1], PS[bank][:, boff + 128:boff + 129]),
                       reads=[prg(bank, boff + 128, boff + 129)], writes=[rg("rcol", 0, 1)],
                       tag=f"rc{hd}_{j}")
                    em("vector", lambda h, bank=bank, boff=boff, j=j: h.tensor_scalar(
                        A["ctxq"][:, (j % 2) * 128:(j % 2) * 128 + 128],
                        PS[bank][:, boff:boff + 128], A["rcol"][:, 0:1], None, ALU.mult),
                       reads=[prg(bank, boff, boff + 128), rg("rcol", 0, 1)],
                       writes=[rg("ctxq", (j % 2) * 128, (j % 2) * 128 + 128)],
                       tag=f"nm{hd}_{j}")
                    em("tensor", lambda h, j=j: h.transpose(
                        PS["t"][:, 0:128], A["ctxq"][:, (j % 2) * 128:(j % 2) * 128 + 128],
                        A["id128"][:, :]),
                       reads=[rg("ctxq", (j % 2) * 128, (j % 2) * 128 + 128), rg("id128")],
                       writes=[prg("t")], tag=f"tp{hd}_{j}")
                    em("scalar", lambda h, hd=hd, j=j: h.copy(
                        A["ctxsb"][:, hd * TQ + j * 128: hd * TQ + (j + 1) * 128],
                        PS["t"][:, 0:128]),
                       reads=[prg("t")],
                       writes=[rg("ctxsb", hd * TQ + j * 128, hd * TQ + (j + 1) * 128)],
                       tag=f"cte{hd}_{j}")
            PEND_EVAC[0] = grp_evac

    load_head_weights(0)
    load_head_weights(1)
    proj_qk(0, "q")
    proj_qk(0, "k")
    for hd in range(NH):
        if hd + 2 < NH:
            load_head_weights(hd + 2)
        if hd + 1 < NH:
            proj_qk(hd + 1, "q")
            proj_qk(hd + 1, "k")
        if hd % 2 == 0:
            proj_v(hd)
        attn(hd)
        CTX_ROT[0] = (CTX_ROT[0] + 8) % 3
        if hd == 0:
            if "qp" in dbg:
                em("sync", lambda h: h.dma_start(out=dbg["qp"][:, :], in_=A["qp0"][:, :]),
                   reads=[rg("qp0")], dma=True, tag="dbg_qp")
            if "kp" in dbg:
                em("sync", lambda h: h.dma_start(out=dbg["kp"][:, :], in_=A["kp0"][:, :]),
                   reads=[rg("kp0")], dma=True, tag="dbg_kp")
            if "vp" in dbg:
                em("sync", lambda h: h.dma_start(out=dbg["vp"][:, :], in_=A["vp0"][:, :]),
                   reads=[rg("vp0")], dma=True, tag="dbg_vp")
    PEND_EVAC[0]()
    PEND_EVAC[0] = None
    if "ctx" in dbg:
        for ic in range(NCK):
            em("sync", lambda h, ic=ic: h.dma_start(
                out=dbg["ctx"][ic * 128:(ic + 1) * 128, :],
                in_=A["ctxsb"][:, ic * TQ:(ic + 1) * TQ]),
               reads=[rg("ctxsb", ic * TQ, (ic + 1) * TQ)], dma=True, tag=f"dbg_ctx{ic}")

    if stop_after(2):
        return materialize()

    def load_band(dst, dram, band, tagp):
        em("gpsimd", lambda h, dst=dst, dram=dram, band=band: h.dma_start(
            out=A[dst][:, :], in_=dram[band * 128:(band + 1) * 128, :]),
           writes=[rg(dst)], dma=True, tag=tagp)

    load_band("wch0", wo_img, 0, "ldwo0")
    for hoc in range(NCK):
        pw = hoc % 2
        if hoc + 1 < NCK:
            load_band(f"wch{(hoc + 1) % 2}", wo_img, hoc + 1, f"ldwo{hoc + 1}")
        em("sync", lambda h, hoc=hoc, pw=pw: h.dma_start(
            out=A[f"xqs{pw}"][:, :], in_=xT[hoc * 128:(hoc + 1) * 128, 0:TQ]),
           writes=[rg(f"xqs{pw}")], dma=True, tag=f"ldxq{hoc}")
        for tt in range(2):
            ps = ("a", "b")[tt]
            pf = tt
            for ic in range(NCK):
                em("tensor", lambda h, ic=ic, tt=tt, pw=pw, ps=ps: h.matmul(
                    PS[ps][:, :], A[f"wch{pw}"][:, ic * 128:(ic + 1) * 128],
                    A["ctxsb"][:, ic * TQ + tt * 512: ic * TQ + (tt + 1) * 512],
                    start=(ic == 0), stop=(ic == NCK - 1)),
                   reads=[rg(f"wch{pw}", ic * 128, (ic + 1) * 128),
                          rg("ctxsb", ic * TQ + tt * 512, ic * TQ + (tt + 1) * 512)],
                   writes=[prg(ps)], tag=f"omm{hoc}_{tt}_{ic}")
            em("scalar", lambda h, hoc=hoc, pf=pf, ps=ps: h.activation(
                A[f"ytmp{pf}"][:, :], PS[ps][:, :], AF.Identity,
                bias=A["boc"][:, hoc:hoc + 1], scale=1.0),
               reads=[prg(ps), rg("boc", hoc, hoc + 1)],
               writes=[rg(f"ytmp{pf}")], tag=f"oev{hoc}_{tt}")
            em("vector", lambda h, hoc=hoc, tt=tt, pw=pw, pf=pf: h.tensor_tensor(
                A["y1"][:, hoc * TQ + tt * 512: hoc * TQ + (tt + 1) * 512],
                A[f"ytmp{pf}"][:, :], A[f"xqs{pw}"][:, tt * 512:(tt + 1) * 512], ALU.add),
               reads=[rg(f"ytmp{pf}"), rg(f"xqs{pw}", tt * 512, (tt + 1) * 512)],
               writes=[rg("y1", hoc * TQ + tt * 512, hoc * TQ + (tt + 1) * 512)],
               tag=f"ores{hoc}_{tt}")
    if "y1" in dbg:
        for ic in range(NCK):
            em("sync", lambda h, ic=ic: h.dma_start(
                out=dbg["y1"][ic * 128:(ic + 1) * 128, :],
                in_=A["y1"][:, ic * TQ:(ic + 1) * TQ]),
               reads=[rg("y1", ic * TQ, (ic + 1) * TQ)], dma=True, tag=f"dbg_y1{ic}")

    if stop_after(3):
        return materialize()

    for ic in range(NCK):
        em("vector", lambda h, ic=ic: h.tensor_copy(
            A["ln2"][:, ic * TQ:(ic + 1) * TQ], A["y1"][:, ic * TQ:(ic + 1) * TQ]),
           reads=[rg("y1", ic * TQ, (ic + 1) * TQ)],
           writes=[rg("ln2", ic * TQ, (ic + 1) * TQ)], tag=f"l2c{ic}")
    ln_stats_center("ln2", TQ, TQ, "mbc", "ln2")
    if "ln2" in dbg:
        em("sync", lambda h: h.dma_start(out=dbg["ln2"][:, :], in_=A["ln2"][:, :]),
           reads=[rg("ln2")], dma=True, tag="dbg_ln2")

    if stop_after(4):
        return materialize()

    TK = A["tk"]
    for tt in range(2):
        ps = ("a", "b")[tt]
        for ic in range(NCK):
            em("tensor", lambda h, ic=ic, tt=tt, ps=ps: h.matmul(
                PS[ps][0:8, :], A["Gb"][:, ic * 8:(ic + 1) * 8],
                A["ln2"][:, ic * TQ + tt * 512: ic * TQ + (tt + 1) * 512],
                start=(ic == 0), stop=(ic == NCK - 1)),
               reads=[rg("Gb", ic * 8, (ic + 1) * 8),
                      rg("ln2", ic * TQ + tt * 512, ic * TQ + (tt + 1) * 512)],
               writes=[prg(ps, 0, 512)], tag=f"lg{tt}_{ic}")
        em("scalar", lambda h, tt=tt, ps=ps: h.copy(
            A["logsb"][0:8, tt * 512:(tt + 1) * 512], PS[ps][0:8, :]),
           reads=[prg(ps, 0, 512)], writes=[rg("logsb", tt * 512, (tt + 1) * 512)],
           tag=f"lge{tt}")

    def router_tg(tg):
        em("tensor", lambda h, tg=tg: h.transpose(
            PS["t2"][:, 0:8], A["logsb"][0:8, tg * 128:(tg + 1) * 128], A["id8f"][0:8, :]),
           reads=[rg("logsb", tg * 128, (tg + 1) * 128), rg("id8f")],
           writes=[prg("t2", 0, 8)], tag=f"rtp{tg}")
        em("vector", lambda h: h.tensor_copy(TK[:, 0:8], PS["t2"][:, 0:8]),
           reads=[prg("t2", 0, 8)], writes=[rg("tk", 0, 8)], tag=f"rtc{tg}")
        em("vector", lambda h: h.reduce_max(TK[:, 16:17], TK[:, 0:8], axis=AX.X,
                                            negate=True),
           reads=[rg("tk", 0, 8)], writes=[rg("tk", 16, 17)], tag=f"rmx{tg}")
        em("scalar", lambda h: h.activation(TK[:, 8:16], TK[:, 0:8], AF.Exp,
                                            bias=TK[:, 16:17], scale=1.0),
           reads=[rg("tk", 0, 8), rg("tk", 16, 17)], writes=[rg("tk", 8, 16)],
           tag=f"rex{tg}")
        em("vector", lambda h: h.reduce_sum(TK[:, 17:18], TK[:, 8:16], axis=AX.X),
           reads=[rg("tk", 8, 16)], writes=[rg("tk", 17, 18)], tag=f"rsm{tg}")
        em("vector", lambda h: h.reciprocal(TK[:, 18:19], TK[:, 17:18]),
           reads=[rg("tk", 17, 18)], writes=[rg("tk", 18, 19)], tag=f"rrc{tg}")
        em("vector", lambda h: h.tensor_scalar(TK[:, 24:32], TK[:, 8:16],
                                               TK[:, 18:19], None, ALU.mult),
           reads=[rg("tk", 8, 16), rg("tk", 18, 19)], writes=[rg("tk", 24, 32)],
           tag=f"rp{tg}")
        em("vector", lambda h: h.reduce_max(TK[:, 19:20], TK[:, 24:32], axis=AX.X),
           reads=[rg("tk", 24, 32)], writes=[rg("tk", 19, 20)], tag=f"rpm{tg}")
        em("vector", lambda h: h.tensor_scalar(TK[:, 32:40], TK[:, 24:32],
                                               TK[:, 19:20], None, ALU.is_lt),
           reads=[rg("tk", 24, 32), rg("tk", 19, 20)], writes=[rg("tk", 32, 40)],
           tag=f"rlt{tg}")
        em("vector", lambda h: h.tensor_tensor(TK[:, 32:40], TK[:, 32:40],
                                               TK[:, 24:32], ALU.mult),
           reads=[rg("tk", 32, 40), rg("tk", 24, 32)], writes=[rg("tk", 32, 40)],
           tag=f"rl2{tg}")
        em("vector", lambda h: h.reduce_max(TK[:, 20:21], TK[:, 32:40], axis=AX.X),
           reads=[rg("tk", 32, 40)], writes=[rg("tk", 20, 21)], tag=f"rsc{tg}")
        em("vector", lambda h: h.tensor_scalar(TK[:, 40:48], TK[:, 24:32],
                                               TK[:, 20:21], None, ALU.is_ge),
           reads=[rg("tk", 24, 32), rg("tk", 20, 21)], writes=[rg("tk", 40, 48)],
           tag=f"rge{tg}")
        em("vector", lambda h: h.tensor_tensor(TK[:, 48:56], TK[:, 40:48],
                                               TK[:, 24:32], ALU.mult),
           reads=[rg("tk", 40, 48), rg("tk", 24, 32)], writes=[rg("tk", 48, 56)],
           tag=f"rw{tg}")
        em("vector", lambda h: h.tensor_copy(A["wtk"][:, 0:8], TK[:, 48:56]),
           reads=[rg("tk", 48, 56)], writes=[rg("wtk")], tag=f"rwb{tg}")

    def router_tg2(tg):
        em("tensor", lambda h: h.transpose(PS["t"][0:8, 0:128], A["wtk"][:, 0:8],
                                           A["id128"][:, :]),
           reads=[rg("wtk"), rg("id128")], writes=[prg("t")], tag=f"rtb{tg}")
        em("scalar", lambda h, tg=tg: h.copy(
            A["w8sb"][0:8, tg * 128:(tg + 1) * 128], PS["t"][0:8, 0:128]),
           reads=[prg("t")], writes=[rg("w8sb", tg * 128, (tg + 1) * 128)],
           tag=f"rwe{tg}")

    def load_fband(dst, dram, band, tagp):
        em("gpsimd", lambda h, dst=dst, dram=dram, band=band: h.dma_start(
            out=A[dst][:, :], in_=dram[band * 128:(band + 1) * 128, :]),
           writes=[rg(dst)], dma=True, tag=tagp)

    for tt in range(2):
        load_fband("wgh0", g_img, 0, f"ldg{tt}_0")
        load_fband("wuh0", u_img, 0, f"ldu{tt}_0")
        for efc in range(NCK):
            pw = efc % 2
            if efc + 1 < NCK:
                load_fband(f"wgh{(efc + 1) % 2}", g_img, efc + 1, f"ldg{tt}_{efc + 1}")
                load_fband(f"wuh{(efc + 1) % 2}", u_img, efc + 1, f"ldu{tt}_{efc + 1}")
            psg = ("a", "b")[efc % 2]
            psu = ("c0", "c1")[efc % 2]
            for ic in range(NCK):
                em("tensor", lambda h, ic=ic, tt=tt, pw=pw, psg=psg: h.matmul(
                    PS[psg][:, :], A[f"wgh{pw}"][:, ic * 128:(ic + 1) * 128],
                    A["ln2"][:, ic * TQ + tt * 512: ic * TQ + (tt + 1) * 512],
                    start=(ic == 0), stop=(ic == NCK - 1)),
                   reads=[rg(f"wgh{pw}", ic * 128, (ic + 1) * 128),
                          rg("ln2", ic * TQ + tt * 512, ic * TQ + (tt + 1) * 512)],
                   writes=[prg(psg)], tag=f"gmm{tt}_{efc}_{ic}")
                em("tensor", lambda h, ic=ic, tt=tt, pw=pw, psu=psu: h.matmul(
                    PS[psu][:, :], A[f"wuh{pw}"][:, ic * 128:(ic + 1) * 128],
                    A["ln2"][:, ic * TQ + tt * 512: ic * TQ + (tt + 1) * 512],
                    start=(ic == 0), stop=(ic == NCK - 1)),
                   reads=[rg(f"wuh{pw}", ic * 128, (ic + 1) * 128),
                          rg("ln2", ic * TQ + tt * 512, ic * TQ + (tt + 1) * 512)],
                   writes=[prg(psu)], tag=f"umm{tt}_{efc}_{ic}")
            sc0 = (efc % 2) * 512
            em("scalar", lambda h, sc0=sc0, psg=psg: h.activation(
                A["sil"][:, sc0:sc0 + 512], PS[psg][:, :], AF.Silu, bias=0.0, scale=1.0),
               reads=[prg(psg)], writes=[rg("sil", sc0, sc0 + 512)], tag=f"sil{tt}_{efc}")
            em("vector", lambda h, efc=efc, sc0=sc0, psu=psu: h.tensor_tensor(
                A["gated"][:, efc * 512:(efc + 1) * 512], PS[psu][:, :],
                A["sil"][:, sc0:sc0 + 512], ALU.mult),
               reads=[prg(psu), rg("sil", sc0, sc0 + 512)],
               writes=[rg("gated", efc * 512, (efc + 1) * 512)], tag=f"gu{tt}_{efc}")
            if tt == 0 and efc % 2 == 0:
                router_tg(efc // 2)
            if tt == 0 and efc % 2 == 1:
                router_tg2(efc // 2)
        if tt == 0:
            for g in range(2):
                ps = ("c0", "c1")[g]
                em("tensor", lambda h, g=g, ps=ps: h.matmul(
                    PS[ps][:, :], A["eer"][0:8, :], A["w8sb"][0:8, g * 512:(g + 1) * 512],
                    start=True, stop=True),
                   reads=[rg("eer"), rg("w8sb", g * 512, (g + 1) * 512)], writes=[prg(ps)],
                   tag=f"wer{g}")
                em("scalar", lambda h, g=g, ps=ps: h.copy(
                    A["wer"][:, g * 512:(g + 1) * 512], PS[ps][:, :]),
                   reads=[prg(ps)], writes=[rg("wer", g * 512, (g + 1) * 512)], tag=f"were{g}")
            for ltt in range(2):
                ps = ("a", "b")[ltt]
                for ic in range(NCK):
                    em("tensor", lambda h, ic=ic, ltt=ltt, ps=ps: h.matmul(
                        PS[ps][:, :], A["lATb"][:, ic * 128:(ic + 1) * 128],
                        A["ln2"][:, ic * TQ + ltt * 512: ic * TQ + (ltt + 1) * 512],
                        start=(ic == 0), stop=(ic == NCK - 1)),
                       reads=[rg("lATb", ic * 128, (ic + 1) * 128),
                              rg("ln2", ic * TQ + ltt * 512, ic * TQ + (ltt + 1) * 512)],
                       writes=[prg(ps)], tag=f"lrmm{ltt}_{ic}")
                em("vector", lambda h, ltt=ltt, ps=ps: h.tensor_tensor(
                    A["lrw"][:, ltt * 512:(ltt + 1) * 512], PS[ps][:, :],
                    A["wer"][:, ltt * 512:(ltt + 1) * 512], ALU.mult),
                   reads=[prg(ps), rg("wer", ltt * 512, (ltt + 1) * 512)],
                   writes=[rg("lrw", ltt * 512, (ltt + 1) * 512)], tag=f"lrw{ltt}")
            if "w8" in dbg:
                em("sync", lambda h: h.dma_start(out=dbg["w8"][:, :], in_=A["w8sb"][0:8, :]),
                   reads=[rg("w8sb")], dma=True, tag="dbg_w8")
            if "lrw" in dbg:
                em("sync", lambda h: h.dma_start(out=dbg["lrw"][:, :], in_=A["lrw"][:, :]),
                   reads=[rg("lrw")], dma=True, tag="dbg_lrw")
        for e_ in range(8):
            pb = e_ % 2
            em("tensor", lambda h, e_=e_, tt=tt: h.matmul(
                PS["t2"][:, :], A["e8s"][0:8, e_ * 128:(e_ + 1) * 128],
                A["w8sb"][0:8, tt * 512:(tt + 1) * 512], start=True, stop=True),
               reads=[rg("e8s", e_ * 128, (e_ + 1) * 128),
                      rg("w8sb", tt * 512, (tt + 1) * 512)],
               writes=[prg("t2")], tag=f"wbc{tt}_{e_}")
            em("scalar", lambda h, pb=pb: h.copy(A[f"wbce{pb}"][:, :], PS["t2"][:, 0:512]),
               reads=[prg("t2")], writes=[rg(f"wbce{pb}")], tag=f"wbe{tt}_{e_}")
            for sub in range(2):
                efc = e_ * 2 + sub
                em("vector", lambda h, efc=efc, pb=pb: h.tensor_tensor(
                    A["gated"][:, efc * 512:(efc + 1) * 512],
                    A["gated"][:, efc * 512:(efc + 1) * 512],
                    A[f"wbce{pb}"][:, :], ALU.mult),
                   reads=[rg("gated", efc * 512, (efc + 1) * 512), rg(f"wbce{pb}")],
                   writes=[rg("gated", efc * 512, (efc + 1) * 512)], tag=f"gw{tt}_{efc}")
        if "gated" in dbg and tt == 0:
            em("sync", lambda h: h.dma_start(out=dbg["gated"][:, :], in_=A["gated"][:, :]),
               reads=[rg("gated")], dma=True, tag="dbg_gated")
        load_fband("wch0", d_img, 0, f"ldd{tt}_0")
        for hoc in range(NCK):
            pw = hoc % 2
            if hoc + 1 < NCK:
                load_fband(f"wch{(hoc + 1) % 2}", d_img, hoc + 1, f"ldd{tt}_{hoc + 1}")
            ps = ("a", "b", "s0", "s1")[hoc % 4]
            pf = hoc % 2
            for efc in range(NCK):
                em("tensor", lambda h, efc=efc, pw=pw, ps=ps: h.matmul(
                    PS[ps][:, :], A[f"wch{pw}"][:, efc * 128:(efc + 1) * 128],
                    A["gated"][:, efc * 512:(efc + 1) * 512],
                    start=(efc == 0), stop=False),
                   reads=[rg(f"wch{pw}", efc * 128, (efc + 1) * 128),
                          rg("gated", efc * 512, (efc + 1) * 512)],
                   writes=[prg(ps)], tag=f"dmm{tt}_{hoc}_{efc}")
            em("tensor", lambda h, hoc=hoc, tt=tt, ps=ps: h.matmul(
                PS[ps][:, :], A["lBTb"][:, hoc * 128:(hoc + 1) * 128],
                A["lrw"][:, tt * 512:(tt + 1) * 512], start=False, stop=True),
               reads=[rg("lBTb", hoc * 128, (hoc + 1) * 128),
                      rg("lrw", tt * 512, (tt + 1) * 512)],
               writes=[prg(ps)], tag=f"lbmm{tt}_{hoc}")
            em("vector", lambda h, hoc=hoc, tt=tt, pf=pf, ps=ps: h.tensor_tensor(
                A[f"fout{pf}"][:, :], PS[ps][:, :],
                A["y1"][:, hoc * TQ + tt * 512: hoc * TQ + (tt + 1) * 512], ALU.add),
               reads=[prg(ps), rg("y1", hoc * TQ + tt * 512, hoc * TQ + (tt + 1) * 512)],
               writes=[rg(f"fout{pf}")], tag=f"fres{tt}_{hoc}")
            em("sync", lambda h, hoc=hoc, tt=tt, pf=pf: h.dma_start(
                out=y_out[hoc * 128:(hoc + 1) * 128, tt * 512:(tt + 1) * 512],
                in_=A[f"fout{pf}"][:, :]),
               reads=[rg(f"fout{pf}")],
               writes=[("yout", (hoc * 2 + tt) * 512, (hoc * 2 + tt + 1) * 512)],
               dma=True, tag=f"st{tt}_{hoc}")

    em("gpsimd", lambda h: h.memset(A["joinpad"][:, 0:1], 0.0),
       reads=[("yout", 0, NCK * 2 * 512)], writes=[rg("joinpad")], tag="join")

    return materialize()


def _rope_tables(pos):
    j = np.arange(0, HD, 2, dtype=np.float32) / HD
    inv = (1.0 / 10000.0 ** j).astype(np.float32)
    fr = pos[:, None].astype(np.float32) * inv[None, :]
    cos = np.cos(fr).astype(np.float32)
    sin = np.sin(fr).astype(np.float32)
    cosT = np.ascontiguousarray(np.concatenate([cos, cos], axis=1).T)
    sinS = np.ascontiguousarray(np.concatenate([sin, -sin], axis=1).T)
    return cosT, sinS


def _band_img(T):
    nb = T.shape[1] // 128
    return np.ascontiguousarray(
        T.reshape(16, 128, nb, 128).transpose(2, 1, 0, 3).reshape(nb * 128, 2048))


def _vgroup_img(T):
    return np.ascontiguousarray(
        T.reshape(16, 128, 8, 256).transpose(2, 1, 0, 3).reshape(1024, 4096))


_NC_CACHE = [None]


def kernel(L0_attn_hidden_states, L0_attn_Wq, L0_attn_bq, L0_attn_Wk, L0_attn_bk,
           L0_attn_Wv, L0_attn_bv, L0_attn_Wo, L0_attn_bo, L0_ffn_G, L0_ffn_gate_w,
           L0_ffn_up_w, L0_ffn_down_w, L0_ffn_lora_A, L0_ffn_lora_B):
    x = np.ascontiguousarray(np.asarray(L0_attn_hidden_states, np.float32))
    wq_img = _band_img(np.asarray(L0_attn_Wq, np.float32).T)
    wk_img = _band_img(np.asarray(L0_attn_Wk, np.float32).T)
    wv_img = _vgroup_img(np.asarray(L0_attn_Wv, np.float32).T)
    wo_img = _band_img(np.asarray(L0_attn_Wo, np.float32).T)
    g_img = _band_img(np.asarray(L0_ffn_gate_w, np.float32).transpose(2, 0, 1).reshape(H, EF))
    u_img = _band_img(np.asarray(L0_ffn_up_w, np.float32).transpose(2, 0, 1).reshape(H, EF))
    d_img = _band_img(np.asarray(L0_ffn_down_w, np.float32).transpose(0, 2, 1).reshape(EF, H))
    bq = np.ascontiguousarray(np.asarray(L0_attn_bq, np.float32).reshape(16, 128).T)
    bk = np.ascontiguousarray(np.asarray(L0_attn_bk, np.float32).reshape(16, 128).T)
    bo = np.ascontiguousarray(np.asarray(L0_attn_bo, np.float32).reshape(16, 128).T)
    bv = np.ascontiguousarray(np.asarray(L0_attn_bv, np.float32).reshape(1, H))
    G = np.asarray(L0_ffn_G, np.float32)
    G_img = np.ascontiguousarray(G.reshape(16, 128, 8).transpose(1, 0, 2).reshape(128, 128))
    lAT = np.asarray(L0_ffn_lora_A, np.float32).transpose(2, 0, 1).reshape(H, 128)
    lAT_img = np.ascontiguousarray(
        lAT.reshape(16, 128, 128).transpose(1, 0, 2).reshape(128, 2048))
    lBT = np.ascontiguousarray(
        np.asarray(L0_ffn_lora_B, np.float32).transpose(0, 2, 1).reshape(128, H))

    if _NC_CACHE[0] is None or _NC_CACHE[0][0] != (PHASE_LIMIT, tuple(DEBUG_DUMP)):
        _NC_CACHE[0] = ((PHASE_LIMIT, tuple(DEBUG_DUMP)), build_nc())
    nc = _NC_CACHE[0][1]

    in_maps = []
    perms = []
    for c in range(8):
        b, h2 = c // 2, c % 2
        chunks = [2 * j + h2 for j in range(8)] + [2 * j + (1 - h2) for j in range(8)]
        cols = np.concatenate([np.arange(t * 128, (t + 1) * 128) for t in chunks])
        perms.append(cols)
        xTc = np.ascontiguousarray(x[b].T[:, cols])
        cosk, sink = _rope_tables(cols.astype(np.float32))
        mcol = np.full((128, 1), MASKNEG if h2 == 0 else 0.0, np.float32)
        in_maps.append({
            "xT": xTc, "wq_img": wq_img, "wk_img": wk_img, "wv_img": wv_img,
            "wo_img": wo_img, "g_img": g_img, "u_img": u_img, "d_img": d_img,
            "lAT": lAT_img, "lBT": lBT, "G_img": G_img,
            "bq_img": bq, "bk_img": bk, "bo_img": bo, "bv_row": bv,
            "cosk": cosk, "sink": sink, "maskcol": mcol,
        })
    res = run_bass_kernel_spmd(nc, in_maps, core_ids=list(range(8)))
    y = np.zeros((B, S, H), np.float32)
    for c in range(8):
        b = c // 2
        yq = res.results[c]["y_out"]
        y[b, perms[c][:TQ], :] = yq.T
    if DEBUG_DUMP:
        kernel._dbg = [{k: v for k, v in r.items() if k.startswith("dbg_")}
                       for r in res.results]
    return y


# version 15 (final)
# speedup vs baseline: 1.4733x; 1.0274x over previous
import numpy as np
import ml_dtypes
import concourse.bass as bass
import concourse.mybir as mybir
from concourse.bass_utils import run_bass_kernel_spmd

F32 = mybir.dt.float32
BF16 = mybir.dt.bfloat16
AF = mybir.ActivationFunctionType
ALU = mybir.AluOpType
AX = mybir.AxisListType
bfd = ml_dtypes.bfloat16

H = 2048
S = 2048
B = 4
NH = 16
HD = 128
NCK = 16
E = 8
EF = 2048
EPS = 1e-5
TQ = 1024
RSQ = float(1.0 / np.sqrt(np.float32(HD)))
MASKNEG = -1.0e9

ENGS = ("sync", "scalar", "vector", "tensor", "gpsimd")

DEBUG_DUMP = []
PHASE_LIMIT = 99
OP_LIMIT = None


LANES = {"gpsimd": 12, "sync": 6}


class Op:
    __slots__ = ("eng", "fn", "dma", "eidx", "deps", "inc", "waits", "tag", "chan",
                 "cidx")

    def __init__(self, eng, fn, dma, eidx, tag):
        self.eng = eng
        self.fn = fn
        self.dma = dma
        self.eidx = eidx
        self.deps = set()
        self.inc = False
        self.waits = []
        self.tag = tag
        self.chan = eng
        self.cidx = 0


class Prog:
    def __init__(self):
        self.eops = {e: [] for e in ENGS}
        self.segs = {}
        self.nemit = 0

    def emit(self, eng, fn, reads=(), writes=(), dma=False, tag=""):
        self.nemit += 1
        if OP_LIMIT is not None and self.nemit > OP_LIMIT and tag not in ("stub_out", "join"):
            return None
        op = Op(eng, fn, dma, len(self.eops[eng]), tag)
        self.eops[eng].append(op)
        for (sp, lo, hi) in reads:
            segs = self.segs.setdefault(sp, [])
            for seg in segs:
                if seg[0] < hi and lo < seg[1]:
                    if seg[2] is not None:
                        op.deps.add(seg[2])
                    seg[3].add(op)
        for (sp, lo, hi) in writes:
            segs = self.segs.setdefault(sp, [])
            new = []
            for seg in segs:
                if seg[0] < hi and lo < seg[1]:
                    if seg[2] is not None:
                        op.deps.add(seg[2])
                    op.deps.update(seg[3])
                    if seg[0] < lo:
                        new.append([seg[0], lo, seg[2], set(seg[3])])
                    if hi < seg[1]:
                        new.append([hi, seg[1], seg[2], set(seg[3])])
                else:
                    new.append(seg)
            new.append([lo, hi, op, set()])
            self.segs[sp] = new
        op.deps.discard(op)
        return op

    def finalize(self):
        lane_prev = {}
        for e in ENGS:
            nl = LANES.get(e, 0)
            ctr = 0
            for op in self.eops[e]:
                if op.dma and nl:
                    lane = ctr % nl
                    ctr += 1
                    op.chan = f"{e}:L{lane}"
                    prev = lane_prev.get(op.chan)
                    if prev is not None:
                        op.deps.add(prev)
                    lane_prev[op.chan] = op
        chans = list(ENGS) + sorted({op.chan for e in ENGS for op in self.eops[e]
                                     if op.chan not in ENGS})
        cctr = {c: 0 for c in chans}
        for e in ENGS:
            for op in self.eops[e]:
                op.cidx = cctr[op.chan]
                cctr[op.chan] += 1
        for e in ENGS:
            for op in self.eops[e]:
                best = {}
                for d in op.deps:
                    if d.eng == "tensor" and op.eng == "tensor":
                        continue
                    if d is op:
                        continue
                    cur = best.get(d.chan)
                    if cur is None or d.cidx > cur.cidx:
                        best[d.chan] = d
                op.deps = best
        wm = {e: {c: -1 for c in chans} for e in ENGS}
        for e in ENGS:
            w = wm[e]
            for op in self.eops[e]:
                kept = []
                for d in sorted(op.deps.values(), key=lambda d: d.chan):
                    if d.cidx <= w[d.chan]:
                        continue
                    if d.chan == op.chan and d.cidx >= op.cidx:
                        raise RuntimeError(f"self-dep ahead: {op.tag} <- {d.tag}")
                    w[d.chan] = d.cidx
                    kept.append(d)
                    d.inc = True
                op.deps = kept
        for e in ENGS:
            for op in self.eops[e]:
                if op.dma:
                    op.inc = True
        ccnt = {c: 0 for c in chans}
        cumv = {}
        for e in ENGS:
            for op in self.eops[e]:
                if op.inc:
                    ccnt[op.chan] += 16 if op.dma else 1
                cumv[(op.chan, op.cidx)] = ccnt[op.chan]
        for e in ENGS:
            for op in self.eops[e]:
                op.waits = [(d.chan, cumv[(d.chan, d.cidx)]) for d in op.deps]
        self.chans = chans
        pos = {e: 0 for e in ENGS}
        cnt = {c: 0 for c in chans}
        progress = True
        while progress:
            progress = False
            for e in ENGS:
                while pos[e] < len(self.eops[e]):
                    op = self.eops[e][pos[e]]
                    if all(cnt[se] >= v for (se, v) in op.waits):
                        if op.inc:
                            cnt[op.chan] += 16 if op.dma else 1
                        pos[e] += 1
                        progress = True
                    else:
                        break
        stuck = [e for e in ENGS if pos[e] < len(self.eops[e])]
        if stuck:
            msgs = []
            for e in stuck:
                op = self.eops[e][pos[e]]
                msgs.append(f"{e}[{pos[e]}] {op.tag} waits {op.waits}")
            raise RuntimeError("DEADLOCK:\n" + "\n".join(msgs))
        return cumv


def build_nc(phase_limit=None):
    if phase_limit is None:
        phase_limit = PHASE_LIMIT
    nc = bass.Bass("TRN2", target_bir_lowering=False, debug=False)

    xT = nc.dram_tensor("xT", [H, S], F32, kind="ExternalInput").ap()
    wq_img = nc.dram_tensor("wq_img", [H, H], F32, kind="ExternalInput").ap()
    wk_img = nc.dram_tensor("wk_img", [H, H], F32, kind="ExternalInput").ap()
    wv_img = nc.dram_tensor("wv_img", [1024, 4096], F32, kind="ExternalInput").ap()
    wo_img = nc.dram_tensor("wo_img", [H, H], F32, kind="ExternalInput").ap()
    g_img = nc.dram_tensor("g_img", [H, H], F32, kind="ExternalInput").ap()
    u_img = nc.dram_tensor("u_img", [H, H], F32, kind="ExternalInput").ap()
    d_img = nc.dram_tensor("d_img", [H, H], F32, kind="ExternalInput").ap()
    lATd = nc.dram_tensor("lAT", [128, 2048], F32, kind="ExternalInput").ap()
    lBTd = nc.dram_tensor("lBT", [128, H], F32, kind="ExternalInput").ap()
    Gd = nc.dram_tensor("G_img", [128, 128], F32, kind="ExternalInput").ap()
    bqd = nc.dram_tensor("bq_img", [128, 16], F32, kind="ExternalInput").ap()
    bkd = nc.dram_tensor("bk_img", [128, 16], F32, kind="ExternalInput").ap()
    bod = nc.dram_tensor("bo_img", [128, 16], F32, kind="ExternalInput").ap()
    bvd = nc.dram_tensor("bv_row", [1, H], F32, kind="ExternalInput").ap()
    coskd = nc.dram_tensor("cosk", [128, S], F32, kind="ExternalInput").ap()
    sinkd = nc.dram_tensor("sink", [128, S], F32, kind="ExternalInput").ap()
    mcold = nc.dram_tensor("maskcol", [128, 1], F32, kind="ExternalInput").ap()
    y_out = nc.dram_tensor("y_out", [H, TQ], F32, kind="ExternalOutput").ap()

    def bfc(a):
        return np.ascontiguousarray(np.asarray(a, np.float32).astype(bfd))

    tri_np = (np.arange(128)[:, None] <= np.arange(128)[None, :]).astype(np.float32)
    tri_d = nc.inline_tensor(bfc(tri_np), "c_tri")
    id128_d = nc.inline_tensor(bfc(np.eye(128)), "c_id128")
    id8f_d = nc.inline_tensor(np.ascontiguousarray(np.eye(8, dtype=np.float32)), "c_id8f")
    eer_np = np.zeros((8, 128), np.float32)
    for e_ in range(8):
        eer_np[e_, e_ * 16:(e_ + 1) * 16] = 1.0
    eer_d = nc.inline_tensor(bfc(eer_np), "c_eer")
    e8s_np = np.zeros((8, 8 * 128), np.float32)
    for e_ in range(8):
        e8s_np[e_, e_ * 128:(e_ + 1) * 128] = 1.0
    e8s_d = nc.inline_tensor(bfc(e8s_np), "c_e8s")

    dbg = {}
    for name, shape, dt in [
        ("xb", [128, NCK * S], BF16), ("qp", [128, TQ], BF16),
        ("kp", [128, S], BF16), ("vp", [128, 16 * 129], BF16),
        ("ctx", [H, TQ], BF16), ("y1", [H, TQ], F32),
        ("ln2", [128, NCK * TQ], BF16), ("w8", [8, TQ], BF16),
        ("gated", [128, NCK * 512], BF16), ("lrw", [128, TQ], BF16),
    ]:
        if name in DEBUG_DUMP:
            dbg[name] = nc.dram_tensor("dbg_" + name, shape, dt, kind="ExternalOutput").ap()

    SBUF_BASE = 16512
    SBUF_LIMIT = 212864
    A = {}
    meta = {}

    def place(name, ncols, dt, off):
        esz = 2 if dt == BF16 else 4
        off = (off + 31) // 32 * 32
        nb = ncols * esz
        assert off + nb <= SBUF_LIMIT, (name, off, nb)
        A[name] = nc.alloc_sbuf_tensor_at(name, [128, ncols], dt, offset=SBUF_BASE + off)
        meta[name] = (off, esz, ncols)
        return off + nb

    o = 0
    for nm, ncols, dt in [
        ("cosk", S, BF16), ("sink", S, BF16), ("bvbc", H, BF16),
        ("lATb", 2048, BF16), ("lBTb", H, BF16), ("Gb", 128, BF16),
        ("tri", 128, BF16), ("id128", 128, BF16), ("id8f", 8, F32),
        ("eer", 128, BF16), ("e8s", 8 * 128, BF16), ("bqc", 16, F32), ("bkc", 16, F32), ("boc", 16, F32),
        ("mcol", 1, F32), ("onecb", 1, BF16), ("onerf", 128, F32),
        ("onerb", 128, BF16), ("mbc", S, BF16), ("ibc", S, BF16),
        ("joinpad", 16, F32),
    ]:
        o = place(nm, ncols, dt, o)
    REG_P_END = o
    REG_X = (REG_P_END + 31) // 32 * 32
    place("xb", NCK * S, BF16, REG_X)
    place("y1", NCK * TQ, F32, REG_X)
    REG_X_END = REG_X + NCK * TQ * 4
    REG_C = (REG_X_END + 31) // 32 * 32
    place("ctxsb", NCK * TQ, BF16, REG_C)
    place("ln2", NCK * TQ, BF16, REG_C)
    REG_C_END = REG_C + NCK * TQ * 2
    o = (REG_C_END + 31) // 32 * 32
    REG_W = o
    for nm, ncols, dt in [
        ("wqh0", 2048, BF16), ("wqh1", 2048, BF16),
        ("wkh0", 2048, BF16), ("wkh1", 2048, BF16),
        ("wvg0", 4096, BF16), ("wvg1", 4096, BF16),
        ("vp0", 16 * 129, BF16), ("vp1", 16 * 129, BF16),
        ("qp0", TQ, BF16), ("qp1", TQ, BF16),
        ("kp0", S, BF16), ("kp1", S, BF16),
        ("pt0", TQ, BF16), ("pt1", TQ, BF16), ("pt2", TQ, BF16),
        ("qktmp", S, BF16), ("r3", S, BF16),
        ("ctxq", 256, BF16), ("rcol", 8, F32),
        ("rows", 1024, F32), ("rows2", 1024, F32), ("rsc", 512, F32),
        ("sq0", 512, BF16), ("sq1", 512, BF16), ("sq2", 512, BF16),
    ]:
        o = place(nm, ncols, dt, o)
    W12_END = o
    o = REG_W
    for nm, ncols, dt in [
        ("gated", NCK * 512, BF16),
        ("wch0", 2048, BF16), ("wch1", 2048, BF16),
        ("wgh0", 2048, BF16), ("wgh1", 2048, BF16),
        ("wuh0", 2048, BF16), ("wuh1", 2048, BF16),
        ("xqs0", TQ, F32), ("xqs1", TQ, F32),
        ("fout0", 512, F32), ("fout1", 512, F32),
        ("ytmp0", 512, F32), ("ytmp1", 512, F32),
        ("wbce0", 512, BF16), ("wbce1", 512, BF16),
        ("wer", TQ, BF16), ("lrw", TQ, BF16),
        ("logsb", 1024, F32), ("tk", 64, F32), ("wtk", 8, BF16),
        ("w8sb", TQ, BF16), ("sil", TQ, BF16),
    ]:
        o = place(nm, ncols, dt, o)
    W36_END = o
    assert max(W12_END, W36_END) <= SBUF_LIMIT, (W12_END, W36_END)

    PS = {}
    psmeta = {}
    psoff = [0]

    def psalloc(name, ncols, dt):
        t = nc.alloc_psum_tensor("ps_" + name, [128, ncols], dt)
        PS[name] = t
        esz = 2 if dt == BF16 else 4
        psmeta[name] = (psoff[0], esz)
        psoff[0] += 2048 * ((ncols * esz + 2047) // 2048)

    psalloc("a", 512, F32)
    psalloc("b", 512, F32)
    psalloc("s0", 512, F32)
    psalloc("s1", 512, F32)
    psalloc("c0", 512, F32)
    psalloc("c1", 512, F32)
    psalloc("t", 128, BF16)
    psalloc("t2", 512, F32)
    assert psoff[0] <= 8 * 2048, psoff[0]

    def rg(name, c0=None, c1=None):
        off, esz, ncols = meta[name]
        if c0 is None:
            c0, c1 = 0, ncols
        return ("sbuf", off + c0 * esz, off + c1 * esz)

    def prg(name, c0=None, c1=None):
        off, esz = psmeta[name]
        if c0 is None:
            return ("psum", off, off + 2048)
        return ("psum", off + c0 * esz, off + c1 * esz)

    P = Prog()
    em = P.emit

    def materialize():
        P.finalize()
        import contextlib
        with contextlib.ExitStack() as stack:
            sems = {}
            for c in P.chans:
                sems[c] = stack.enter_context(nc.semaphore("sem_" + c.replace(":", "_")))
            block = stack.enter_context(nc.Block())

            def runner(eng):
                def run(h):
                    for op in P.eops[eng]:
                        for (se, v) in op.waits:
                            h.wait_ge(sems[se], v)
                        inst = op.fn(h)
                        if op.inc:
                            inst.then_inc(sems[op.chan], 16 if op.dma else 1)
                return run

            block.sync(runner("sync"))
            block.scalar(runner("scalar"))
            block.vector(runner("vector"))
            block.tensor(runner("tensor"))
            block.gpsimd(runner("gpsimd"))
        return nc

    def stop_after(k):
        if phase_limit > k:
            return False
        em("gpsimd", lambda h: h.dma_start(out=y_out[0:128, :], in_=A["mbc"][:, 0:TQ]),
           reads=[rg("mbc", 0, TQ)], writes=[("yout", 0, 512)], dma=True, tag="stub_out")
        em("gpsimd", lambda h: h.memset(A["joinpad"][:, 0:1], 0.0),
           reads=[("yout", 0, 512)], writes=[rg("joinpad")], tag="join")
        return True

    for nm, src, np_ in [("tri", tri_d.ap(), 128), ("id128", id128_d.ap(), 128),
                         ("eer", eer_d.ap(), 8), ("e8s", e8s_d.ap(), 8)]:
        em("sync", lambda h, nm=nm, src=src, np_=np_: h.dma_start(out=A[nm][0:np_, :], in_=src),
           writes=[rg(nm)], dma=True, tag="ld_" + nm)
    em("sync", lambda h: h.dma_start(out=A["id8f"][0:8, :], in_=id8f_d.ap()),
       writes=[rg("id8f")], dma=True, tag="ld_id8f")
    for nm, src in [("bqc", bqd), ("bkc", bkd), ("boc", bod), ("mcol", mcold)]:
        em("sync", lambda h, nm=nm, src=src: h.dma_start(out=A[nm][:], in_=src[:, :]),
           writes=[rg(nm)], dma=True, tag="ld_" + nm)
    em("sync", lambda h: h.dma_start(out=A["rows"][0:1, :], in_=bvd[0:1, 0:1024]),
       writes=[rg("rows")], dma=True, tag="ld_bv0")
    em("sync", lambda h: h.dma_start(out=A["rows2"][0:1, :], in_=bvd[0:1, 1024:2048]),
       writes=[rg("rows2")], dma=True, tag="ld_bv1")
    for nm, src in [("cosk", coskd), ("sink", sinkd), ("Gb", Gd)]:
        em("gpsimd", lambda h, nm=nm, src=src: h.dma_start(out=A[nm][:], in_=src[:, :]),
           writes=[rg(nm)], dma=True, tag="ld_" + nm)
    em("gpsimd", lambda h: h.dma_start(out=A["lATb"][:], in_=lATd[:, :]),
       writes=[rg("lATb")], dma=True, tag="ld_lAT")
    em("gpsimd", lambda h: h.dma_start(out=A["lBTb"][:], in_=lBTd[:, :]),
       writes=[rg("lBTb")], dma=True, tag="ld_lBT")
    em("vector", lambda h: h.memset(A["onecb"][:], 1.0), writes=[rg("onecb")], tag="ms1")
    em("vector", lambda h: h.memset(A["onerf"][0:1, :], 1.0), writes=[rg("onerf")], tag="ms2")
    em("vector", lambda h: h.memset(A["onerb"][0:1, :], 1.0), writes=[rg("onerb")], tag="ms3")
    for p in range(2):
        em("vector", lambda h, p=p: h.memset(A[f"vp{p}"][:, 128:16 * 129:129], 1.0),
           writes=[rg(f"vp{p}")], tag=f"ms_vp{p}")
    for g in range(4):
        src = "rows" if g < 2 else "rows2"
        sc0 = (g % 2) * 512
        ps = ("a", "b")[g % 2]
        em("tensor", lambda h, src=src, sc0=sc0, ps=ps: h.matmul(
            PS[ps][:, :], A["onerf"][0:1, :], A[src][0:1, sc0:sc0 + 512],
            start=True, stop=True),
           reads=[rg("onerf"), rg(src, sc0, sc0 + 512)], writes=[prg(ps)], tag=f"bvb{g}")
        em("scalar", lambda h, g=g, ps=ps: h.copy(
            A["bvbc"][:, g * 512:(g + 1) * 512], PS[ps][:, :]),
           reads=[prg(ps)], writes=[rg("bvbc", g * 512, (g + 1) * 512)], tag=f"bve{g}")

    if stop_after(0):
        return materialize()

    for ic in range(NCK):
        em("gpsimd", lambda h, ic=ic: h.dma_start(
            out=A["xb"][:, ic * S:(ic + 1) * S], in_=xT[ic * 128:(ic + 1) * 128, :]),
           writes=[rg("xb", ic * S, (ic + 1) * S)], dma=True, tag=f"ld_x{ic}")

    def ln_stats_center(buf, bufstride, ntok, mdst, tag, upto=3):
        nhalf = ntok // 1024
        for hf in range(max(nhalf, 1)):
            if upto < 1:
                break
            t0 = hf * 1024
            tn = min(1024, ntok - t0)
            for ic in range(NCK):
                for g in range(2):
                    if g * 512 >= tn:
                        continue
                    sqb = f"sq{(ic * 2 + g) % 3}"
                    ps = ("a", "b")[g]
                    ps2 = ("c0", "c1")[g]
                    c0 = t0 + g * 512
                    em("vector", lambda h, ic=ic, c0=c0, sqb=sqb: h.tensor_tensor(
                        A[sqb][:, 0:512],
                        A[buf][:, ic * bufstride + c0: ic * bufstride + c0 + 512],
                        A[buf][:, ic * bufstride + c0: ic * bufstride + c0 + 512], ALU.mult),
                       reads=[rg(buf, ic * bufstride + c0, ic * bufstride + c0 + 512)],
                       writes=[rg(sqb, 0, 512)], tag=f"{tag}sq{hf}{g}_{ic}")
                    em("tensor", lambda h, ic=ic, c0=c0, ps=ps: h.matmul(
                        PS[ps][0:1, :], A["onecb"][:, :],
                        A[buf][:, ic * bufstride + c0: ic * bufstride + c0 + 512],
                        start=(ic == 0), stop=(ic == NCK - 1)),
                       reads=[rg("onecb"), rg(buf, ic * bufstride + c0, ic * bufstride + c0 + 512)],
                       writes=[prg(ps, 0, 512)], tag=f"{tag}sx{hf}{g}_{ic}")
                    em("tensor", lambda h, ic=ic, sqb=sqb, ps2=ps2: h.matmul(
                        PS[ps2][0:1, :], A["onecb"][:, :], A[sqb][:, 0:512],
                        start=(ic == 0), stop=(ic == NCK - 1)),
                       reads=[rg("onecb"), rg(sqb, 0, 512)],
                       writes=[prg(ps2, 0, 512)], tag=f"{tag}s2{hf}{g}_{ic}")
            if upto < 2:
                continue
            for g in range(2):
                if g * 512 >= tn:
                    continue
                ps = ("a", "b")[g]
                ps2 = ("c0", "c1")[g]
                gc = g * 512
                em("vector", lambda h, gc=gc, ps=ps: h.tensor_scalar(
                    A["rows"][0:1, gc:gc + 512], PS[ps][0:1, :], 1.0 / H, None, ALU.mult),
                   reads=[prg(ps, 0, 512)], writes=[rg("rows", gc, gc + 512)],
                   tag=f"{tag}m{hf}{g}")
                em("vector", lambda h, gc=gc: h.tensor_tensor(
                    A["rsc"][0:1, :], A["rows"][0:1, gc:gc + 512],
                    A["rows"][0:1, gc:gc + 512], ALU.mult),
                   reads=[rg("rows", gc, gc + 512)], writes=[rg("rsc")], tag=f"{tag}ms{hf}{g}")
                em("vector", lambda h, gc=gc, ps2=ps2: h.tensor_scalar(
                    A["rows2"][0:1, gc:gc + 512], PS[ps2][0:1, :], 1.0 / H, None, ALU.mult),
                   reads=[prg(ps2, 0, 512)], writes=[rg("rows2", gc, gc + 512)],
                   tag=f"{tag}v{hf}{g}")
                em("vector", lambda h, gc=gc: h.tensor_tensor(
                    A["rows2"][0:1, gc:gc + 512], A["rows2"][0:1, gc:gc + 512],
                    A["rsc"][0:1, :], ALU.subtract),
                   reads=[rg("rows2", gc, gc + 512), rg("rsc")],
                   writes=[rg("rows2", gc, gc + 512)], tag=f"{tag}v2{hf}{g}")
                em("vector", lambda h, gc=gc: h.tensor_scalar(
                    A["rows2"][0:1, gc:gc + 512], A["rows2"][0:1, gc:gc + 512],
                    float(EPS), None, ALU.add),
                   reads=[rg("rows2", gc, gc + 512)], writes=[rg("rows2", gc, gc + 512)],
                   tag=f"{tag}v3{hf}{g}")
                em("scalar", lambda h, gc=gc: h.activation(
                    A["rsc"][0:1, :], A["rows2"][0:1, gc:gc + 512], AF.Sqrt,
                    bias=0.0, scale=1.0),
                   reads=[rg("rows2", gc, gc + 512)], writes=[rg("rsc")], tag=f"{tag}sd{hf}{g}")
                em("vector", lambda h, gc=gc: h.reciprocal(
                    A["rows2"][0:1, gc:gc + 512], A["rsc"][0:1, :]),
                   reads=[rg("rsc")], writes=[rg("rows2", gc, gc + 512)], tag=f"{tag}i{hf}{g}")
                for (src, dst) in (("rows", "mbc"), ("rows2", "ibc")):
                    ps3 = (("a", "c0"), ("b", "c1"))[g][src == "rows2"]
                    em("tensor", lambda h, gc=gc, src=src, ps3=ps3: h.matmul(
                        PS[ps3][:, :], A["onerf"][0:1, :], A[src][0:1, gc:gc + 512],
                        start=True, stop=True),
                       reads=[rg("onerf"), rg(src, gc, gc + 512)], writes=[prg(ps3)],
                       tag=f"{tag}bc{hf}{g}{dst}")
                    em("scalar", lambda h, t0=t0, gc=gc, dst=dst, ps3=ps3: h.copy(
                        A[dst][:, t0 + gc: t0 + gc + 512], PS[ps3][:, :]),
                       reads=[prg(ps3)], writes=[rg(dst, t0 + gc, t0 + gc + 512)],
                       tag=f"{tag}be{hf}{g}{dst}")
        if upto < 3:
            return
        for ic in range(NCK):
            em("vector", lambda h, ic=ic: h.tensor_tensor(
                A[buf][:, ic * bufstride: ic * bufstride + ntok],
                A[buf][:, ic * bufstride: ic * bufstride + ntok],
                A["mbc"][:, 0:ntok], ALU.subtract),
               reads=[rg(buf, ic * bufstride, ic * bufstride + ntok), rg("mbc", 0, ntok)],
               writes=[rg(buf, ic * bufstride, ic * bufstride + ntok)], tag=f"{tag}ce{ic}")
            em("vector", lambda h, ic=ic: h.tensor_tensor(
                A[buf][:, ic * bufstride: ic * bufstride + ntok],
                A[buf][:, ic * bufstride: ic * bufstride + ntok],
                A["ibc"][:, 0:ntok], ALU.mult),
               reads=[rg(buf, ic * bufstride, ic * bufstride + ntok), rg("ibc", 0, ntok)],
               writes=[rg(buf, ic * bufstride, ic * bufstride + ntok)], tag=f"{tag}cs{ic}")

    if stop_after(0.2):
        return materialize()
    ln_stats_center("xb", S, S, "mbc", "ln1",
                    upto=(0 if phase_limit <= 0.4 else 1 if phase_limit <= 0.6
                          else 2 if phase_limit <= 0.8 else 3))
    if "xb" in dbg:
        em("sync", lambda h: h.dma_start(out=dbg["xb"][:, :], in_=A["xb"][:, :]),
           reads=[rg("xb")], dma=True, tag="dbg_xb")

    if stop_after(1):
        return materialize()

    def load_head_weights(hd):
        p = hd % 2
        em("gpsimd", lambda h, hd=hd, p=p: h.dma_start(
            out=A[f"wqh{p}"][:, :], in_=wq_img[hd * 128:(hd + 1) * 128, :]),
           writes=[rg(f"wqh{p}")], dma=True, tag=f"ldwq{hd}")
        em("gpsimd", lambda h, hd=hd, p=p: h.dma_start(
            out=A[f"wkh{p}"][:, :], in_=wk_img[hd * 128:(hd + 1) * 128, :]),
           writes=[rg(f"wkh{p}")], dma=True, tag=f"ldwk{hd}")
        if hd % 2 == 0:
            g = hd // 2
            pv = g % 2
            em("gpsimd", lambda h, g=g, pv=pv: h.dma_start(
                out=A[f"wvg{pv}"][:, :], in_=wv_img[g * 128:(g + 1) * 128, :]),
               writes=[rg(f"wvg{pv}")], dma=True, tag=f"ldwv{g}")

    PROT = [0]
    PEND_EVAC = [None]
    CTX_ROT = [0]

    def proj_qk(hd, kind):
        p = hd % 2
        wb = f"wqh{p}" if kind == "q" else f"wkh{p}"
        dst = f"qp{p}" if kind == "q" else f"kp{p}"
        bias = "bqc" if kind == "q" else "bkc"
        ntt = 2 if kind == "q" else 4
        n = ntt * 512
        for tt in range(ntt):
            ps = ("a", "b")[(PROT[0] + tt) % 2]
            for ic in range(NCK):
                em("tensor", lambda h, ic=ic, tt=tt, wb=wb, ps=ps: h.matmul(
                    PS[ps][:, :], A[wb][:, ic * 128:(ic + 1) * 128],
                    A["xb"][:, ic * S + tt * 512: ic * S + (tt + 1) * 512],
                    start=(ic == 0), stop=(ic == NCK - 1)),
                   reads=[rg(wb, ic * 128, (ic + 1) * 128),
                          rg("xb", ic * S + tt * 512, ic * S + (tt + 1) * 512)],
                   writes=[prg(ps)], tag=f"{kind}mm{hd}_{tt}_{ic}")
            em("vector", lambda h, tt=tt, hd=hd, bias=bias, ps=ps: h.tensor_scalar(
                A["qktmp"][:, tt * 512:(tt + 1) * 512], PS[ps][:, :],
                A[bias][:, hd:hd + 1], None, ALU.add),
               reads=[prg(ps), rg(bias, hd, hd + 1)],
               writes=[rg("qktmp", tt * 512, (tt + 1) * 512)], tag=f"{kind}ev{hd}_{tt}")
        PROT[0] = (PROT[0] + ntt) % 2
        em("vector", lambda h, dst=dst, n=n: h.tensor_tensor(
            A[dst][:, 0:n], A["qktmp"][:, 0:n], A["cosk"][:, 0:n], ALU.mult),
           reads=[rg("qktmp", 0, n), rg("cosk", 0, n)], writes=[rg(dst, 0, n)],
           tag=f"{kind}rc{hd}")
        em("vector", lambda h, n=n: h.tensor_tensor(
            A["r3"][0:64, 0:n], A["qktmp"][64:128, 0:n], A["sink"][64:128, 0:n], ALU.mult),
           reads=[rg("qktmp", 0, n), rg("sink", 0, n)], writes=[rg("r3", 0, n)],
           tag=f"{kind}rs1{hd}")
        em("vector", lambda h, n=n: h.tensor_tensor(
            A["r3"][64:128, 0:n], A["qktmp"][0:64, 0:n], A["sink"][0:64, 0:n], ALU.mult),
           reads=[rg("qktmp", 0, n), rg("sink", 0, n)], writes=[rg("r3", 0, n)],
           tag=f"{kind}rs2{hd}")
        em("vector", lambda h, dst=dst, n=n: h.tensor_tensor(
            A[dst][:, 0:n], A[dst][:, 0:n], A["r3"][:, 0:n], ALU.add),
           reads=[rg(dst, 0, n), rg("r3", 0, n)], writes=[rg(dst, 0, n)],
           tag=f"{kind}ra{hd}")

    def proj_v(hd):
        g = hd // 2
        pv = g % 2
        for tc in range(NCK):
            ps = ("a", "b")[(PROT[0] + tc) % 2]
            for ic in range(NCK):
                em("tensor", lambda h, ic=ic, tc=tc, pv=pv, ps=ps: h.matmul(
                    PS[ps][:, 0:256], A["xb"][:, ic * S + tc * 128: ic * S + (tc + 1) * 128],
                    A[f"wvg{pv}"][:, ic * 256:(ic + 1) * 256],
                    start=(ic == 0), stop=(ic == NCK - 1)),
                   reads=[rg("xb", ic * S + tc * 128, ic * S + (tc + 1) * 128),
                          rg(f"wvg{pv}", ic * 256, (ic + 1) * 256)],
                   writes=[prg(ps, 0, 256)], tag=f"vmm{g}_{tc}_{ic}")
            for sub in range(2):
                hh = g * 2 + sub
                pw = hh % 2
                em("vector", lambda h, tc=tc, sub=sub, hh=hh, pw=pw, ps=ps: h.tensor_tensor(
                    A[f"vp{pw}"][:, tc * 129: tc * 129 + 128],
                    PS[ps][:, sub * 128:(sub + 1) * 128],
                    A["bvbc"][:, hh * 128:(hh + 1) * 128], ALU.add),
                   reads=[prg(ps, sub * 128, (sub + 1) * 128),
                          rg("bvbc", hh * 128, (hh + 1) * 128)],
                   writes=[rg(f"vp{pw}", tc * 129, tc * 129 + 128)],
                   tag=f"vev{hh}_{tc}")
        PROT[0] = (PROT[0] + NCK) % 2

    def attn(hd):
        p = hd % 2
        tilectr = [0]
        CTXB = ("c0", "c1", "t2")
        for grp in range(4):
            gb = (CTX_ROT[0] + 2 * grp) % 3
            ctx_slot = [(CTXB[gb], 0), (CTXB[(gb + 1) % 3], 0)]
            slots = list(range(grp * 2, grp * 2 + 2))
            kcs = [kc for kc in range(16) if (kc % 8) <= slots[-1]]
            pend = []
            kcount = 0
            for kc in kcs:
                j0 = max(kc % 8, slots[0])
                if j0 > slots[-1]:
                    continue
                n = (slots[-1] - j0 + 1) * 128
                q0 = j0 * 128
                sl = tilectr[0] % 2
                tilectr[0] += 1
                pp = f"pt{sl}"
                ps, so = (("s0", 0), ("s1", 0))[sl]
                w = n
                em("tensor", lambda h, kc=kc, q0=q0, w=w, p=p, ps=ps, so=so: h.matmul(
                    PS[ps][:, so:so + w], A[f"kp{p}"][:, kc * 128:(kc + 1) * 128],
                    A[f"qp{p}"][:, q0: q0 + w], start=True, stop=True),
                   reads=[rg(f"kp{p}", kc * 128, (kc + 1) * 128),
                          rg(f"qp{p}", q0, q0 + w)],
                   writes=[prg(ps, so, so + w)], tag=f"sc{hd}_{kc}")
                if kc >= 8 and j0 == kc - 8:
                    em("scalar", lambda h, ps=ps, pp=pp, q0=q0, so=so: h.activation(
                        A[pp][:, q0: q0 + 128], PS[ps][:, so:so + 128], AF.Exp,
                        bias=A["mcol"][:, 0:1], scale=RSQ),
                       reads=[prg(ps, so, so + 128), rg("mcol")],
                       writes=[rg(pp, q0, q0 + 128)], tag=f"exm{hd}_{kc}")
                    if w > 128:
                        em("scalar", lambda h, ps=ps, pp=pp, q0=q0, w=w, so=so: h.activation(
                            A[pp][:, q0 + 128: q0 + w], PS[ps][:, so + 128:so + w], AF.Exp,
                            bias=0.0, scale=RSQ),
                           reads=[prg(ps, so + 128, so + w)],
                           writes=[rg(pp, q0 + 128, q0 + w)], tag=f"exr{hd}_{kc}")
                else:
                    em("scalar", lambda h, ps=ps, pp=pp, q0=q0, w=w, so=so: h.activation(
                        A[pp][:, q0: q0 + w], PS[ps][:, so:so + w], AF.Exp,
                        bias=0.0, scale=RSQ),
                       reads=[prg(ps, so, so + w)], writes=[rg(pp, q0, q0 + w)],
                       tag=f"ex{hd}_{kc}")
                if kc < 8 and j0 == kc:
                    em("vector", lambda h, pp=pp, q0=q0: h.tensor_tensor(
                        A[pp][:, q0: q0 + 128], A[pp][:, q0: q0 + 128],
                        A["tri"][:, :], ALU.mult),
                       reads=[rg(pp, q0, q0 + 128), rg("tri")],
                       writes=[rg(pp, q0, q0 + 128)], tag=f"tri{hd}_{kc}")

                def ctx_emit(kc=kc, j0=j0, w=w, pp=pp):
                    for j in range(j0, j0 + w // 128):
                        bank, boff = ctx_slot[j % 2]
                        em("tensor", lambda h, j=j, kc=kc, p=p, pp=pp, bank=bank,
                           boff=boff: h.matmul(
                            PS[bank][:, boff:boff + 129], A[pp][:, j * 128:(j + 1) * 128],
                            A[f"vp{p}"][:, kc * 129:(kc + 1) * 129],
                            start=(kc == 0), stop=(kc == 8 + j)),
                           reads=[rg(pp, j * 128, (j + 1) * 128),
                                  rg(f"vp{p}", kc * 129, (kc + 1) * 129)],
                           writes=[prg(bank, boff, boff + 129)], tag=f"cx{hd}_{kc}_{j}")
                kcount += 1
                if kcount == 2 and PEND_EVAC[0] is not None:
                    PEND_EVAC[0]()
                    PEND_EVAC[0] = None
                pend.append(ctx_emit)
                if len(pend) > 1:
                    pend.pop(0)()
            for f in pend:
                f()
            if PEND_EVAC[0] is not None:
                PEND_EVAC[0]()
                PEND_EVAC[0] = None

            def grp_evac(slots=slots, hd=hd, ctx_slot=tuple(ctx_slot)):
                for j in slots:
                    bank, boff = ctx_slot[j % 2]
                    em("vector", lambda h, bank=bank, boff=boff, j=j: h.reciprocal(
                        A["rcol"][:, (j % 2):(j % 2) + 1], PS[bank][:, boff + 128:boff + 129]),
                       reads=[prg(bank, boff + 128, boff + 129)],
                       writes=[rg("rcol", j % 2, j % 2 + 1)], tag=f"rc{hd}_{j}")
                    em("vector", lambda h, bank=bank, boff=boff, j=j: h.tensor_scalar(
                        A["ctxq"][:, (j % 2) * 128:(j % 2) * 128 + 128],
                        PS[bank][:, boff:boff + 128], A["rcol"][:, (j % 2):(j % 2) + 1],
                        None, ALU.mult),
                       reads=[prg(bank, boff, boff + 128), rg("rcol", j % 2, j % 2 + 1)],
                       writes=[rg("ctxq", (j % 2) * 128, (j % 2) * 128 + 128)],
                       tag=f"nm{hd}_{j}")
                for j in slots:
                    em("tensor", lambda h, j=j: h.transpose(
                        PS["t"][:, 0:128], A["ctxq"][:, (j % 2) * 128:(j % 2) * 128 + 128],
                        A["id128"][:, :]),
                       reads=[rg("ctxq", (j % 2) * 128, (j % 2) * 128 + 128), rg("id128")],
                       writes=[prg("t")], tag=f"tp{hd}_{j}")
                    em("scalar", lambda h, hd=hd, j=j: h.copy(
                        A["ctxsb"][:, hd * TQ + j * 128: hd * TQ + (j + 1) * 128],
                        PS["t"][:, 0:128]),
                       reads=[prg("t")],
                       writes=[rg("ctxsb", hd * TQ + j * 128, hd * TQ + (j + 1) * 128)],
                       tag=f"cte{hd}_{j}")
            PEND_EVAC[0] = grp_evac

    load_head_weights(0)
    load_head_weights(1)
    proj_qk(0, "q")
    proj_qk(0, "k")
    for hd in range(NH):
        if hd + 2 < NH:
            load_head_weights(hd + 2)
        if hd + 1 < NH:
            proj_qk(hd + 1, "q")
            proj_qk(hd + 1, "k")
        if hd % 2 == 0:
            proj_v(hd)
        attn(hd)
        CTX_ROT[0] = (CTX_ROT[0] + 8) % 3
        if hd == 0:
            if "qp" in dbg:
                em("sync", lambda h: h.dma_start(out=dbg["qp"][:, :], in_=A["qp0"][:, :]),
                   reads=[rg("qp0")], dma=True, tag="dbg_qp")
            if "kp" in dbg:
                em("sync", lambda h: h.dma_start(out=dbg["kp"][:, :], in_=A["kp0"][:, :]),
                   reads=[rg("kp0")], dma=True, tag="dbg_kp")
            if "vp" in dbg:
                em("sync", lambda h: h.dma_start(out=dbg["vp"][:, :], in_=A["vp0"][:, :]),
                   reads=[rg("vp0")], dma=True, tag="dbg_vp")
    PEND_EVAC[0]()
    PEND_EVAC[0] = None
    if "ctx" in dbg:
        for ic in range(NCK):
            em("sync", lambda h, ic=ic: h.dma_start(
                out=dbg["ctx"][ic * 128:(ic + 1) * 128, :],
                in_=A["ctxsb"][:, ic * TQ:(ic + 1) * TQ]),
               reads=[rg("ctxsb", ic * TQ, (ic + 1) * TQ)], dma=True, tag=f"dbg_ctx{ic}")

    if stop_after(2):
        return materialize()

    def load_band(dst, dram, band, tagp):
        em("gpsimd", lambda h, dst=dst, dram=dram, band=band: h.dma_start(
            out=A[dst][:, :], in_=dram[band * 128:(band + 1) * 128, :]),
           writes=[rg(dst)], dma=True, tag=tagp)

    load_band("wch0", wo_img, 0, "ldwo0")
    for hoc in range(NCK):
        pw = hoc % 2
        if hoc + 1 < NCK:
            load_band(f"wch{(hoc + 1) % 2}", wo_img, hoc + 1, f"ldwo{hoc + 1}")
        em("sync", lambda h, hoc=hoc, pw=pw: h.dma_start(
            out=A[f"xqs{pw}"][:, :], in_=xT[hoc * 128:(hoc + 1) * 128, 0:TQ]),
           writes=[rg(f"xqs{pw}")], dma=True, tag=f"ldxq{hoc}")
        for tt in range(2):
            ps = ("a", "b")[tt]
            pf = tt
            for ic in range(NCK):
                em("tensor", lambda h, ic=ic, tt=tt, pw=pw, ps=ps: h.matmul(
                    PS[ps][:, :], A[f"wch{pw}"][:, ic * 128:(ic + 1) * 128],
                    A["ctxsb"][:, ic * TQ + tt * 512: ic * TQ + (tt + 1) * 512],
                    start=(ic == 0), stop=(ic == NCK - 1)),
                   reads=[rg(f"wch{pw}", ic * 128, (ic + 1) * 128),
                          rg("ctxsb", ic * TQ + tt * 512, ic * TQ + (tt + 1) * 512)],
                   writes=[prg(ps)], tag=f"omm{hoc}_{tt}_{ic}")
            em("scalar", lambda h, hoc=hoc, pf=pf, ps=ps: h.activation(
                A[f"ytmp{pf}"][:, :], PS[ps][:, :], AF.Identity,
                bias=A["boc"][:, hoc:hoc + 1], scale=1.0),
               reads=[prg(ps), rg("boc", hoc, hoc + 1)],
               writes=[rg(f"ytmp{pf}")], tag=f"oev{hoc}_{tt}")
            em("vector", lambda h, hoc=hoc, tt=tt, pw=pw, pf=pf: h.tensor_tensor(
                A["y1"][:, hoc * TQ + tt * 512: hoc * TQ + (tt + 1) * 512],
                A[f"ytmp{pf}"][:, :], A[f"xqs{pw}"][:, tt * 512:(tt + 1) * 512], ALU.add),
               reads=[rg(f"ytmp{pf}"), rg(f"xqs{pw}", tt * 512, (tt + 1) * 512)],
               writes=[rg("y1", hoc * TQ + tt * 512, hoc * TQ + (tt + 1) * 512)],
               tag=f"ores{hoc}_{tt}")
    if "y1" in dbg:
        for ic in range(NCK):
            em("sync", lambda h, ic=ic: h.dma_start(
                out=dbg["y1"][ic * 128:(ic + 1) * 128, :],
                in_=A["y1"][:, ic * TQ:(ic + 1) * TQ]),
               reads=[rg("y1", ic * TQ, (ic + 1) * TQ)], dma=True, tag=f"dbg_y1{ic}")

    if stop_after(3):
        return materialize()

    for ic in range(NCK):
        em("vector", lambda h, ic=ic: h.tensor_copy(
            A["ln2"][:, ic * TQ:(ic + 1) * TQ], A["y1"][:, ic * TQ:(ic + 1) * TQ]),
           reads=[rg("y1", ic * TQ, (ic + 1) * TQ)],
           writes=[rg("ln2", ic * TQ, (ic + 1) * TQ)], tag=f"l2c{ic}")
    ln_stats_center("ln2", TQ, TQ, "mbc", "ln2")
    if "ln2" in dbg:
        em("sync", lambda h: h.dma_start(out=dbg["ln2"][:, :], in_=A["ln2"][:, :]),
           reads=[rg("ln2")], dma=True, tag="dbg_ln2")

    if stop_after(4):
        return materialize()

    TK = A["tk"]
    for tt in range(2):
        ps = ("a", "b")[tt]
        for ic in range(NCK):
            em("tensor", lambda h, ic=ic, tt=tt, ps=ps: h.matmul(
                PS[ps][0:8, :], A["Gb"][:, ic * 8:(ic + 1) * 8],
                A["ln2"][:, ic * TQ + tt * 512: ic * TQ + (tt + 1) * 512],
                start=(ic == 0), stop=(ic == NCK - 1)),
               reads=[rg("Gb", ic * 8, (ic + 1) * 8),
                      rg("ln2", ic * TQ + tt * 512, ic * TQ + (tt + 1) * 512)],
               writes=[prg(ps, 0, 512)], tag=f"lg{tt}_{ic}")
        em("scalar", lambda h, tt=tt, ps=ps: h.copy(
            A["logsb"][0:8, tt * 512:(tt + 1) * 512], PS[ps][0:8, :]),
           reads=[prg(ps, 0, 512)], writes=[rg("logsb", tt * 512, (tt + 1) * 512)],
           tag=f"lge{tt}")

    def router_tg(tg):
        em("tensor", lambda h, tg=tg: h.transpose(
            PS["t2"][:, 0:8], A["logsb"][0:8, tg * 128:(tg + 1) * 128], A["id8f"][0:8, :]),
           reads=[rg("logsb", tg * 128, (tg + 1) * 128), rg("id8f")],
           writes=[prg("t2", 0, 8)], tag=f"rtp{tg}")
        em("vector", lambda h: h.tensor_copy(TK[:, 0:8], PS["t2"][:, 0:8]),
           reads=[prg("t2", 0, 8)], writes=[rg("tk", 0, 8)], tag=f"rtc{tg}")
        em("vector", lambda h: h.reduce_max(TK[:, 16:17], TK[:, 0:8], axis=AX.X,
                                            negate=True),
           reads=[rg("tk", 0, 8)], writes=[rg("tk", 16, 17)], tag=f"rmx{tg}")
        em("scalar", lambda h: h.activation(TK[:, 8:16], TK[:, 0:8], AF.Exp,
                                            bias=TK[:, 16:17], scale=1.0),
           reads=[rg("tk", 0, 8), rg("tk", 16, 17)], writes=[rg("tk", 8, 16)],
           tag=f"rex{tg}")
        em("vector", lambda h: h.reduce_sum(TK[:, 17:18], TK[:, 8:16], axis=AX.X),
           reads=[rg("tk", 8, 16)], writes=[rg("tk", 17, 18)], tag=f"rsm{tg}")
        em("vector", lambda h: h.reciprocal(TK[:, 18:19], TK[:, 17:18]),
           reads=[rg("tk", 17, 18)], writes=[rg("tk", 18, 19)], tag=f"rrc{tg}")
        em("vector", lambda h: h.tensor_scalar(TK[:, 24:32], TK[:, 8:16],
                                               TK[:, 18:19], None, ALU.mult),
           reads=[rg("tk", 8, 16), rg("tk", 18, 19)], writes=[rg("tk", 24, 32)],
           tag=f"rp{tg}")
        em("vector", lambda h: h.reduce_max(TK[:, 19:20], TK[:, 24:32], axis=AX.X),
           reads=[rg("tk", 24, 32)], writes=[rg("tk", 19, 20)], tag=f"rpm{tg}")
        em("vector", lambda h: h.tensor_scalar(TK[:, 32:40], TK[:, 24:32],
                                               TK[:, 19:20], None, ALU.is_lt),
           reads=[rg("tk", 24, 32), rg("tk", 19, 20)], writes=[rg("tk", 32, 40)],
           tag=f"rlt{tg}")
        em("vector", lambda h: h.tensor_tensor(TK[:, 32:40], TK[:, 32:40],
                                               TK[:, 24:32], ALU.mult),
           reads=[rg("tk", 32, 40), rg("tk", 24, 32)], writes=[rg("tk", 32, 40)],
           tag=f"rl2{tg}")
        em("vector", lambda h: h.reduce_max(TK[:, 20:21], TK[:, 32:40], axis=AX.X),
           reads=[rg("tk", 32, 40)], writes=[rg("tk", 20, 21)], tag=f"rsc{tg}")
        em("vector", lambda h: h.tensor_scalar(TK[:, 40:48], TK[:, 24:32],
                                               TK[:, 20:21], None, ALU.is_ge),
           reads=[rg("tk", 24, 32), rg("tk", 20, 21)], writes=[rg("tk", 40, 48)],
           tag=f"rge{tg}")
        em("vector", lambda h: h.tensor_tensor(TK[:, 48:56], TK[:, 40:48],
                                               TK[:, 24:32], ALU.mult),
           reads=[rg("tk", 40, 48), rg("tk", 24, 32)], writes=[rg("tk", 48, 56)],
           tag=f"rw{tg}")
        em("vector", lambda h: h.tensor_copy(A["wtk"][:, 0:8], TK[:, 48:56]),
           reads=[rg("tk", 48, 56)], writes=[rg("wtk")], tag=f"rwb{tg}")

    def router_tg2(tg):
        em("tensor", lambda h: h.transpose(PS["t"][0:8, 0:128], A["wtk"][:, 0:8],
                                           A["id128"][:, :]),
           reads=[rg("wtk"), rg("id128")], writes=[prg("t")], tag=f"rtb{tg}")
        em("scalar", lambda h, tg=tg: h.copy(
            A["w8sb"][0:8, tg * 128:(tg + 1) * 128], PS["t"][0:8, 0:128]),
           reads=[prg("t")], writes=[rg("w8sb", tg * 128, (tg + 1) * 128)],
           tag=f"rwe{tg}")

    def load_fband(dst, dram, band, tagp):
        em("gpsimd", lambda h, dst=dst, dram=dram, band=band: h.dma_start(
            out=A[dst][:, :], in_=dram[band * 128:(band + 1) * 128, :]),
           writes=[rg(dst)], dma=True, tag=tagp)

    for tt in range(2):
        load_fband("wgh0", g_img, 0, f"ldg{tt}_0")
        load_fband("wuh0", u_img, 0, f"ldu{tt}_0")
        for efc in range(NCK):
            pw = efc % 2
            if efc + 1 < NCK:
                load_fband(f"wgh{(efc + 1) % 2}", g_img, efc + 1, f"ldg{tt}_{efc + 1}")
                load_fband(f"wuh{(efc + 1) % 2}", u_img, efc + 1, f"ldu{tt}_{efc + 1}")
            psg = ("a", "b")[efc % 2]
            psu = ("c0", "c1")[efc % 2]
            for ic in range(NCK):
                em("tensor", lambda h, ic=ic, tt=tt, pw=pw, psg=psg: h.matmul(
                    PS[psg][:, :], A[f"wgh{pw}"][:, ic * 128:(ic + 1) * 128],
                    A["ln2"][:, ic * TQ + tt * 512: ic * TQ + (tt + 1) * 512],
                    start=(ic == 0), stop=(ic == NCK - 1)),
                   reads=[rg(f"wgh{pw}", ic * 128, (ic + 1) * 128),
                          rg("ln2", ic * TQ + tt * 512, ic * TQ + (tt + 1) * 512)],
                   writes=[prg(psg)], tag=f"gmm{tt}_{efc}_{ic}")
                em("tensor", lambda h, ic=ic, tt=tt, pw=pw, psu=psu: h.matmul(
                    PS[psu][:, :], A[f"wuh{pw}"][:, ic * 128:(ic + 1) * 128],
                    A["ln2"][:, ic * TQ + tt * 512: ic * TQ + (tt + 1) * 512],
                    start=(ic == 0), stop=(ic == NCK - 1)),
                   reads=[rg(f"wuh{pw}", ic * 128, (ic + 1) * 128),
                          rg("ln2", ic * TQ + tt * 512, ic * TQ + (tt + 1) * 512)],
                   writes=[prg(psu)], tag=f"umm{tt}_{efc}_{ic}")
            sc0 = (efc % 2) * 512
            em("scalar", lambda h, sc0=sc0, psg=psg: h.activation(
                A["sil"][:, sc0:sc0 + 512], PS[psg][:, :], AF.Silu, bias=0.0, scale=1.0),
               reads=[prg(psg)], writes=[rg("sil", sc0, sc0 + 512)], tag=f"sil{tt}_{efc}")
            em("vector", lambda h, efc=efc, sc0=sc0, psu=psu: h.tensor_tensor(
                A["gated"][:, efc * 512:(efc + 1) * 512], PS[psu][:, :],
                A["sil"][:, sc0:sc0 + 512], ALU.mult),
               reads=[prg(psu), rg("sil", sc0, sc0 + 512)],
               writes=[rg("gated", efc * 512, (efc + 1) * 512)], tag=f"gu{tt}_{efc}")
            if tt == 0 and efc % 2 == 0:
                router_tg(efc // 2)
            if tt == 0 and efc % 2 == 1:
                router_tg2(efc // 2)
        if tt == 0:
            for g in range(2):
                ps = ("c0", "c1")[g]
                em("tensor", lambda h, g=g, ps=ps: h.matmul(
                    PS[ps][:, :], A["eer"][0:8, :], A["w8sb"][0:8, g * 512:(g + 1) * 512],
                    start=True, stop=True),
                   reads=[rg("eer"), rg("w8sb", g * 512, (g + 1) * 512)], writes=[prg(ps)],
                   tag=f"wer{g}")
                em("scalar", lambda h, g=g, ps=ps: h.copy(
                    A["wer"][:, g * 512:(g + 1) * 512], PS[ps][:, :]),
                   reads=[prg(ps)], writes=[rg("wer", g * 512, (g + 1) * 512)], tag=f"were{g}")
            for ltt in range(2):
                ps = ("a", "b")[ltt]
                for ic in range(NCK):
                    em("tensor", lambda h, ic=ic, ltt=ltt, ps=ps: h.matmul(
                        PS[ps][:, :], A["lATb"][:, ic * 128:(ic + 1) * 128],
                        A["ln2"][:, ic * TQ + ltt * 512: ic * TQ + (ltt + 1) * 512],
                        start=(ic == 0), stop=(ic == NCK - 1)),
                       reads=[rg("lATb", ic * 128, (ic + 1) * 128),
                              rg("ln2", ic * TQ + ltt * 512, ic * TQ + (ltt + 1) * 512)],
                       writes=[prg(ps)], tag=f"lrmm{ltt}_{ic}")
                em("vector", lambda h, ltt=ltt, ps=ps: h.tensor_tensor(
                    A["lrw"][:, ltt * 512:(ltt + 1) * 512], PS[ps][:, :],
                    A["wer"][:, ltt * 512:(ltt + 1) * 512], ALU.mult),
                   reads=[prg(ps), rg("wer", ltt * 512, (ltt + 1) * 512)],
                   writes=[rg("lrw", ltt * 512, (ltt + 1) * 512)], tag=f"lrw{ltt}")
            if "w8" in dbg:
                em("sync", lambda h: h.dma_start(out=dbg["w8"][:, :], in_=A["w8sb"][0:8, :]),
                   reads=[rg("w8sb")], dma=True, tag="dbg_w8")
            if "lrw" in dbg:
                em("sync", lambda h: h.dma_start(out=dbg["lrw"][:, :], in_=A["lrw"][:, :]),
                   reads=[rg("lrw")], dma=True, tag="dbg_lrw")
        for e_ in range(8):
            pb = e_ % 2
            em("tensor", lambda h, e_=e_, tt=tt: h.matmul(
                PS["t2"][:, :], A["e8s"][0:8, e_ * 128:(e_ + 1) * 128],
                A["w8sb"][0:8, tt * 512:(tt + 1) * 512], start=True, stop=True),
               reads=[rg("e8s", e_ * 128, (e_ + 1) * 128),
                      rg("w8sb", tt * 512, (tt + 1) * 512)],
               writes=[prg("t2")], tag=f"wbc{tt}_{e_}")
            em("scalar", lambda h, pb=pb: h.copy(A[f"wbce{pb}"][:, :], PS["t2"][:, 0:512]),
               reads=[prg("t2")], writes=[rg(f"wbce{pb}")], tag=f"wbe{tt}_{e_}")
            for sub in range(2):
                efc = e_ * 2 + sub
                em("vector", lambda h, efc=efc, pb=pb: h.tensor_tensor(
                    A["gated"][:, efc * 512:(efc + 1) * 512],
                    A["gated"][:, efc * 512:(efc + 1) * 512],
                    A[f"wbce{pb}"][:, :], ALU.mult),
                   reads=[rg("gated", efc * 512, (efc + 1) * 512), rg(f"wbce{pb}")],
                   writes=[rg("gated", efc * 512, (efc + 1) * 512)], tag=f"gw{tt}_{efc}")
        if "gated" in dbg and tt == 0:
            em("sync", lambda h: h.dma_start(out=dbg["gated"][:, :], in_=A["gated"][:, :]),
               reads=[rg("gated")], dma=True, tag="dbg_gated")
        load_fband("wch0", d_img, 0, f"ldd{tt}_0")
        for hoc in range(NCK):
            pw = hoc % 2
            if hoc + 1 < NCK:
                load_fband(f"wch{(hoc + 1) % 2}", d_img, hoc + 1, f"ldd{tt}_{hoc + 1}")
            ps = ("a", "b", "s0", "s1")[hoc % 4]
            pf = hoc % 2
            for efc in range(NCK):
                em("tensor", lambda h, efc=efc, pw=pw, ps=ps: h.matmul(
                    PS[ps][:, :], A[f"wch{pw}"][:, efc * 128:(efc + 1) * 128],
                    A["gated"][:, efc * 512:(efc + 1) * 512],
                    start=(efc == 0), stop=False),
                   reads=[rg(f"wch{pw}", efc * 128, (efc + 1) * 128),
                          rg("gated", efc * 512, (efc + 1) * 512)],
                   writes=[prg(ps)], tag=f"dmm{tt}_{hoc}_{efc}")
            em("tensor", lambda h, hoc=hoc, tt=tt, ps=ps: h.matmul(
                PS[ps][:, :], A["lBTb"][:, hoc * 128:(hoc + 1) * 128],
                A["lrw"][:, tt * 512:(tt + 1) * 512], start=False, stop=True),
               reads=[rg("lBTb", hoc * 128, (hoc + 1) * 128),
                      rg("lrw", tt * 512, (tt + 1) * 512)],
               writes=[prg(ps)], tag=f"lbmm{tt}_{hoc}")
            em("vector", lambda h, hoc=hoc, tt=tt, pf=pf, ps=ps: h.tensor_tensor(
                A[f"fout{pf}"][:, :], PS[ps][:, :],
                A["y1"][:, hoc * TQ + tt * 512: hoc * TQ + (tt + 1) * 512], ALU.add),
               reads=[prg(ps), rg("y1", hoc * TQ + tt * 512, hoc * TQ + (tt + 1) * 512)],
               writes=[rg(f"fout{pf}")], tag=f"fres{tt}_{hoc}")
            em("sync", lambda h, hoc=hoc, tt=tt, pf=pf: h.dma_start(
                out=y_out[hoc * 128:(hoc + 1) * 128, tt * 512:(tt + 1) * 512],
                in_=A[f"fout{pf}"][:, :]),
               reads=[rg(f"fout{pf}")],
               writes=[("yout", (hoc * 2 + tt) * 512, (hoc * 2 + tt + 1) * 512)],
               dma=True, tag=f"st{tt}_{hoc}")

    em("gpsimd", lambda h: h.memset(A["joinpad"][:, 0:1], 0.0),
       reads=[("yout", 0, NCK * 2 * 512)], writes=[rg("joinpad")], tag="join")

    return materialize()


def _rope_tables(pos):
    j = np.arange(0, HD, 2, dtype=np.float32) / HD
    inv = (1.0 / 10000.0 ** j).astype(np.float32)
    fr = pos[:, None].astype(np.float32) * inv[None, :]
    cos = np.cos(fr).astype(np.float32)
    sin = np.sin(fr).astype(np.float32)
    cosT = np.ascontiguousarray(np.concatenate([cos, cos], axis=1).T)
    sinS = np.ascontiguousarray(np.concatenate([sin, -sin], axis=1).T)
    return cosT, sinS


def _band_img(T):
    nb = T.shape[1] // 128
    return np.ascontiguousarray(
        T.reshape(16, 128, nb, 128).transpose(2, 1, 0, 3).reshape(nb * 128, 2048))


def _vgroup_img(T):
    return np.ascontiguousarray(
        T.reshape(16, 128, 8, 256).transpose(2, 1, 0, 3).reshape(1024, 4096))


_NC_CACHE = [None]


def kernel(L0_attn_hidden_states, L0_attn_Wq, L0_attn_bq, L0_attn_Wk, L0_attn_bk,
           L0_attn_Wv, L0_attn_bv, L0_attn_Wo, L0_attn_bo, L0_ffn_G, L0_ffn_gate_w,
           L0_ffn_up_w, L0_ffn_down_w, L0_ffn_lora_A, L0_ffn_lora_B):
    x = np.ascontiguousarray(np.asarray(L0_attn_hidden_states, np.float32))
    wq_img = _band_img(np.asarray(L0_attn_Wq, np.float32).T)
    wk_img = _band_img(np.asarray(L0_attn_Wk, np.float32).T)
    wv_img = _vgroup_img(np.asarray(L0_attn_Wv, np.float32).T)
    wo_img = _band_img(np.asarray(L0_attn_Wo, np.float32).T)
    g_img = _band_img(np.asarray(L0_ffn_gate_w, np.float32).transpose(2, 0, 1).reshape(H, EF))
    u_img = _band_img(np.asarray(L0_ffn_up_w, np.float32).transpose(2, 0, 1).reshape(H, EF))
    d_img = _band_img(np.asarray(L0_ffn_down_w, np.float32).transpose(0, 2, 1).reshape(EF, H))
    bq = np.ascontiguousarray(np.asarray(L0_attn_bq, np.float32).reshape(16, 128).T)
    bk = np.ascontiguousarray(np.asarray(L0_attn_bk, np.float32).reshape(16, 128).T)
    bo = np.ascontiguousarray(np.asarray(L0_attn_bo, np.float32).reshape(16, 128).T)
    bv = np.ascontiguousarray(np.asarray(L0_attn_bv, np.float32).reshape(1, H))
    G = np.asarray(L0_ffn_G, np.float32)
    G_img = np.ascontiguousarray(G.reshape(16, 128, 8).transpose(1, 0, 2).reshape(128, 128))
    lAT = np.asarray(L0_ffn_lora_A, np.float32).transpose(2, 0, 1).reshape(H, 128)
    lAT_img = np.ascontiguousarray(
        lAT.reshape(16, 128, 128).transpose(1, 0, 2).reshape(128, 2048))
    lBT = np.ascontiguousarray(
        np.asarray(L0_ffn_lora_B, np.float32).transpose(0, 2, 1).reshape(128, H))

    if _NC_CACHE[0] is None or _NC_CACHE[0][0] != (PHASE_LIMIT, tuple(DEBUG_DUMP)):
        _NC_CACHE[0] = ((PHASE_LIMIT, tuple(DEBUG_DUMP)), build_nc())
    nc = _NC_CACHE[0][1]

    in_maps = []
    perms = []
    for c in range(8):
        b, h2 = c // 2, c % 2
        chunks = [2 * j + h2 for j in range(8)] + [2 * j + (1 - h2) for j in range(8)]
        cols = np.concatenate([np.arange(t * 128, (t + 1) * 128) for t in chunks])
        perms.append(cols)
        xTc = np.ascontiguousarray(x[b].T[:, cols])
        cosk, sink = _rope_tables(cols.astype(np.float32))
        mcol = np.full((128, 1), MASKNEG if h2 == 0 else 0.0, np.float32)
        in_maps.append({
            "xT": xTc, "wq_img": wq_img, "wk_img": wk_img, "wv_img": wv_img,
            "wo_img": wo_img, "g_img": g_img, "u_img": u_img, "d_img": d_img,
            "lAT": lAT_img, "lBT": lBT, "G_img": G_img,
            "bq_img": bq, "bk_img": bk, "bo_img": bo, "bv_row": bv,
            "cosk": cosk, "sink": sink, "maskcol": mcol,
        })
    res = run_bass_kernel_spmd(nc, in_maps, core_ids=list(range(8)))
    y = np.zeros((B, S, H), np.float32)
    for c in range(8):
        b = c // 2
        yq = res.results[c]["y_out"]
        y[b, perms[c][:TQ], :] = yq.T
    if DEBUG_DUMP:
        kernel._dbg = [{k: v for k, v in r.items() if k.startswith("dbg_")}
                       for r in res.results]
    return y


# version 16
# speedup vs baseline: 1.4861x; 1.0087x over previous
import numpy as np
import ml_dtypes
import concourse.bass as bass
import concourse.mybir as mybir
from concourse.bass_utils import run_bass_kernel_spmd

F32 = mybir.dt.float32
BF16 = mybir.dt.bfloat16
AF = mybir.ActivationFunctionType
ALU = mybir.AluOpType
AX = mybir.AxisListType
bfd = ml_dtypes.bfloat16

H = 2048
S = 2048
B = 4
NH = 16
HD = 128
NCK = 16
E = 8
EF = 2048
EPS = 1e-5
TQ = 1024
RSQ = float(1.0 / np.sqrt(np.float32(HD)))
MASKNEG = -1.0e9

ENGS = ("sync", "scalar", "vector", "tensor", "gpsimd")

DEBUG_DUMP = []
PHASE_LIMIT = 99
OP_LIMIT = None


LANES = {"gpsimd": 12, "sync": 6}


class Op:
    __slots__ = ("eng", "fn", "dma", "eidx", "deps", "inc", "waits", "tag", "chan",
                 "cidx")

    def __init__(self, eng, fn, dma, eidx, tag):
        self.eng = eng
        self.fn = fn
        self.dma = dma
        self.eidx = eidx
        self.deps = set()
        self.inc = False
        self.waits = []
        self.tag = tag
        self.chan = eng
        self.cidx = 0


class Prog:
    def __init__(self):
        self.eops = {e: [] for e in ENGS}
        self.segs = {}
        self.nemit = 0

    def emit(self, eng, fn, reads=(), writes=(), dma=False, tag=""):
        self.nemit += 1
        if OP_LIMIT is not None and self.nemit > OP_LIMIT and tag not in ("stub_out", "join"):
            return None
        op = Op(eng, fn, dma, len(self.eops[eng]), tag)
        self.eops[eng].append(op)
        for (sp, lo, hi) in reads:
            segs = self.segs.setdefault(sp, [])
            for seg in segs:
                if seg[0] < hi and lo < seg[1]:
                    if seg[2] is not None:
                        op.deps.add(seg[2])
                    seg[3].add(op)
        for (sp, lo, hi) in writes:
            segs = self.segs.setdefault(sp, [])
            new = []
            for seg in segs:
                if seg[0] < hi and lo < seg[1]:
                    if seg[2] is not None:
                        op.deps.add(seg[2])
                    op.deps.update(seg[3])
                    if seg[0] < lo:
                        new.append([seg[0], lo, seg[2], set(seg[3])])
                    if hi < seg[1]:
                        new.append([hi, seg[1], seg[2], set(seg[3])])
                else:
                    new.append(seg)
            new.append([lo, hi, op, set()])
            self.segs[sp] = new
        op.deps.discard(op)
        return op

    def finalize(self):
        lane_prev = {}
        for e in ENGS:
            nl = LANES.get(e, 0)
            ctr = 0
            for op in self.eops[e]:
                if op.dma and nl:
                    lane = ctr % nl
                    ctr += 1
                    op.chan = f"{e}:L{lane}"
                    prev = lane_prev.get(op.chan)
                    if prev is not None:
                        op.deps.add(prev)
                    lane_prev[op.chan] = op
        chans = list(ENGS) + sorted({op.chan for e in ENGS for op in self.eops[e]
                                     if op.chan not in ENGS})
        cctr = {c: 0 for c in chans}
        for e in ENGS:
            for op in self.eops[e]:
                op.cidx = cctr[op.chan]
                cctr[op.chan] += 1
        for e in ENGS:
            for op in self.eops[e]:
                best = {}
                for d in op.deps:
                    if d.eng == "tensor" and op.eng == "tensor":
                        continue
                    if d is op:
                        continue
                    cur = best.get(d.chan)
                    if cur is None or d.cidx > cur.cidx:
                        best[d.chan] = d
                op.deps = best
        wm = {e: {c: -1 for c in chans} for e in ENGS}
        for e in ENGS:
            w = wm[e]
            for op in self.eops[e]:
                kept = []
                for d in sorted(op.deps.values(), key=lambda d: d.chan):
                    if d.cidx <= w[d.chan]:
                        continue
                    if d.chan == op.chan and d.cidx >= op.cidx:
                        raise RuntimeError(f"self-dep ahead: {op.tag} <- {d.tag}")
                    w[d.chan] = d.cidx
                    kept.append(d)
                    d.inc = True
                op.deps = kept
        for e in ENGS:
            for op in self.eops[e]:
                if op.dma:
                    op.inc = True
        ccnt = {c: 0 for c in chans}
        cumv = {}
        for e in ENGS:
            for op in self.eops[e]:
                if op.inc:
                    ccnt[op.chan] += 16 if op.dma else 1
                cumv[(op.chan, op.cidx)] = ccnt[op.chan]
        for e in ENGS:
            for op in self.eops[e]:
                op.waits = [(d.chan, cumv[(d.chan, d.cidx)]) for d in op.deps]
        self.chans = chans
        pos = {e: 0 for e in ENGS}
        cnt = {c: 0 for c in chans}
        progress = True
        while progress:
            progress = False
            for e in ENGS:
                while pos[e] < len(self.eops[e]):
                    op = self.eops[e][pos[e]]
                    if all(cnt[se] >= v for (se, v) in op.waits):
                        if op.inc:
                            cnt[op.chan] += 16 if op.dma else 1
                        pos[e] += 1
                        progress = True
                    else:
                        break
        stuck = [e for e in ENGS if pos[e] < len(self.eops[e])]
        if stuck:
            msgs = []
            for e in stuck:
                op = self.eops[e][pos[e]]
                msgs.append(f"{e}[{pos[e]}] {op.tag} waits {op.waits}")
            raise RuntimeError("DEADLOCK:\n" + "\n".join(msgs))
        return cumv


def build_nc(phase_limit=None):
    if phase_limit is None:
        phase_limit = PHASE_LIMIT
    nc = bass.Bass("TRN2", target_bir_lowering=False, debug=False)

    xT = nc.dram_tensor("xT", [H, S], F32, kind="ExternalInput").ap()
    wq_img = nc.dram_tensor("wq_img", [H, H], F32, kind="ExternalInput").ap()
    wk_img = nc.dram_tensor("wk_img", [H, H], F32, kind="ExternalInput").ap()
    wv_img = nc.dram_tensor("wv_img", [1024, 4096], F32, kind="ExternalInput").ap()
    wo_img = nc.dram_tensor("wo_img", [H, H], F32, kind="ExternalInput").ap()
    g_img = nc.dram_tensor("g_img", [H, H], F32, kind="ExternalInput").ap()
    u_img = nc.dram_tensor("u_img", [H, H], F32, kind="ExternalInput").ap()
    d_img = nc.dram_tensor("d_img", [H, H], F32, kind="ExternalInput").ap()
    lATd = nc.dram_tensor("lAT", [128, 2048], F32, kind="ExternalInput").ap()
    lBTd = nc.dram_tensor("lBT", [128, H], F32, kind="ExternalInput").ap()
    Gd = nc.dram_tensor("G_img", [128, 128], F32, kind="ExternalInput").ap()
    bqd = nc.dram_tensor("bq_img", [128, 16], F32, kind="ExternalInput").ap()
    bkd = nc.dram_tensor("bk_img", [128, 16], F32, kind="ExternalInput").ap()
    bod = nc.dram_tensor("bo_img", [128, 16], F32, kind="ExternalInput").ap()
    bvd = nc.dram_tensor("bv_row", [1, H], F32, kind="ExternalInput").ap()
    coskd = nc.dram_tensor("cosk", [128, S], F32, kind="ExternalInput").ap()
    sinkd = nc.dram_tensor("sink", [128, S], F32, kind="ExternalInput").ap()
    mcold = nc.dram_tensor("maskcol", [128, 1], F32, kind="ExternalInput").ap()
    y_out = nc.dram_tensor("y_out", [H, TQ], F32, kind="ExternalOutput").ap()

    def bfc(a):
        return np.ascontiguousarray(np.asarray(a, np.float32).astype(bfd))

    tri_np = (np.arange(128)[:, None] <= np.arange(128)[None, :]).astype(np.float32)
    tri_d = nc.inline_tensor(bfc(tri_np), "c_tri")
    id128_d = nc.inline_tensor(bfc(np.eye(128)), "c_id128")
    id8f_d = nc.inline_tensor(np.ascontiguousarray(np.eye(8, dtype=np.float32)), "c_id8f")
    eer_np = np.zeros((8, 128), np.float32)
    for e_ in range(8):
        eer_np[e_, e_ * 16:(e_ + 1) * 16] = 1.0
    eer_d = nc.inline_tensor(bfc(eer_np), "c_eer")
    e8s_np = np.zeros((8, 8 * 128), np.float32)
    for e_ in range(8):
        e8s_np[e_, e_ * 128:(e_ + 1) * 128] = 1.0
    e8s_d = nc.inline_tensor(bfc(e8s_np), "c_e8s")

    dbg = {}
    for name, shape, dt in [
        ("xb", [128, NCK * S], BF16), ("qp", [128, TQ], BF16),
        ("kp", [128, S], BF16), ("vp", [128, 16 * 129], BF16),
        ("ctx", [H, TQ], BF16), ("y1", [H, TQ], F32),
        ("ln2", [128, NCK * TQ], BF16), ("w8", [8, TQ], BF16),
        ("gated", [128, NCK * 512], BF16), ("lrw", [128, TQ], BF16),
    ]:
        if name in DEBUG_DUMP:
            dbg[name] = nc.dram_tensor("dbg_" + name, shape, dt, kind="ExternalOutput").ap()

    SBUF_BASE = 16512
    SBUF_LIMIT = 212864
    A = {}
    meta = {}

    def place(name, ncols, dt, off):
        esz = 2 if dt == BF16 else 4
        off = (off + 31) // 32 * 32
        nb = ncols * esz
        assert off + nb <= SBUF_LIMIT, (name, off, nb)
        A[name] = nc.alloc_sbuf_tensor_at(name, [128, ncols], dt, offset=SBUF_BASE + off)
        meta[name] = (off, esz, ncols)
        return off + nb

    o = 0
    for nm, ncols, dt in [
        ("cosk", S, BF16), ("sink", S, BF16), ("bvbc", H, BF16),
        ("lATb", 2048, BF16), ("lBTb", H, BF16), ("Gb", 128, BF16),
        ("tri", 128, BF16), ("id128", 128, BF16), ("id8f", 8, F32),
        ("eer", 128, BF16), ("e8s", 8 * 128, BF16), ("bqc", 16, F32), ("bkc", 16, F32), ("boc", 16, F32),
        ("mcol", 1, F32), ("onecb", 1, BF16), ("onerf", 128, F32),
        ("onerb", 128, BF16), ("mbc", S, BF16), ("ibc", S, BF16),
        ("joinpad", 16, F32),
    ]:
        o = place(nm, ncols, dt, o)
    REG_P_END = o
    REG_X = (REG_P_END + 31) // 32 * 32
    place("xb", NCK * S, BF16, REG_X)
    place("y1", NCK * TQ, F32, REG_X)
    REG_X_END = REG_X + NCK * TQ * 4
    REG_C = (REG_X_END + 31) // 32 * 32
    place("ctxsb", NCK * TQ, BF16, REG_C)
    place("ln2", NCK * TQ, BF16, REG_C)
    REG_C_END = REG_C + NCK * TQ * 2
    o = (REG_C_END + 31) // 32 * 32
    REG_W = o
    for nm, ncols, dt in [
        ("wqh0", 2048, BF16), ("wqh1", 2048, BF16),
        ("wkh0", 2048, BF16), ("wkh1", 2048, BF16),
        ("wvg0", 4096, BF16), ("wvg1", 4096, BF16),
        ("vp0", 16 * 129, BF16), ("vp1", 16 * 129, BF16),
        ("qp0", TQ, BF16), ("qp1", TQ, BF16),
        ("kp0", S, BF16), ("kp1", S, BF16),
        ("pt0", TQ, BF16), ("pt1", TQ, BF16), ("pt2", TQ, BF16),
        ("qktmp", S, BF16), ("r3", S, BF16),
        ("ctxq", 256, BF16), ("rcol", 8, F32),
        ("rows", 1024, F32), ("rows2", 1024, F32), ("rsc", 512, F32),
        ("sq0", 512, BF16), ("sq1", 512, BF16), ("sq2", 512, BF16),
    ]:
        o = place(nm, ncols, dt, o)
    W12_END = o
    o = REG_W
    for nm, ncols, dt in [
        ("gated", NCK * 512, BF16),
        ("wch0", 2048, BF16), ("wch1", 2048, BF16),
        ("wgh0", 2048, BF16), ("wgh1", 2048, BF16),
        ("wuh0", 2048, BF16), ("wuh1", 2048, BF16),
        ("xqs0", TQ, F32), ("xqs1", TQ, F32),
        ("fout0", 512, F32), ("fout1", 512, F32),
        ("ytmp0", 512, F32), ("ytmp1", 512, F32),
        ("wbce0", 512, BF16), ("wbce1", 512, BF16),
        ("wer", TQ, BF16), ("lrw", TQ, BF16),
        ("logsb", 1024, F32), ("tk", 64, F32), ("wtk", 8, BF16),
        ("w8sb", TQ, BF16), ("sil", TQ, BF16),
    ]:
        o = place(nm, ncols, dt, o)
    W36_END = o
    assert max(W12_END, W36_END) <= SBUF_LIMIT, (W12_END, W36_END)

    PS = {}
    psmeta = {}
    psoff = [0]

    def psalloc(name, ncols, dt):
        t = nc.alloc_psum_tensor("ps_" + name, [128, ncols], dt)
        PS[name] = t
        esz = 2 if dt == BF16 else 4
        psmeta[name] = (psoff[0], esz)
        psoff[0] += 2048 * ((ncols * esz + 2047) // 2048)

    psalloc("a", 512, F32)
    psalloc("b", 512, F32)
    psalloc("s0", 512, F32)
    psalloc("s1", 512, F32)
    psalloc("c0", 512, F32)
    psalloc("c1", 512, F32)
    psalloc("t", 128, BF16)
    psalloc("t3", 128, BF16)
    assert psoff[0] <= 8 * 2048, psoff[0]

    def rg(name, c0=None, c1=None):
        off, esz, ncols = meta[name]
        if c0 is None:
            c0, c1 = 0, ncols
        return ("sbuf", off + c0 * esz, off + c1 * esz)

    def prg(name, c0=None, c1=None):
        off, esz = psmeta[name]
        if c0 is None:
            return ("psum", off, off + 2048)
        return ("psum", off + c0 * esz, off + c1 * esz)

    P = Prog()
    em = P.emit

    def materialize():
        P.finalize()
        import contextlib
        with contextlib.ExitStack() as stack:
            sems = {}
            for c in P.chans:
                sems[c] = stack.enter_context(nc.semaphore("sem_" + c.replace(":", "_")))
            block = stack.enter_context(nc.Block())

            def runner(eng):
                def run(h):
                    for op in P.eops[eng]:
                        for (se, v) in op.waits:
                            h.wait_ge(sems[se], v)
                        inst = op.fn(h)
                        if op.inc:
                            inst.then_inc(sems[op.chan], 16 if op.dma else 1)
                return run

            block.sync(runner("sync"))
            block.scalar(runner("scalar"))
            block.vector(runner("vector"))
            block.tensor(runner("tensor"))
            block.gpsimd(runner("gpsimd"))
        return nc

    def stop_after(k):
        if phase_limit > k:
            return False
        em("gpsimd", lambda h: h.dma_start(out=y_out[0:128, :], in_=A["mbc"][:, 0:TQ]),
           reads=[rg("mbc", 0, TQ)], writes=[("yout", 0, 512)], dma=True, tag="stub_out")
        em("gpsimd", lambda h: h.memset(A["joinpad"][:, 0:1], 0.0),
           reads=[("yout", 0, 512)], writes=[rg("joinpad")], tag="join")
        return True

    for nm, src, np_ in [("tri", tri_d.ap(), 128), ("id128", id128_d.ap(), 128),
                         ("eer", eer_d.ap(), 8), ("e8s", e8s_d.ap(), 8)]:
        em("sync", lambda h, nm=nm, src=src, np_=np_: h.dma_start(out=A[nm][0:np_, :], in_=src),
           writes=[rg(nm)], dma=True, tag="ld_" + nm)
    em("sync", lambda h: h.dma_start(out=A["id8f"][0:8, :], in_=id8f_d.ap()),
       writes=[rg("id8f")], dma=True, tag="ld_id8f")
    for nm, src in [("bqc", bqd), ("bkc", bkd), ("boc", bod), ("mcol", mcold)]:
        em("sync", lambda h, nm=nm, src=src: h.dma_start(out=A[nm][:], in_=src[:, :]),
           writes=[rg(nm)], dma=True, tag="ld_" + nm)
    em("sync", lambda h: h.dma_start(out=A["rows"][0:1, :], in_=bvd[0:1, 0:1024]),
       writes=[rg("rows")], dma=True, tag="ld_bv0")
    em("sync", lambda h: h.dma_start(out=A["rows2"][0:1, :], in_=bvd[0:1, 1024:2048]),
       writes=[rg("rows2")], dma=True, tag="ld_bv1")
    for nm, src in [("cosk", coskd), ("sink", sinkd), ("Gb", Gd)]:
        em("gpsimd", lambda h, nm=nm, src=src: h.dma_start(out=A[nm][:], in_=src[:, :]),
           writes=[rg(nm)], dma=True, tag="ld_" + nm)
    em("gpsimd", lambda h: h.dma_start(out=A["lATb"][:], in_=lATd[:, :]),
       writes=[rg("lATb")], dma=True, tag="ld_lAT")
    em("gpsimd", lambda h: h.dma_start(out=A["lBTb"][:], in_=lBTd[:, :]),
       writes=[rg("lBTb")], dma=True, tag="ld_lBT")
    em("vector", lambda h: h.memset(A["onecb"][:], 1.0), writes=[rg("onecb")], tag="ms1")
    em("vector", lambda h: h.memset(A["onerf"][0:1, :], 1.0), writes=[rg("onerf")], tag="ms2")
    em("vector", lambda h: h.memset(A["onerb"][0:1, :], 1.0), writes=[rg("onerb")], tag="ms3")
    for p in range(2):
        em("vector", lambda h, p=p: h.memset(A[f"vp{p}"][:, 128:16 * 129:129], 1.0),
           writes=[rg(f"vp{p}")], tag=f"ms_vp{p}")
    for g in range(4):
        src = "rows" if g < 2 else "rows2"
        sc0 = (g % 2) * 512
        ps = ("a", "b")[g % 2]
        em("tensor", lambda h, src=src, sc0=sc0, ps=ps: h.matmul(
            PS[ps][:, :], A["onerf"][0:1, :], A[src][0:1, sc0:sc0 + 512],
            start=True, stop=True),
           reads=[rg("onerf"), rg(src, sc0, sc0 + 512)], writes=[prg(ps)], tag=f"bvb{g}")
        em("scalar", lambda h, g=g, ps=ps: h.copy(
            A["bvbc"][:, g * 512:(g + 1) * 512], PS[ps][:, :]),
           reads=[prg(ps)], writes=[rg("bvbc", g * 512, (g + 1) * 512)], tag=f"bve{g}")

    if stop_after(0):
        return materialize()

    for ic in range(NCK):
        em("gpsimd", lambda h, ic=ic: h.dma_start(
            out=A["xb"][:, ic * S:(ic + 1) * S], in_=xT[ic * 128:(ic + 1) * 128, :]),
           writes=[rg("xb", ic * S, (ic + 1) * S)], dma=True, tag=f"ld_x{ic}")

    def ln_stats_center(buf, bufstride, ntok, mdst, tag, upto=3):
        nhalf = ntok // 1024
        for hf in range(max(nhalf, 1)):
            if upto < 1:
                break
            t0 = hf * 1024
            tn = min(1024, ntok - t0)
            for ic in range(NCK):
                for g in range(2):
                    if g * 512 >= tn:
                        continue
                    sqb = f"sq{(ic * 2 + g) % 3}"
                    ps = ("a", "b")[g]
                    ps2 = ("c0", "c1")[g]
                    c0 = t0 + g * 512
                    em("vector", lambda h, ic=ic, c0=c0, sqb=sqb: h.tensor_tensor(
                        A[sqb][:, 0:512],
                        A[buf][:, ic * bufstride + c0: ic * bufstride + c0 + 512],
                        A[buf][:, ic * bufstride + c0: ic * bufstride + c0 + 512], ALU.mult),
                       reads=[rg(buf, ic * bufstride + c0, ic * bufstride + c0 + 512)],
                       writes=[rg(sqb, 0, 512)], tag=f"{tag}sq{hf}{g}_{ic}")
                    em("tensor", lambda h, ic=ic, c0=c0, ps=ps: h.matmul(
                        PS[ps][0:1, :], A["onecb"][:, :],
                        A[buf][:, ic * bufstride + c0: ic * bufstride + c0 + 512],
                        start=(ic == 0), stop=(ic == NCK - 1)),
                       reads=[rg("onecb"), rg(buf, ic * bufstride + c0, ic * bufstride + c0 + 512)],
                       writes=[prg(ps, 0, 512)], tag=f"{tag}sx{hf}{g}_{ic}")
                    em("tensor", lambda h, ic=ic, sqb=sqb, ps2=ps2: h.matmul(
                        PS[ps2][0:1, :], A["onecb"][:, :], A[sqb][:, 0:512],
                        start=(ic == 0), stop=(ic == NCK - 1)),
                       reads=[rg("onecb"), rg(sqb, 0, 512)],
                       writes=[prg(ps2, 0, 512)], tag=f"{tag}s2{hf}{g}_{ic}")
            if upto < 2:
                continue
            for g in range(2):
                if g * 512 >= tn:
                    continue
                ps = ("a", "b")[g]
                ps2 = ("c0", "c1")[g]
                gc = g * 512
                em("vector", lambda h, gc=gc, ps=ps: h.tensor_scalar(
                    A["rows"][0:1, gc:gc + 512], PS[ps][0:1, :], 1.0 / H, None, ALU.mult),
                   reads=[prg(ps, 0, 512)], writes=[rg("rows", gc, gc + 512)],
                   tag=f"{tag}m{hf}{g}")
                em("vector", lambda h, gc=gc: h.tensor_tensor(
                    A["rsc"][0:1, :], A["rows"][0:1, gc:gc + 512],
                    A["rows"][0:1, gc:gc + 512], ALU.mult),
                   reads=[rg("rows", gc, gc + 512)], writes=[rg("rsc")], tag=f"{tag}ms{hf}{g}")
                em("vector", lambda h, gc=gc, ps2=ps2: h.tensor_scalar(
                    A["rows2"][0:1, gc:gc + 512], PS[ps2][0:1, :], 1.0 / H, None, ALU.mult),
                   reads=[prg(ps2, 0, 512)], writes=[rg("rows2", gc, gc + 512)],
                   tag=f"{tag}v{hf}{g}")
                em("vector", lambda h, gc=gc: h.tensor_tensor(
                    A["rows2"][0:1, gc:gc + 512], A["rows2"][0:1, gc:gc + 512],
                    A["rsc"][0:1, :], ALU.subtract),
                   reads=[rg("rows2", gc, gc + 512), rg("rsc")],
                   writes=[rg("rows2", gc, gc + 512)], tag=f"{tag}v2{hf}{g}")
                em("vector", lambda h, gc=gc: h.tensor_scalar(
                    A["rows2"][0:1, gc:gc + 512], A["rows2"][0:1, gc:gc + 512],
                    float(EPS), None, ALU.add),
                   reads=[rg("rows2", gc, gc + 512)], writes=[rg("rows2", gc, gc + 512)],
                   tag=f"{tag}v3{hf}{g}")
                em("scalar", lambda h, gc=gc: h.activation(
                    A["rsc"][0:1, :], A["rows2"][0:1, gc:gc + 512], AF.Sqrt,
                    bias=0.0, scale=1.0),
                   reads=[rg("rows2", gc, gc + 512)], writes=[rg("rsc")], tag=f"{tag}sd{hf}{g}")
                em("vector", lambda h, gc=gc: h.reciprocal(
                    A["rows2"][0:1, gc:gc + 512], A["rsc"][0:1, :]),
                   reads=[rg("rsc")], writes=[rg("rows2", gc, gc + 512)], tag=f"{tag}i{hf}{g}")
                for (src, dst) in (("rows", "mbc"), ("rows2", "ibc")):
                    ps3 = (("a", "c0"), ("b", "c1"))[g][src == "rows2"]
                    em("tensor", lambda h, gc=gc, src=src, ps3=ps3: h.matmul(
                        PS[ps3][:, :], A["onerf"][0:1, :], A[src][0:1, gc:gc + 512],
                        start=True, stop=True),
                       reads=[rg("onerf"), rg(src, gc, gc + 512)], writes=[prg(ps3)],
                       tag=f"{tag}bc{hf}{g}{dst}")
                    em("scalar", lambda h, t0=t0, gc=gc, dst=dst, ps3=ps3: h.copy(
                        A[dst][:, t0 + gc: t0 + gc + 512], PS[ps3][:, :]),
                       reads=[prg(ps3)], writes=[rg(dst, t0 + gc, t0 + gc + 512)],
                       tag=f"{tag}be{hf}{g}{dst}")
        if upto < 3:
            return
        for ic in range(NCK):
            em("vector", lambda h, ic=ic: h.tensor_tensor(
                A[buf][:, ic * bufstride: ic * bufstride + ntok],
                A[buf][:, ic * bufstride: ic * bufstride + ntok],
                A["mbc"][:, 0:ntok], ALU.subtract),
               reads=[rg(buf, ic * bufstride, ic * bufstride + ntok), rg("mbc", 0, ntok)],
               writes=[rg(buf, ic * bufstride, ic * bufstride + ntok)], tag=f"{tag}ce{ic}")
            em("vector", lambda h, ic=ic: h.tensor_tensor(
                A[buf][:, ic * bufstride: ic * bufstride + ntok],
                A[buf][:, ic * bufstride: ic * bufstride + ntok],
                A["ibc"][:, 0:ntok], ALU.mult),
               reads=[rg(buf, ic * bufstride, ic * bufstride + ntok), rg("ibc", 0, ntok)],
               writes=[rg(buf, ic * bufstride, ic * bufstride + ntok)], tag=f"{tag}cs{ic}")

    if stop_after(0.2):
        return materialize()
    ln_stats_center("xb", S, S, "mbc", "ln1",
                    upto=(0 if phase_limit <= 0.4 else 1 if phase_limit <= 0.6
                          else 2 if phase_limit <= 0.8 else 3))
    if "xb" in dbg:
        em("sync", lambda h: h.dma_start(out=dbg["xb"][:, :], in_=A["xb"][:, :]),
           reads=[rg("xb")], dma=True, tag="dbg_xb")

    if stop_after(1):
        return materialize()

    def load_head_weights(hd):
        p = hd % 2
        em("gpsimd", lambda h, hd=hd, p=p: h.dma_start(
            out=A[f"wqh{p}"][:, :], in_=wq_img[hd * 128:(hd + 1) * 128, :]),
           writes=[rg(f"wqh{p}")], dma=True, tag=f"ldwq{hd}")
        em("gpsimd", lambda h, hd=hd, p=p: h.dma_start(
            out=A[f"wkh{p}"][:, :], in_=wk_img[hd * 128:(hd + 1) * 128, :]),
           writes=[rg(f"wkh{p}")], dma=True, tag=f"ldwk{hd}")
        if hd % 2 == 0:
            g = hd // 2
            pv = g % 2
            em("gpsimd", lambda h, g=g, pv=pv: h.dma_start(
                out=A[f"wvg{pv}"][:, :], in_=wv_img[g * 128:(g + 1) * 128, :]),
               writes=[rg(f"wvg{pv}")], dma=True, tag=f"ldwv{g}")

    PROT = [0]
    PEND_EVAC = [None]
    PEND_PRE = [None]

    def proj_qk(hd, kind):
        p = hd % 2
        wb = f"wqh{p}" if kind == "q" else f"wkh{p}"
        dst = f"qp{p}" if kind == "q" else f"kp{p}"
        bias = "bqc" if kind == "q" else "bkc"
        ntt = 2 if kind == "q" else 4
        n = ntt * 512
        for tt in range(ntt):
            ps = ("a", "b")[(PROT[0] + tt) % 2]
            for ic in range(NCK):
                em("tensor", lambda h, ic=ic, tt=tt, wb=wb, ps=ps: h.matmul(
                    PS[ps][:, :], A[wb][:, ic * 128:(ic + 1) * 128],
                    A["xb"][:, ic * S + tt * 512: ic * S + (tt + 1) * 512],
                    start=(ic == 0), stop=(ic == NCK - 1)),
                   reads=[rg(wb, ic * 128, (ic + 1) * 128),
                          rg("xb", ic * S + tt * 512, ic * S + (tt + 1) * 512)],
                   writes=[prg(ps)], tag=f"{kind}mm{hd}_{tt}_{ic}")
            em("vector", lambda h, tt=tt, hd=hd, bias=bias, ps=ps: h.tensor_scalar(
                A["qktmp"][:, tt * 512:(tt + 1) * 512], PS[ps][:, :],
                A[bias][:, hd:hd + 1], None, ALU.add),
               reads=[prg(ps), rg(bias, hd, hd + 1)],
               writes=[rg("qktmp", tt * 512, (tt + 1) * 512)], tag=f"{kind}ev{hd}_{tt}")
        PROT[0] = (PROT[0] + ntt) % 2
        em("vector", lambda h, dst=dst, n=n: h.tensor_tensor(
            A[dst][:, 0:n], A["qktmp"][:, 0:n], A["cosk"][:, 0:n], ALU.mult),
           reads=[rg("qktmp", 0, n), rg("cosk", 0, n)], writes=[rg(dst, 0, n)],
           tag=f"{kind}rc{hd}")
        em("vector", lambda h, n=n: h.tensor_tensor(
            A["r3"][0:64, 0:n], A["qktmp"][64:128, 0:n], A["sink"][64:128, 0:n], ALU.mult),
           reads=[rg("qktmp", 0, n), rg("sink", 0, n)], writes=[rg("r3", 0, n)],
           tag=f"{kind}rs1{hd}")
        em("vector", lambda h, n=n: h.tensor_tensor(
            A["r3"][64:128, 0:n], A["qktmp"][0:64, 0:n], A["sink"][0:64, 0:n], ALU.mult),
           reads=[rg("qktmp", 0, n), rg("sink", 0, n)], writes=[rg("r3", 0, n)],
           tag=f"{kind}rs2{hd}")
        em("vector", lambda h, dst=dst, n=n: h.tensor_tensor(
            A[dst][:, 0:n], A[dst][:, 0:n], A["r3"][:, 0:n], ALU.add),
           reads=[rg(dst, 0, n), rg("r3", 0, n)], writes=[rg(dst, 0, n)],
           tag=f"{kind}ra{hd}")

    def proj_v(hd):
        g = hd // 2
        pv = g % 2
        for tc in range(NCK):
            ps = ("a", "b")[(PROT[0] + tc) % 2]
            for ic in range(NCK):
                em("tensor", lambda h, ic=ic, tc=tc, pv=pv, ps=ps: h.matmul(
                    PS[ps][:, 0:256], A["xb"][:, ic * S + tc * 128: ic * S + (tc + 1) * 128],
                    A[f"wvg{pv}"][:, ic * 256:(ic + 1) * 256],
                    start=(ic == 0), stop=(ic == NCK - 1)),
                   reads=[rg("xb", ic * S + tc * 128, ic * S + (tc + 1) * 128),
                          rg(f"wvg{pv}", ic * 256, (ic + 1) * 256)],
                   writes=[prg(ps, 0, 256)], tag=f"vmm{g}_{tc}_{ic}")
            for sub in range(2):
                hh = g * 2 + sub
                pw = hh % 2
                em("vector", lambda h, tc=tc, sub=sub, hh=hh, pw=pw, ps=ps: h.tensor_tensor(
                    A[f"vp{pw}"][:, tc * 129: tc * 129 + 128],
                    PS[ps][:, sub * 128:(sub + 1) * 128],
                    A["bvbc"][:, hh * 128:(hh + 1) * 128], ALU.add),
                   reads=[prg(ps, sub * 128, (sub + 1) * 128),
                          rg("bvbc", hh * 128, (hh + 1) * 128)],
                   writes=[rg(f"vp{pw}", tc * 129, tc * 129 + 128)],
                   tag=f"vev{hh}_{tc}")
        PROT[0] = (PROT[0] + NCK) % 2

    def attn(hd):
        p = hd % 2
        tilectr = [0]
        for grp in range(4):
            ctx_slot = [("c0", 0), ("c1", 0)]
            slots = list(range(grp * 2, grp * 2 + 2))
            kcs = [kc for kc in range(16) if (kc % 8) <= slots[-1]]
            pend = []
            kcount = 0
            for kc in kcs:
                j0 = max(kc % 8, slots[0])
                if j0 > slots[-1]:
                    continue
                n = (slots[-1] - j0 + 1) * 128
                q0 = j0 * 128
                sl = tilectr[0] % 2
                tilectr[0] += 1
                pp = f"pt{sl}"
                ps, so = (("s0", 0), ("s1", 0))[sl]
                w = n
                em("tensor", lambda h, kc=kc, q0=q0, w=w, p=p, ps=ps, so=so: h.matmul(
                    PS[ps][:, so:so + w], A[f"kp{p}"][:, kc * 128:(kc + 1) * 128],
                    A[f"qp{p}"][:, q0: q0 + w], start=True, stop=True),
                   reads=[rg(f"kp{p}", kc * 128, (kc + 1) * 128),
                          rg(f"qp{p}", q0, q0 + w)],
                   writes=[prg(ps, so, so + w)], tag=f"sc{hd}_{kc}")
                if kc >= 8 and j0 == kc - 8:
                    em("scalar", lambda h, ps=ps, pp=pp, q0=q0, so=so: h.activation(
                        A[pp][:, q0: q0 + 128], PS[ps][:, so:so + 128], AF.Exp,
                        bias=A["mcol"][:, 0:1], scale=RSQ),
                       reads=[prg(ps, so, so + 128), rg("mcol")],
                       writes=[rg(pp, q0, q0 + 128)], tag=f"exm{hd}_{kc}")
                    if w > 128:
                        em("scalar", lambda h, ps=ps, pp=pp, q0=q0, w=w, so=so: h.activation(
                            A[pp][:, q0 + 128: q0 + w], PS[ps][:, so + 128:so + w], AF.Exp,
                            bias=0.0, scale=RSQ),
                           reads=[prg(ps, so + 128, so + w)],
                           writes=[rg(pp, q0 + 128, q0 + w)], tag=f"exr{hd}_{kc}")
                else:
                    em("scalar", lambda h, ps=ps, pp=pp, q0=q0, w=w, so=so: h.activation(
                        A[pp][:, q0: q0 + w], PS[ps][:, so:so + w], AF.Exp,
                        bias=0.0, scale=RSQ),
                       reads=[prg(ps, so, so + w)], writes=[rg(pp, q0, q0 + w)],
                       tag=f"ex{hd}_{kc}")
                if kc < 8 and j0 == kc:
                    em("vector", lambda h, pp=pp, q0=q0: h.tensor_tensor(
                        A[pp][:, q0: q0 + 128], A[pp][:, q0: q0 + 128],
                        A["tri"][:, :], ALU.mult),
                       reads=[rg(pp, q0, q0 + 128), rg("tri")],
                       writes=[rg(pp, q0, q0 + 128)], tag=f"tri{hd}_{kc}")

                def ctx_emit(kc=kc, j0=j0, w=w, pp=pp):
                    for j in range(j0, j0 + w // 128):
                        bank, boff = ctx_slot[j % 2]
                        em("tensor", lambda h, j=j, kc=kc, p=p, pp=pp, bank=bank,
                           boff=boff: h.matmul(
                            PS[bank][:, boff:boff + 129], A[pp][:, j * 128:(j + 1) * 128],
                            A[f"vp{p}"][:, kc * 129:(kc + 1) * 129],
                            start=(kc == 0), stop=(kc == 8 + j)),
                           reads=[rg(pp, j * 128, (j + 1) * 128),
                                  rg(f"vp{p}", kc * 129, (kc + 1) * 129)],
                           writes=[prg(bank, boff, boff + 129)], tag=f"cx{hd}_{kc}_{j}")
                kcount += 1
                if kcount == 2 and PEND_EVAC[0] is not None:
                    PEND_EVAC[0]()
                    PEND_EVAC[0] = None
                    if PEND_PRE[0] is not None:
                        PEND_PRE[0]()
                        PEND_PRE[0] = None
                pend.append(ctx_emit)
                if len(pend) > 1:
                    pend.pop(0)()
            for f in pend:
                f()
            if PEND_EVAC[0] is not None:
                PEND_EVAC[0]()
                PEND_EVAC[0] = None

            def grp_evac(slots=slots, hd=hd, ctx_slot=tuple(ctx_slot)):
                for j in slots:
                    bank, boff = ctx_slot[j % 2]
                    em("vector", lambda h, bank=bank, boff=boff, j=j: h.reciprocal(
                        A["rcol"][:, (j % 2):(j % 2) + 1], PS[bank][:, boff + 128:boff + 129]),
                       reads=[prg(bank, boff + 128, boff + 129)],
                       writes=[rg("rcol", j % 2, j % 2 + 1)], tag=f"rc{hd}_{j}")
                    em("vector", lambda h, bank=bank, boff=boff, j=j, hd=hd: h.tensor_scalar(
                        A["ctxsb"][:, hd * TQ + j * 128: hd * TQ + (j + 1) * 128],
                        PS[bank][:, boff:boff + 128], A["rcol"][:, (j % 2):(j % 2) + 1],
                        None, ALU.mult),
                       reads=[prg(bank, boff, boff + 128), rg("rcol", j % 2, j % 2 + 1)],
                       writes=[rg("ctxsb", hd * TQ + j * 128, hd * TQ + (j + 1) * 128)],
                       tag=f"nm{hd}_{j}")
            PEND_EVAC[0] = grp_evac

    def mk_prepass(hd):
        def prepass():
            for j in range(8):
                bank = ("t", "t3")[j % 2]
                em("tensor", lambda h, j=j, hd=hd, bank=bank: h.transpose(
                    PS[bank][:, 0:128],
                    A["ctxsb"][:, hd * TQ + j * 128: hd * TQ + (j + 1) * 128],
                    A["id128"][:, :]),
                   reads=[rg("ctxsb", hd * TQ + j * 128, hd * TQ + (j + 1) * 128),
                          rg("id128")],
                   writes=[prg(bank)], tag=f"tp{hd}_{j}")
                eng = ("scalar", "vector")[j % 2]
                if eng == "scalar":
                    em("scalar", lambda h, hd=hd, j=j, bank=bank: h.copy(
                        A["ctxsb"][:, hd * TQ + j * 128: hd * TQ + (j + 1) * 128],
                        PS[bank][:, 0:128]),
                       reads=[prg(bank)],
                       writes=[rg("ctxsb", hd * TQ + j * 128, hd * TQ + (j + 1) * 128)],
                       tag=f"cte{hd}_{j}")
                else:
                    em("vector", lambda h, hd=hd, j=j, bank=bank: h.tensor_copy(
                        A["ctxsb"][:, hd * TQ + j * 128: hd * TQ + (j + 1) * 128],
                        PS[bank][:, 0:128]),
                       reads=[prg(bank)],
                       writes=[rg("ctxsb", hd * TQ + j * 128, hd * TQ + (j + 1) * 128)],
                       tag=f"cte{hd}_{j}")
        return prepass

    load_head_weights(0)
    load_head_weights(1)
    proj_qk(0, "q")
    proj_qk(0, "k")
    for hd in range(NH):
        if hd + 2 < NH:
            load_head_weights(hd + 2)
        if hd + 1 < NH:
            proj_qk(hd + 1, "q")
            proj_qk(hd + 1, "k")
        if hd % 2 == 0:
            proj_v(hd)
        attn(hd)
        PEND_PRE[0] = mk_prepass(hd)
        if hd == 0:
            if "qp" in dbg:
                em("sync", lambda h: h.dma_start(out=dbg["qp"][:, :], in_=A["qp0"][:, :]),
                   reads=[rg("qp0")], dma=True, tag="dbg_qp")
            if "kp" in dbg:
                em("sync", lambda h: h.dma_start(out=dbg["kp"][:, :], in_=A["kp0"][:, :]),
                   reads=[rg("kp0")], dma=True, tag="dbg_kp")
            if "vp" in dbg:
                em("sync", lambda h: h.dma_start(out=dbg["vp"][:, :], in_=A["vp0"][:, :]),
                   reads=[rg("vp0")], dma=True, tag="dbg_vp")
    PEND_EVAC[0]()
    PEND_EVAC[0] = None
    PEND_PRE[0]()
    PEND_PRE[0] = None
    if "ctx" in dbg:
        for ic in range(NCK):
            em("sync", lambda h, ic=ic: h.dma_start(
                out=dbg["ctx"][ic * 128:(ic + 1) * 128, :],
                in_=A["ctxsb"][:, ic * TQ:(ic + 1) * TQ]),
               reads=[rg("ctxsb", ic * TQ, (ic + 1) * TQ)], dma=True, tag=f"dbg_ctx{ic}")

    if stop_after(2):
        return materialize()

    def load_band(dst, dram, band, tagp):
        em("gpsimd", lambda h, dst=dst, dram=dram, band=band: h.dma_start(
            out=A[dst][:, :], in_=dram[band * 128:(band + 1) * 128, :]),
           writes=[rg(dst)], dma=True, tag=tagp)

    load_band("wch0", wo_img, 0, "ldwo0")
    for hoc in range(NCK):
        pw = hoc % 2
        if hoc + 1 < NCK:
            load_band(f"wch{(hoc + 1) % 2}", wo_img, hoc + 1, f"ldwo{hoc + 1}")
        em("sync", lambda h, hoc=hoc, pw=pw: h.dma_start(
            out=A[f"xqs{pw}"][:, :], in_=xT[hoc * 128:(hoc + 1) * 128, 0:TQ]),
           writes=[rg(f"xqs{pw}")], dma=True, tag=f"ldxq{hoc}")
        for tt in range(2):
            ps = ("a", "b")[tt]
            pf = tt
            for ic in range(NCK):
                em("tensor", lambda h, ic=ic, tt=tt, pw=pw, ps=ps: h.matmul(
                    PS[ps][:, :], A[f"wch{pw}"][:, ic * 128:(ic + 1) * 128],
                    A["ctxsb"][:, ic * TQ + tt * 512: ic * TQ + (tt + 1) * 512],
                    start=(ic == 0), stop=(ic == NCK - 1)),
                   reads=[rg(f"wch{pw}", ic * 128, (ic + 1) * 128),
                          rg("ctxsb", ic * TQ + tt * 512, ic * TQ + (tt + 1) * 512)],
                   writes=[prg(ps)], tag=f"omm{hoc}_{tt}_{ic}")
            em("scalar", lambda h, hoc=hoc, pf=pf, ps=ps: h.activation(
                A[f"ytmp{pf}"][:, :], PS[ps][:, :], AF.Identity,
                bias=A["boc"][:, hoc:hoc + 1], scale=1.0),
               reads=[prg(ps), rg("boc", hoc, hoc + 1)],
               writes=[rg(f"ytmp{pf}")], tag=f"oev{hoc}_{tt}")
            em("vector", lambda h, hoc=hoc, tt=tt, pw=pw, pf=pf: h.tensor_tensor(
                A["y1"][:, hoc * TQ + tt * 512: hoc * TQ + (tt + 1) * 512],
                A[f"ytmp{pf}"][:, :], A[f"xqs{pw}"][:, tt * 512:(tt + 1) * 512], ALU.add),
               reads=[rg(f"ytmp{pf}"), rg(f"xqs{pw}", tt * 512, (tt + 1) * 512)],
               writes=[rg("y1", hoc * TQ + tt * 512, hoc * TQ + (tt + 1) * 512)],
               tag=f"ores{hoc}_{tt}")
    if "y1" in dbg:
        for ic in range(NCK):
            em("sync", lambda h, ic=ic: h.dma_start(
                out=dbg["y1"][ic * 128:(ic + 1) * 128, :],
                in_=A["y1"][:, ic * TQ:(ic + 1) * TQ]),
               reads=[rg("y1", ic * TQ, (ic + 1) * TQ)], dma=True, tag=f"dbg_y1{ic}")

    if stop_after(3):
        return materialize()

    for ic in range(NCK):
        em("vector", lambda h, ic=ic: h.tensor_copy(
            A["ln2"][:, ic * TQ:(ic + 1) * TQ], A["y1"][:, ic * TQ:(ic + 1) * TQ]),
           reads=[rg("y1", ic * TQ, (ic + 1) * TQ)],
           writes=[rg("ln2", ic * TQ, (ic + 1) * TQ)], tag=f"l2c{ic}")
    ln_stats_center("ln2", TQ, TQ, "mbc", "ln2")
    if "ln2" in dbg:
        em("sync", lambda h: h.dma_start(out=dbg["ln2"][:, :], in_=A["ln2"][:, :]),
           reads=[rg("ln2")], dma=True, tag="dbg_ln2")

    if stop_after(4):
        return materialize()

    TK = A["tk"]
    for tt in range(2):
        ps = ("a", "b")[tt]
        for ic in range(NCK):
            em("tensor", lambda h, ic=ic, tt=tt, ps=ps: h.matmul(
                PS[ps][0:8, :], A["Gb"][:, ic * 8:(ic + 1) * 8],
                A["ln2"][:, ic * TQ + tt * 512: ic * TQ + (tt + 1) * 512],
                start=(ic == 0), stop=(ic == NCK - 1)),
               reads=[rg("Gb", ic * 8, (ic + 1) * 8),
                      rg("ln2", ic * TQ + tt * 512, ic * TQ + (tt + 1) * 512)],
               writes=[prg(ps, 0, 512)], tag=f"lg{tt}_{ic}")
        em("scalar", lambda h, tt=tt, ps=ps: h.copy(
            A["logsb"][0:8, tt * 512:(tt + 1) * 512], PS[ps][0:8, :]),
           reads=[prg(ps, 0, 512)], writes=[rg("logsb", tt * 512, (tt + 1) * 512)],
           tag=f"lge{tt}")

    def router_tg(tg):
        em("tensor", lambda h, tg=tg: h.transpose(
            PS["s1"][:, 0:8], A["logsb"][0:8, tg * 128:(tg + 1) * 128], A["id8f"][0:8, :]),
           reads=[rg("logsb", tg * 128, (tg + 1) * 128), rg("id8f")],
           writes=[prg("s1", 0, 8)], tag=f"rtp{tg}")
        em("vector", lambda h: h.tensor_copy(TK[:, 0:8], PS["s1"][:, 0:8]),
           reads=[prg("s1", 0, 8)], writes=[rg("tk", 0, 8)], tag=f"rtc{tg}")
        em("vector", lambda h: h.reduce_max(TK[:, 16:17], TK[:, 0:8], axis=AX.X,
                                            negate=True),
           reads=[rg("tk", 0, 8)], writes=[rg("tk", 16, 17)], tag=f"rmx{tg}")
        em("scalar", lambda h: h.activation(TK[:, 8:16], TK[:, 0:8], AF.Exp,
                                            bias=TK[:, 16:17], scale=1.0),
           reads=[rg("tk", 0, 8), rg("tk", 16, 17)], writes=[rg("tk", 8, 16)],
           tag=f"rex{tg}")
        em("vector", lambda h: h.reduce_sum(TK[:, 17:18], TK[:, 8:16], axis=AX.X),
           reads=[rg("tk", 8, 16)], writes=[rg("tk", 17, 18)], tag=f"rsm{tg}")
        em("vector", lambda h: h.reciprocal(TK[:, 18:19], TK[:, 17:18]),
           reads=[rg("tk", 17, 18)], writes=[rg("tk", 18, 19)], tag=f"rrc{tg}")
        em("vector", lambda h: h.tensor_scalar(TK[:, 24:32], TK[:, 8:16],
                                               TK[:, 18:19], None, ALU.mult),
           reads=[rg("tk", 8, 16), rg("tk", 18, 19)], writes=[rg("tk", 24, 32)],
           tag=f"rp{tg}")
        em("vector", lambda h: h.reduce_max(TK[:, 19:20], TK[:, 24:32], axis=AX.X),
           reads=[rg("tk", 24, 32)], writes=[rg("tk", 19, 20)], tag=f"rpm{tg}")
        em("vector", lambda h: h.tensor_scalar(TK[:, 32:40], TK[:, 24:32],
                                               TK[:, 19:20], None, ALU.is_lt),
           reads=[rg("tk", 24, 32), rg("tk", 19, 20)], writes=[rg("tk", 32, 40)],
           tag=f"rlt{tg}")
        em("vector", lambda h: h.tensor_tensor(TK[:, 32:40], TK[:, 32:40],
                                               TK[:, 24:32], ALU.mult),
           reads=[rg("tk", 32, 40), rg("tk", 24, 32)], writes=[rg("tk", 32, 40)],
           tag=f"rl2{tg}")
        em("vector", lambda h: h.reduce_max(TK[:, 20:21], TK[:, 32:40], axis=AX.X),
           reads=[rg("tk", 32, 40)], writes=[rg("tk", 20, 21)], tag=f"rsc{tg}")
        em("vector", lambda h: h.tensor_scalar(TK[:, 40:48], TK[:, 24:32],
                                               TK[:, 20:21], None, ALU.is_ge),
           reads=[rg("tk", 24, 32), rg("tk", 20, 21)], writes=[rg("tk", 40, 48)],
           tag=f"rge{tg}")
        em("vector", lambda h: h.tensor_tensor(TK[:, 48:56], TK[:, 40:48],
                                               TK[:, 24:32], ALU.mult),
           reads=[rg("tk", 40, 48), rg("tk", 24, 32)], writes=[rg("tk", 48, 56)],
           tag=f"rw{tg}")
        em("vector", lambda h: h.tensor_copy(A["wtk"][:, 0:8], TK[:, 48:56]),
           reads=[rg("tk", 48, 56)], writes=[rg("wtk")], tag=f"rwb{tg}")

    def router_tg2(tg):
        em("tensor", lambda h: h.transpose(PS["t"][0:8, 0:128], A["wtk"][:, 0:8],
                                           A["id128"][:, :]),
           reads=[rg("wtk"), rg("id128")], writes=[prg("t")], tag=f"rtb{tg}")
        em("scalar", lambda h, tg=tg: h.copy(
            A["w8sb"][0:8, tg * 128:(tg + 1) * 128], PS["t"][0:8, 0:128]),
           reads=[prg("t")], writes=[rg("w8sb", tg * 128, (tg + 1) * 128)],
           tag=f"rwe{tg}")

    def load_fband(dst, dram, band, tagp):
        em("gpsimd", lambda h, dst=dst, dram=dram, band=band: h.dma_start(
            out=A[dst][:, :], in_=dram[band * 128:(band + 1) * 128, :]),
           writes=[rg(dst)], dma=True, tag=tagp)

    for tt in range(2):
        load_fband("wgh0", g_img, 0, f"ldg{tt}_0")
        load_fband("wuh0", u_img, 0, f"ldu{tt}_0")
        for efc in range(NCK):
            pw = efc % 2
            if efc + 1 < NCK:
                load_fband(f"wgh{(efc + 1) % 2}", g_img, efc + 1, f"ldg{tt}_{efc + 1}")
                load_fband(f"wuh{(efc + 1) % 2}", u_img, efc + 1, f"ldu{tt}_{efc + 1}")
            psg = ("a", "b")[efc % 2]
            psu = ("c0", "c1")[efc % 2]
            for ic in range(NCK):
                em("tensor", lambda h, ic=ic, tt=tt, pw=pw, psg=psg: h.matmul(
                    PS[psg][:, :], A[f"wgh{pw}"][:, ic * 128:(ic + 1) * 128],
                    A["ln2"][:, ic * TQ + tt * 512: ic * TQ + (tt + 1) * 512],
                    start=(ic == 0), stop=(ic == NCK - 1)),
                   reads=[rg(f"wgh{pw}", ic * 128, (ic + 1) * 128),
                          rg("ln2", ic * TQ + tt * 512, ic * TQ + (tt + 1) * 512)],
                   writes=[prg(psg)], tag=f"gmm{tt}_{efc}_{ic}")
                em("tensor", lambda h, ic=ic, tt=tt, pw=pw, psu=psu: h.matmul(
                    PS[psu][:, :], A[f"wuh{pw}"][:, ic * 128:(ic + 1) * 128],
                    A["ln2"][:, ic * TQ + tt * 512: ic * TQ + (tt + 1) * 512],
                    start=(ic == 0), stop=(ic == NCK - 1)),
                   reads=[rg(f"wuh{pw}", ic * 128, (ic + 1) * 128),
                          rg("ln2", ic * TQ + tt * 512, ic * TQ + (tt + 1) * 512)],
                   writes=[prg(psu)], tag=f"umm{tt}_{efc}_{ic}")
            sc0 = (efc % 2) * 512
            em("scalar", lambda h, sc0=sc0, psg=psg: h.activation(
                A["sil"][:, sc0:sc0 + 512], PS[psg][:, :], AF.Silu, bias=0.0, scale=1.0),
               reads=[prg(psg)], writes=[rg("sil", sc0, sc0 + 512)], tag=f"sil{tt}_{efc}")
            em("vector", lambda h, efc=efc, sc0=sc0, psu=psu: h.tensor_tensor(
                A["gated"][:, efc * 512:(efc + 1) * 512], PS[psu][:, :],
                A["sil"][:, sc0:sc0 + 512], ALU.mult),
               reads=[prg(psu), rg("sil", sc0, sc0 + 512)],
               writes=[rg("gated", efc * 512, (efc + 1) * 512)], tag=f"gu{tt}_{efc}")
            if tt == 0 and efc % 2 == 0:
                router_tg(efc // 2)
            if tt == 0 and efc % 2 == 1:
                router_tg2(efc // 2)
        if tt == 0:
            for g in range(2):
                ps = ("c0", "c1")[g]
                em("tensor", lambda h, g=g, ps=ps: h.matmul(
                    PS[ps][:, :], A["eer"][0:8, :], A["w8sb"][0:8, g * 512:(g + 1) * 512],
                    start=True, stop=True),
                   reads=[rg("eer"), rg("w8sb", g * 512, (g + 1) * 512)], writes=[prg(ps)],
                   tag=f"wer{g}")
                em("scalar", lambda h, g=g, ps=ps: h.copy(
                    A["wer"][:, g * 512:(g + 1) * 512], PS[ps][:, :]),
                   reads=[prg(ps)], writes=[rg("wer", g * 512, (g + 1) * 512)], tag=f"were{g}")
            for ltt in range(2):
                ps = ("a", "b")[ltt]
                for ic in range(NCK):
                    em("tensor", lambda h, ic=ic, ltt=ltt, ps=ps: h.matmul(
                        PS[ps][:, :], A["lATb"][:, ic * 128:(ic + 1) * 128],
                        A["ln2"][:, ic * TQ + ltt * 512: ic * TQ + (ltt + 1) * 512],
                        start=(ic == 0), stop=(ic == NCK - 1)),
                       reads=[rg("lATb", ic * 128, (ic + 1) * 128),
                              rg("ln2", ic * TQ + ltt * 512, ic * TQ + (ltt + 1) * 512)],
                       writes=[prg(ps)], tag=f"lrmm{ltt}_{ic}")
                em("vector", lambda h, ltt=ltt, ps=ps: h.tensor_tensor(
                    A["lrw"][:, ltt * 512:(ltt + 1) * 512], PS[ps][:, :],
                    A["wer"][:, ltt * 512:(ltt + 1) * 512], ALU.mult),
                   reads=[prg(ps), rg("wer", ltt * 512, (ltt + 1) * 512)],
                   writes=[rg("lrw", ltt * 512, (ltt + 1) * 512)], tag=f"lrw{ltt}")
            if "w8" in dbg:
                em("sync", lambda h: h.dma_start(out=dbg["w8"][:, :], in_=A["w8sb"][0:8, :]),
                   reads=[rg("w8sb")], dma=True, tag="dbg_w8")
            if "lrw" in dbg:
                em("sync", lambda h: h.dma_start(out=dbg["lrw"][:, :], in_=A["lrw"][:, :]),
                   reads=[rg("lrw")], dma=True, tag="dbg_lrw")
        for e_ in range(8):
            pb = e_ % 2
            em("tensor", lambda h, e_=e_, tt=tt: h.matmul(
                PS["s0"][:, :], A["e8s"][0:8, e_ * 128:(e_ + 1) * 128],
                A["w8sb"][0:8, tt * 512:(tt + 1) * 512], start=True, stop=True),
               reads=[rg("e8s", e_ * 128, (e_ + 1) * 128),
                      rg("w8sb", tt * 512, (tt + 1) * 512)],
               writes=[prg("s0")], tag=f"wbc{tt}_{e_}")
            em("scalar", lambda h, pb=pb: h.copy(A[f"wbce{pb}"][:, :], PS["s0"][:, 0:512]),
               reads=[prg("s0")], writes=[rg(f"wbce{pb}")], tag=f"wbe{tt}_{e_}")
            for sub in range(2):
                efc = e_ * 2 + sub
                em("vector", lambda h, efc=efc, pb=pb: h.tensor_tensor(
                    A["gated"][:, efc * 512:(efc + 1) * 512],
                    A["gated"][:, efc * 512:(efc + 1) * 512],
                    A[f"wbce{pb}"][:, :], ALU.mult),
                   reads=[rg("gated", efc * 512, (efc + 1) * 512), rg(f"wbce{pb}")],
                   writes=[rg("gated", efc * 512, (efc + 1) * 512)], tag=f"gw{tt}_{efc}")
        if "gated" in dbg and tt == 0:
            em("sync", lambda h: h.dma_start(out=dbg["gated"][:, :], in_=A["gated"][:, :]),
               reads=[rg("gated")], dma=True, tag="dbg_gated")
        load_fband("wch0", d_img, 0, f"ldd{tt}_0")
        for hoc in range(NCK):
            pw = hoc % 2
            if hoc + 1 < NCK:
                load_fband(f"wch{(hoc + 1) % 2}", d_img, hoc + 1, f"ldd{tt}_{hoc + 1}")
            ps = ("a", "b", "s0", "s1")[hoc % 4]
            pf = hoc % 2
            for efc in range(NCK):
                em("tensor", lambda h, efc=efc, pw=pw, ps=ps: h.matmul(
                    PS[ps][:, :], A[f"wch{pw}"][:, efc * 128:(efc + 1) * 128],
                    A["gated"][:, efc * 512:(efc + 1) * 512],
                    start=(efc == 0), stop=False),
                   reads=[rg(f"wch{pw}", efc * 128, (efc + 1) * 128),
                          rg("gated", efc * 512, (efc + 1) * 512)],
                   writes=[prg(ps)], tag=f"dmm{tt}_{hoc}_{efc}")
            em("tensor", lambda h, hoc=hoc, tt=tt, ps=ps: h.matmul(
                PS[ps][:, :], A["lBTb"][:, hoc * 128:(hoc + 1) * 128],
                A["lrw"][:, tt * 512:(tt + 1) * 512], start=False, stop=True),
               reads=[rg("lBTb", hoc * 128, (hoc + 1) * 128),
                      rg("lrw", tt * 512, (tt + 1) * 512)],
               writes=[prg(ps)], tag=f"lbmm{tt}_{hoc}")
            em("vector", lambda h, hoc=hoc, tt=tt, pf=pf, ps=ps: h.tensor_tensor(
                A[f"fout{pf}"][:, :], PS[ps][:, :],
                A["y1"][:, hoc * TQ + tt * 512: hoc * TQ + (tt + 1) * 512], ALU.add),
               reads=[prg(ps), rg("y1", hoc * TQ + tt * 512, hoc * TQ + (tt + 1) * 512)],
               writes=[rg(f"fout{pf}")], tag=f"fres{tt}_{hoc}")
            em("sync", lambda h, hoc=hoc, tt=tt, pf=pf: h.dma_start(
                out=y_out[hoc * 128:(hoc + 1) * 128, tt * 512:(tt + 1) * 512],
                in_=A[f"fout{pf}"][:, :]),
               reads=[rg(f"fout{pf}")],
               writes=[("yout", (hoc * 2 + tt) * 512, (hoc * 2 + tt + 1) * 512)],
               dma=True, tag=f"st{tt}_{hoc}")

    em("gpsimd", lambda h: h.memset(A["joinpad"][:, 0:1], 0.0),
       reads=[("yout", 0, NCK * 2 * 512)], writes=[rg("joinpad")], tag="join")

    return materialize()


def _rope_tables(pos):
    j = np.arange(0, HD, 2, dtype=np.float32) / HD
    inv = (1.0 / 10000.0 ** j).astype(np.float32)
    fr = pos[:, None].astype(np.float32) * inv[None, :]
    cos = np.cos(fr).astype(np.float32)
    sin = np.sin(fr).astype(np.float32)
    cosT = np.ascontiguousarray(np.concatenate([cos, cos], axis=1).T)
    sinS = np.ascontiguousarray(np.concatenate([sin, -sin], axis=1).T)
    return cosT, sinS


def _band_img(T):
    nb = T.shape[1] // 128
    return np.ascontiguousarray(
        T.reshape(16, 128, nb, 128).transpose(2, 1, 0, 3).reshape(nb * 128, 2048))


def _vgroup_img(T):
    return np.ascontiguousarray(
        T.reshape(16, 128, 8, 256).transpose(2, 1, 0, 3).reshape(1024, 4096))


_NC_CACHE = [None]


def kernel(L0_attn_hidden_states, L0_attn_Wq, L0_attn_bq, L0_attn_Wk, L0_attn_bk,
           L0_attn_Wv, L0_attn_bv, L0_attn_Wo, L0_attn_bo, L0_ffn_G, L0_ffn_gate_w,
           L0_ffn_up_w, L0_ffn_down_w, L0_ffn_lora_A, L0_ffn_lora_B):
    x = np.ascontiguousarray(np.asarray(L0_attn_hidden_states, np.float32))
    wq_img = _band_img(np.asarray(L0_attn_Wq, np.float32).T)
    wk_img = _band_img(np.asarray(L0_attn_Wk, np.float32).T)
    wv_img = _vgroup_img(np.asarray(L0_attn_Wv, np.float32).T)
    wo_img = _band_img(np.asarray(L0_attn_Wo, np.float32).T)
    g_img = _band_img(np.asarray(L0_ffn_gate_w, np.float32).transpose(2, 0, 1).reshape(H, EF))
    u_img = _band_img(np.asarray(L0_ffn_up_w, np.float32).transpose(2, 0, 1).reshape(H, EF))
    d_img = _band_img(np.asarray(L0_ffn_down_w, np.float32).transpose(0, 2, 1).reshape(EF, H))
    bq = np.ascontiguousarray(np.asarray(L0_attn_bq, np.float32).reshape(16, 128).T)
    bk = np.ascontiguousarray(np.asarray(L0_attn_bk, np.float32).reshape(16, 128).T)
    bo = np.ascontiguousarray(np.asarray(L0_attn_bo, np.float32).reshape(16, 128).T)
    bv = np.ascontiguousarray(np.asarray(L0_attn_bv, np.float32).reshape(1, H))
    G = np.asarray(L0_ffn_G, np.float32)
    G_img = np.ascontiguousarray(G.reshape(16, 128, 8).transpose(1, 0, 2).reshape(128, 128))
    lAT = np.asarray(L0_ffn_lora_A, np.float32).transpose(2, 0, 1).reshape(H, 128)
    lAT_img = np.ascontiguousarray(
        lAT.reshape(16, 128, 128).transpose(1, 0, 2).reshape(128, 2048))
    lBT = np.ascontiguousarray(
        np.asarray(L0_ffn_lora_B, np.float32).transpose(0, 2, 1).reshape(128, H))

    if _NC_CACHE[0] is None or _NC_CACHE[0][0] != (PHASE_LIMIT, tuple(DEBUG_DUMP)):
        _NC_CACHE[0] = ((PHASE_LIMIT, tuple(DEBUG_DUMP)), build_nc())
    nc = _NC_CACHE[0][1]

    in_maps = []
    perms = []
    for c in range(8):
        b, h2 = c // 2, c % 2
        chunks = [2 * j + h2 for j in range(8)] + [2 * j + (1 - h2) for j in range(8)]
        cols = np.concatenate([np.arange(t * 128, (t + 1) * 128) for t in chunks])
        perms.append(cols)
        xTc = np.ascontiguousarray(x[b].T[:, cols])
        cosk, sink = _rope_tables(cols.astype(np.float32))
        mcol = np.full((128, 1), MASKNEG if h2 == 0 else 0.0, np.float32)
        in_maps.append({
            "xT": xTc, "wq_img": wq_img, "wk_img": wk_img, "wv_img": wv_img,
            "wo_img": wo_img, "g_img": g_img, "u_img": u_img, "d_img": d_img,
            "lAT": lAT_img, "lBT": lBT, "G_img": G_img,
            "bq_img": bq, "bk_img": bk, "bo_img": bo, "bv_row": bv,
            "cosk": cosk, "sink": sink, "maskcol": mcol,
        })
    res = run_bass_kernel_spmd(nc, in_maps, core_ids=list(range(8)))
    y = np.zeros((B, S, H), np.float32)
    for c in range(8):
        b = c // 2
        yq = res.results[c]["y_out"]
        y[b, perms[c][:TQ], :] = yq.T
    if DEBUG_DUMP:
        kernel._dbg = [{k: v for k, v in r.items() if k.startswith("dbg_")}
                       for r in res.results]
    return y
